# version 06
# baseline (speedup 1.0000x reference)
import contextlib
import math

import numpy as np
import ml_dtypes

import concourse.bass as bass
import concourse.mybir as mybir
from concourse.bass_utils import run_bass_kernel_spmd

F32 = mybir.dt.float32
BF16 = mybir.dt.bfloat16
I16 = mybir.dt.int16
U16 = mybir.dt.uint16
I32 = mybir.dt.int32
AF = mybir.ActivationFunctionType
ALU = mybir.AluOpType
AX = mybir.AxisListType.X
NPBF = ml_dtypes.bfloat16

NCORES = 8
B, S, D = 2, 16384, 1024
TPC = 4096
NT = 32
NG = 8
GT = 512
DC = 8
I3, IC = 3072, 24
D4, FC = 4096, 32
D2, EC2 = 2048, 16
LN_EPS = 1e-5
TWO_PI = 2.0 * math.pi
RC1 = float(np.float32(TWO_PI))
RC2 = float(np.float32(TWO_PI - RC1))
QMAX = 32767.0
PIH = float(np.pi / 2)
PIF = float(np.pi)

SEMS = ("le", "lo", "l2e", "l2o", "wae", "wao", "lx", "pe", "po", "px", "tc", "vc", "sc", "cc", "c1", "c2", "c3")


class Ctx:
    def __init__(self, nc, block, es):
        self.nc = nc
        self.block = block
        self.es = es
        self.n = {}
        self.sem = {}
        for name in SEMS:
            self.sem[name] = es.enter_context(nc.semaphore(name))
            self.n[name] = 0
        self.bufs = {}

    def inc(self, instr, name, k=1):
        instr.then_inc(self.sem[name], k)
        self.n[name] += k
        return self.n[name]

    def dinc(self, instr, name):
        return self.inc(instr, name, 16)

    def w(self, eng, name, val):
        if val > 0:
            eng.wait_ge(self.sem[name], val)

    def snap(self):
        return dict(self.n)

    def stream(self, kind):
        return DStream(self, kind)

    def fence(self, eng, name):
        if self.n[name] > 0:
            eng.wait_ge(self.sem[name], self.n[name])


class DStream:

    def __init__(self, C, kind):
        self.C = C
        self.names = {"l": ("le", "lo"), "l2": ("l2e", "l2o"), "w": ("wae", "wao"),
                      "p": ("pe", "po")}[kind]
        self.base = [C.n[self.names[0]], C.n[self.names[1]]]
        self.cnt = [0, 0]

    def issue(self, instr, j):
        nm = self.names[j % 2]
        self.C.inc(instr, nm, 16)
        self.cnt[j % 2] += 16

    def wait(self, eng, j, ndma=1):
        nm = self.names[j % 2]
        eng.wait_ge(self.C.sem[nm], self.base[j % 2] + 16 * ndma * (j // 2 + 1))

    def wait_all(self, eng):
        for i in (0, 1):
            if self.cnt[i]:
                eng.wait_ge(self.C.sem[self.names[i]], self.base[i] + self.cnt[i])


def full_barrier(C):
    snap = C.snap()

    def _(eng):
        for s in SEMS:
            if snap[s]:
                eng.wait_ge(C.sem[s], snap[s])
    for sec in (C.block.sync, C.block.gpsimd, C.block.tensor, C.block.vector, C.block.scalar):
        sec(_)



def emit_allreduce_max(C, cols, nvals):
    nc, b = C.nc, C.block
    bufs = C.bufs
    rowf, colk, bc16, bcp = bufs["rowf"], bufs["colk"], bufs["bc16"], bufs["bcp"]
    cc_in, cc_out, rowb_h = bufs["cc_in"], bufs["cc_out"], bufs["rowb_h"]
    v0 = C.n["vc"]

    def _(g):
        C.w(g, "vc", v0)
        C.fence(g, "px")
        C.dinc(g.dma_start(out=cc_in, in_=cols[:, 0:2]), "px")
        C.fence(g, "px")
        C.inc(g.collective_compute(
            "AllReduce", ALU.max, replica_groups=[list(range(NCORES))],
            ins=[cc_in.opt()], outs=[cc_out.opt()]), "cc")
        C.w(g, "cc", C.n["cc"])
        C.dinc(g.dma_start(out=bcp[:], in_=cc_out), "px")
        C.fence(g, "px")
        C.dinc(g.dma_start(out=rowf[0:1, 0:256], in_=bcp[:, 0:2]), "px")
    b.gpsimd(_)
    sp1 = C.n["px"]

    def _(v):
        C.w(v, "px", sp1)
        C.inc(v.tensor_reduce(out=colk[0:1, 0:2],
                              in_=rowf[0:1, 0:256].rearrange("r (p c) -> r c p", c=2),
                              axis=AX, op=ALU.max), "vc")
    b.vector(_)
    v1 = C.n["vc"]

    def _(g):
        C.w(g, "vc", v1)
        C.dinc(g.dma_start(out=rowb_h[0:1, 0:8], in_=colk[0:1, :]), "px")
        C.fence(g, "px")
        for j in range(nvals):
            C.dinc(g.dma_start(out=bc16[:, j:j + 1],
                               in_=rowb_h[0:1, j:j + 1].broadcast_to((128, 1))), "px")
    b.gpsimd(_)


def emit_carry_gather(C, carry_list, off_list):
    nc, b = C.nc, C.block
    bufs = C.bufs
    gat, cmaskt = bufs["gat"], bufs["cmaskt"]
    k = len(carry_list)
    v0 = C.n["vc"]
    w = k * DC
    ag_in = bufs["ag_in8"] if k == 1 else bufs["ag_in16"]
    ag_out = bufs["ag_out8"] if k == 1 else bufs["ag_out16"]

    def _(g):
        C.w(g, "vc", v0)
        C.fence(g, "px")
        for j, car in enumerate(carry_list):
            C.dinc(g.dma_start(out=ag_in[:, j * DC:(j + 1) * DC], in_=car[:]), "px")
        C.fence(g, "px")
        C.inc(g.collective_compute(
            "AllGather", ALU.bypass, replica_groups=[list(range(NCORES))],
            ins=[ag_in[:, 0:w].opt()], outs=[ag_out[:, 0:w].opt()]), "cc")
        C.w(g, "cc", C.n["cc"])
        C.dinc(g.dma_start(out=gat[:, :, 0:w],
                           in_=ag_out[:, 0:w].rearrange("(c p) n -> p c n", p=128)), "px")
    b.gpsimd(_)
    sp1 = C.n["px"]

    def _(v):
        C.w(v, "px", sp1)
        C.inc(v.tensor_tensor(out=gat[:, :, 0:w], in0=gat[:, :, 0:w],
                              in1=cmaskt[:, 0:NCORES].rearrange("p (c o) -> p c o", o=1)
                              .broadcast_to((128, NCORES, w)),
                              op=ALU.mult), "vc")
        C.w(v, "vc", C.n["vc"])
        for j, off in enumerate(off_list):
            C.inc(v.tensor_reduce(out=off[:],
                                  in_=gat[:, :, j * DC:(j + 1) * DC].rearrange("p c n -> p n c"),
                                  axis=AX, op=ALU.add), "vc")
    b.vector(_)
    full_barrier(C)


def emit_ln0_stats(C, xsrc, with_minmax, mu_dst, r_dst):
    nc, b = C.nc, C.block
    bufs = C.bufs
    s1c, s2c, mxc, mnc = bufs["s1c"], bufs["s2c"], bufs["mxc"], bufs["mnc"]
    junkb = bufs["junkb"]
    t0, t1, t2 = bufs["t0_c"], bufs["t1_c"], bufs["t2_c"]
    xa = bufs["xstats"]

    stx = C.stream("l")
    v0 = C.n["vc"]
    s0 = C.n["sc"]

    def _(sync):
        for i in range(NT):
            if i >= 2:
                C.w(sync, "vc", v0 + (i - 1))
                C.w(sync, "sc", s0 + (i - 1))
            stx.issue(sync.dma_start(out=xa[:, i % 2, :], in_=xsrc[i, :, :]), i)
    b.sync(_)

    def _(v):
        for i in range(NT):
            stx.wait(v, i)
            xt = xa[:, i % 2, :]
            v.tensor_scalar(out=junkb[:, 0:D], in0=xt, scalar1=1.0, scalar2=None,
                            op0=ALU.mult, op1=ALU.add, accum_out=s1c[:, i:i + 1])
            if with_minmax:
                v.tensor_scalar(out=junkb[:, 0:D], in0=xt, scalar1=1.0, scalar2=None,
                                op0=ALU.mult, op1=ALU.max, accum_out=mxc[:, i:i + 1])
                ins = v.tensor_scalar(out=junkb[:, 0:D], in0=xt, scalar1=-1.0,
                                      scalar2=None, op0=ALU.mult, op1=ALU.max,
                                      accum_out=mnc[:, i:i + 1])
            else:
                ins = v.tensor_copy(junkb[:, 0:1], xt[:, 0:1])
            C.inc(ins, "vc")
    b.vector(_)

    def _(s):
        for i in range(NT):
            stx.wait(s, i)
            C.inc(s.activation(out=junkb[:, 0:D], in_=xa[:, i % 2, :], func=AF.Square,
                               accum_out=s2c[:, i:i + 1]), "sc")
    b.scalar(_)

    v1, s1 = C.n["vc"], C.n["sc"]

    def _(v):
        C.w(v, "vc", v1)
        C.w(v, "sc", s1)
        v.tensor_scalar(out=mu_dst[:], in0=s1c[:], scalar1=1.0 / D, scalar2=None, op0=ALU.mult)
        C.inc(v.tensor_scalar(out=t0[:], in0=s2c[:], scalar1=1.0 / D, scalar2=None,
                              op0=ALU.mult), "vc")
        C.w(v, "vc", C.n["vc"])
        C.inc(v.tensor_mul(t1[:], mu_dst[:], mu_dst[:]), "vc")
        C.w(v, "vc", C.n["vc"])
        C.inc(v.tensor_sub(t0[:], t0[:], t1[:]), "vc")
        C.w(v, "vc", C.n["vc"])
        C.inc(v.tensor_scalar(out=t0[:], in0=t0[:], scalar1=LN_EPS, scalar2=None,
                              op0=ALU.add), "vc")
    b.vector(_)
    v2 = C.n["vc"]

    def _(s):
        C.w(s, "vc", v2)
        C.inc(s.sqrt(t1[:], t0[:]), "sc")
    b.scalar(_)
    s2 = C.n["sc"]

    def _(v):
        C.w(v, "sc", s2)
        C.inc(v.reciprocal(t2[:], t1[:]), "vc")
        C.w(v, "vc", C.n["vc"])
        C.inc(v.tensor_mul(t1[:], t2[:], t2[:]), "vc")
        C.w(v, "vc", C.n["vc"])
        C.inc(v.tensor_mul(t1[:], t1[:], t0[:]), "vc")
        C.w(v, "vc", C.n["vc"])
        C.inc(v.tensor_scalar(out=t1[:], in0=t1[:], scalar1=-0.5, scalar2=1.5,
                              op0=ALU.mult, op1=ALU.add), "vc")
        C.w(v, "vc", C.n["vc"])
        C.inc(v.tensor_mul(r_dst[:], t2[:], t1[:]), "vc")
    b.vector(_)



def emit_prologue(C, io):
    nc, b = C.nc, C.block
    bufs = C.bufs
    identb = bufs["identb"]
    ones_row, ones_colb = bufs["ones_row"], bufs["ones_colb"]
    bom_row, b2_row = bufs["bom_row"], bufs["b2_row"]
    const_bf, colsum_bf, const_f = bufs["const_bf"], bufs["colsum_bf"], bufs["const_f"]
    gamma_c, beta_c, exp_ls, cmaskt = bufs["gamma_c"], bufs["beta_c"], bufs["exp_ls"], bufs["cmaskt"]
    ps, psT2 = bufs["ps"], bufs["psT2"]

    with contextlib.ExitStack() as pes:
        PE = pes.enter_context
        wbuf = PE(nc.sbuf_tensor("p_wbuf", [128, 2, I3], F32))
        ls_raw = PE(nc.sbuf_tensor("p_lsraw", [128, DC], F32))
        b1_rowf = PE(nc.sbuf_tensor("p_b1row", [1, D2], F32))
        bmax = PE(nc.sbuf_tensor("p_bmax", [128, 2, 96], F32))
        f7 = PE(nc.sbuf_tensor("p_f7", [128, 2, 96], F32))
        smul = PE(nc.sbuf_tensor("p_smul", [128, 2, 96], F32))
        qi = PE(nc.sbuf_tensor("p_qi", [128, I3], I32))
        wq = PE(nc.sbuf_tensor("p_wq", [128, 2, I3], BF16))
        wtr = PE(nc.sbuf_tensor("p_wtr", [128, 2, 128], BF16))
        w1chunk = PE(nc.sbuf_tensor("p_w1c", [128, 2, D2], F32))
        w1bf = PE(nc.sbuf_tensor("p_w1bf", [128, 2, D2], BF16))

        def _(sync):
            for dst, src in ((identb[:], io["identb_in"]), (bom_row[:], io["bom_in"]),
                             (b1_rowf[:], io["b1_in"]), (b2_row[:], io["b2_in"]),
                             (gamma_c[:], io["gamma_in"]), (beta_c[:], io["beta_in"]),
                             (ls_raw[:], io["ls_in"])):
                C.dinc(sync.dma_start(out=dst, in_=src), "lx")
            C.dinc(sync.dma_start(out=cmaskt[:, 0:NCORES],
                                  in_=io["cmask_in"][0:1, :].broadcast_to((128, NCORES))), "lx")
            C.fence(sync, "lx")
        b.sync(_)
        ld_consts = C.n["lx"]

        def _(v):
            v.memset(ones_row[:], 1.0)
            v.memset(bufs["colk"][:], 0.0)
            v.memset(bufs["colv"][:], 0.0)
            C.inc(v.memset(ones_colb[:], 1.0), "vc")
        b.vector(_)



        def _(s):
            C.w(s, "lx", ld_consts)
            C.inc(s.activation(out=exp_ls[:], in_=ls_raw[:], func=AF.Exp), "sc")
        b.scalar(_)

        jobs = [(io["wup_raw"], io["wupT_h"], IC, D),
                (io["wgate_raw"], io["wgT_h"], IC, D),
                (io["wdown_raw"], io["wdT_h"], DC, I3)]
        for raw, dstT, nrc, ncols in jobs:
            nb = ncols // 32
            ncc = ncols // 128
            raw_t = raw.rearrange("(n p) c -> n p c", p=128)
            dst_t = dstT.rearrange("(n p) c -> n p c", p=128)
            stw = C.stream("l")
            stp = C.stream("p")
            v0 = C.n["vc"]
            t0n = C.n["tc"]
            s0 = C.n["sc"]

            def _(sync, raw_t=raw_t, nrc=nrc, ncols=ncols, v0=v0, stw=stw):
                for i in range(nrc):
                    if i >= 2:
                        C.w(sync, "vc", v0 + 5 * (i - 1))
                    stw.issue(sync.dma_start(out=wbuf[:, i % 2, 0:ncols], in_=raw_t[i, :, :]), i)
            b.sync(_)

            def _(v, nrc=nrc, ncols=ncols, nb=nb, t0n=t0n, ncc=ncc, stw=stw):
                for i in range(nrc):
                    sl = i % 2
                    stw.wait(v, i)
                    if i >= 2:
                        C.w(v, "tc", t0n + (i - 1) * ncc)
                    C.inc(v.tensor_reduce(out=bmax[:, sl, 0:nb],
                                          in_=wbuf[:, sl, 0:ncols]
                                          .rearrange("p (b k) -> p b k", k=32),
                                          axis=AX, op=ALU.max,
                                          apply_absolute_value=True), "vc")
                    C.w(v, "vc", C.n["vc"])
                    C.inc(v.tensor_scalar(out=bmax[:, sl, 0:nb], in0=bmax[:, sl, 0:nb],
                                           scalar1=7e-12, scalar2=None, op0=ALU.max), "vc")
                    C.w(v, "vc", C.n["vc"])
                    C.inc(v.reciprocal(f7[:, sl, 0:nb], bmax[:, sl, 0:nb]), "vc")
                    C.w(v, "vc", C.n["vc"])
                    v.tensor_scalar(out=f7[:, sl, 0:nb], in0=f7[:, sl, 0:nb],
                                    scalar1=7.0, scalar2=None, op0=ALU.mult)
                    C.inc(v.tensor_scalar(out=smul[:, sl, 0:nb], in0=bmax[:, sl, 0:nb],
                                          scalar1=1.0 / 7.0, scalar2=None,
                                          op0=ALU.mult), "vc")
                    C.w(v, "vc", C.n["vc"])
                    v.tensor_tensor(out=qi[:, 0:ncols].rearrange("p (b k) -> p b k", k=32),
                                    in0=wbuf[:, sl, 0:ncols].rearrange("p (b k) -> p b k", k=32),
                                    in1=f7[:, sl, 0:nb].rearrange("p (b o) -> p b o", o=1)
                                    .broadcast_to((128, nb, 32)),
                                    op=ALU.mult)
                    C.inc(v.tensor_tensor(
                        out=wq[:, sl, 0:ncols].rearrange("p (b k) -> p b k", k=32),
                        in0=qi[:, 0:ncols].rearrange("p (b k) -> p b k", k=32),
                        in1=smul[:, sl, 0:nb].rearrange("p (b o) -> p b o", o=1)
                        .broadcast_to((128, nb, 32)),
                        op=ALU.mult), "vc")
            b.vector(_)

            def _(t, nrc=nrc, ncc=ncc, v0=v0, s0=s0):
                for i in range(nrc):
                    C.w(t, "vc", v0 + 5 * (i + 1))
                    for cc_ in range(ncc):
                        k = i * ncc + cc_
                        if k >= 2:
                            C.w(t, "sc", s0 + k - 1)
                        C.inc(t.transpose(psT2[k % 2][:, 0:128],
                                          wq[:, i % 2, cc_ * 128:(cc_ + 1) * 128],
                                          identb[:]), "tc")
            b.tensor(_)

            def _(s, nrc=nrc, ncc=ncc, t0n=t0n, stp=stp):
                for k in range(nrc * ncc):
                    C.w(s, "tc", t0n + k + 1)
                    if k >= 2:
                        stp.wait(s, k - 2)
                    C.inc(s.activation(out=wtr[:, k % 2, :], in_=psT2[k % 2][:, 0:128],
                                       func=AF.Identity), "sc")
            b.scalar(_)

            def _(g, nrc=nrc, ncc=ncc, s0=s0, dst_t=dst_t, stp=stp):
                for k in range(nrc * ncc):
                    i, cc_ = divmod(k, ncc)
                    C.w(g, "sc", s0 + k + 1)
                    stp.issue(g.dma_start(out=dst_t[cc_, :, i * 128:(i + 1) * 128],
                                          in_=wtr[:, k % 2, :]), k)
            b.gpsimd(_)
            full_barrier(C)

        for src_in, dst_h, nch, ncols in ((io["womT_in"], io["womT_h"], DC, D),
                                          (io["w2T_in"], io["w2T_h"], EC2, D)):
            src_t = src_in.rearrange("(n p) c -> n p c", p=128)
            dst_t = dst_h.rearrange("(n p) c -> n p c", p=128)
            stw = C.stream("l")
            stp = C.stream("p")
            v0 = C.n["vc"]

            def _(sync, src_t=src_t, nch=nch, ncols=ncols, v0=v0, stw=stw):
                for i in range(nch):
                    if i >= 2:
                        C.w(sync, "vc", v0 + (i - 1))
                    stw.issue(sync.dma_start(out=wbuf[:, i % 2, 0:ncols], in_=src_t[i, :, :]), i)
            b.sync(_)

            t_start = C.n["tc"]

            def _(v, nch=nch, ncols=ncols, t_start=t_start, stw=stw, stp=stp):
                for i in range(nch):
                    stw.wait(v, i)
                    if i < 2:
                        C.w(v, "tc", t_start)
                    if i >= 2:
                        stp.wait(v, i - 2)
                    C.inc(v.tensor_copy(wq[:, i % 2, 0:ncols], wbuf[:, i % 2, 0:ncols]), "vc")
            b.vector(_)

            def _(g, nch=nch, ncols=ncols, v0=v0, dst_t=dst_t, stp=stp):
                for i in range(nch):
                    C.w(g, "vc", v0 + i + 1)
                    stp.issue(g.dma_start(out=dst_t[i, :, :], in_=wq[:, i % 2, 0:ncols]), i)
            b.gpsimd(_)
            full_barrier(C)

        w1T_t = io["w1T_in"].rearrange("(n p) c -> n p c", p=128)
        w1g_t = io["w1g_h"].rearrange("(n p) c -> n p c", p=128)
        stw1 = C.stream("l")
        stp1 = C.stream("p")
        v0 = C.n["vc"]
        t0n = C.n["tc"]

        def _(sync, v0=v0, t0n=t0n, stw1=stw1):
            for i in range(FC):
                if i >= 2:
                    C.w(sync, "vc", v0 + (i - 1))
                    C.w(sync, "tc", t0n + 4 * (i - 1))
                stw1.issue(sync.dma_start(out=w1chunk[:, i % 2, :], in_=w1T_t[i, :, :]), i)
        b.sync(_)

        def _(v, stw1=stw1, stp1=stp1):
            for i in range(FC):
                stw1.wait(v, i)
                if i >= 2:
                    stp1.wait(v, i - 2)
                C.inc(v.tensor_scalar(out=w1bf[:, i % 2, :], in0=w1chunk[:, i % 2, :],
                                      scalar1=gamma_c[:, i:i + 1], scalar2=None,
                                      op0=ALU.mult), "vc")
        b.vector(_)

        def _(g, v0=v0, stp1=stp1):
            for i in range(FC):
                C.w(g, "vc", v0 + i + 1)
                stp1.issue(g.dma_start(out=w1g_t[i, :, :], in_=w1bf[:, i % 2, :]), i)
        b.gpsimd(_)

        def _(t, stw1=stw1):
            for i in range(FC):
                stw1.wait(t, i)
                for j in range(4):
                    C.inc(t.matmul(ps[j][0:1, :], beta_c[:, i:i + 1],
                                   w1chunk[:, i % 2, j * 512:(j + 1) * 512],
                                   start=(i == 0), stop=(i == FC - 1)), "tc")
        b.tensor(_)
        t_beta = C.n["tc"]

        def _(v, t_beta=t_beta, stw1=stw1):
            C.w(v, "tc", t_beta)
            stw1.wait_all(v)
            for j in range(4):
                v.tensor_add(const_f[0:1, j * 512:(j + 1) * 512], ps[j][0:1, :],
                             b1_rowf[0:1, j * 512:(j + 1) * 512])
            C.inc(v.tensor_copy(const_bf[0:1, :], const_f[0:1, :]), "vc")
        b.vector(_)
        full_barrier(C)

        stc = C.stream("l")
        v0 = C.n["vc"]
        t0n = C.n["tc"]

        def _(sync, t0n=t0n, stc=stc, stp1=stp1):
            stp1.wait_all(sync)
            C.w(sync, "vc", C.n["vc"])
            for i in range(FC):
                if i >= 2:
                    C.w(sync, "tc", t0n + 4 * (i - 1))
                stc.issue(sync.dma_start(out=w1bf[:, i % 2, :], in_=w1g_t[i, :, :]), i)
        b.sync(_)

        def _(t, stc=stc):
            for i in range(FC):
                stc.wait(t, i)
                for j in range(4):
                    C.inc(t.matmul(ps[j][0:1, :], ones_colb[:, 0:1],
                                   w1bf[:, i % 2, j * 512:(j + 1) * 512],
                                   start=(i == 0), stop=(i == FC - 1)), "tc")
        b.tensor(_)

        def _(v):
            C.w(v, "tc", C.n["tc"])
            for j in range(4):
                C.inc(v.tensor_copy(colsum_bf[0:1, j * 512:(j + 1) * 512], ps[j][0:1, :]), "vc")
        b.vector(_)

        full_barrier(C)



def emit_ffn(C, call, xsrc, xdst, io):
    nc, b = C.nc, C.block
    bufs = C.bufs
    ps, psT2 = bufs["ps"], bufs["psT2"]
    identb = bufs["identb"]
    bc16, sc16 = bufs["bc16"], bufs["sc16"]
    mu_c, r_c, a_c, bq_c = bufs["mu_c"], bufs["r_c"], bufs["a_c"], bufs["bq_c"]
    mxc, mnc = bufs["mxc"], bufs["mnc"]
    t0, t1 = bufs["t0_c"], bufs["t1_c"]
    colv = bufs["colv"]
    sp_up, sp_gate, sp_gi, sp_sig, sp_p = (io["sp_up"], io["sp_gate"], io["sp_gi"],
                                           io["sp_sig"], io["sp_p"])

    emit_ln0_stats(C, xsrc, True, mu_c, r_c)

    def _(v):
        C.w(v, "vc", C.n["vc"])
        v.tensor_sub(t0[:], mxc[:], mu_c[:])
        C.inc(v.tensor_add(t1[:], mnc[:], mu_c[:]), "vc")
        C.w(v, "vc", C.n["vc"])
        C.inc(v.tensor_max(t0[:], t0[:], t1[:]), "vc")
        C.w(v, "vc", C.n["vc"])
        C.inc(v.tensor_mul(t0[:], t0[:], r_c[:]), "vc")
        C.w(v, "vc", C.n["vc"])
        C.inc(v.tensor_reduce(out=colv[:, 0:1], in_=t0[:], axis=AX, op=ALU.max), "vc")
    b.vector(_)

    emit_allreduce_max(C, colv, 1)
    sp_bc = C.n["px"]

    def _(v):
        C.w(v, "px", sp_bc)
        C.inc(v.tensor_scalar(out=sc16[:, 0:1], in0=bc16[:, 0:1], scalar1=1.0 / QMAX,
                              scalar2=1e-12, op0=ALU.mult, op1=ALU.max), "vc")
        C.w(v, "vc", C.n["vc"])
        C.inc(v.reciprocal(sc16[:, 1:2], sc16[:, 0:1]), "vc")
        C.w(v, "vc", C.n["vc"])
        C.inc(v.tensor_scalar(out=a_c[:], in0=r_c[:], scalar1=sc16[:, 1:2], scalar2=None,
                              op0=ALU.mult), "vc")
        C.w(v, "vc", C.n["vc"])
        C.inc(v.tensor_mul(bq_c[:], mu_c[:], a_c[:]), "vc")
        C.w(v, "vc", C.n["vc"])
        C.inc(v.tensor_scalar(out=bq_c[:], in0=bq_c[:], scalar1=-1.0, scalar2=None,
                              op0=ALU.mult), "vc")
    b.vector(_)
    full_barrier(C)

    with contextlib.ExitStack() as pes:
        PE = pes.enter_context
        xa = bufs["xstats"]
        qtile = PE(nc.sbuf_tensor(f"f{call}_q", [128, 2, D], I16))
        xqtm = PE(nc.sbuf_tensor(f"f{call}_xqtm", [128, 2, D], BF16))
        xqT = PE(nc.sbuf_tensor(f"f{call}_xqT", [128, DC, TPC], BF16))
        wst = PE(nc.sbuf_tensor(f"f{call}_wst", [128, 2, DC, 128], BF16))
        ev = PE(nc.sbuf_tensor(f"f{call}_ev", [128, 8, GT], BF16))
        amx = PE(nc.sbuf_tensor(f"f{call}_amx", [128, 2, 2 * IC * NG], F32))
        junk = PE(nc.sbuf_tensor(f"f{call}_junk", [128, GT], BF16))

        stx = C.stream("l")
        s0 = C.n["sc"]
        v0 = C.n["vc"]
        t0n = C.n["tc"]

        def _(sync, v0=v0, stx=stx):
            for i in range(NT):
                if i >= 2:
                    C.w(sync, "vc", v0 + (i - 1))
                stx.issue(sync.dma_start(out=xa[:, i % 2, :], in_=xsrc[i, :, :]), i)
        b.sync(_)

        def _(s, v0=v0, t0n=t0n, stx=stx):
            for i in range(NT):
                stx.wait(s, i)
                C.inc(s.activation(out=qtile[:, i % 2, :], in_=xa[:, i % 2, :],
                                   func=AF.Identity,
                                   scale=a_c[:, i:i + 1], bias=bq_c[:, i:i + 1]), "sc")
                for dc in range(DC):
                    k = i * DC + dc
                    C.w(s, "tc", t0n + k + 1)
                    C.inc(s.activation(out=xqT[:, dc, i * 128:(i + 1) * 128],
                                       in_=psT2[k % 2][:, 0:128], func=AF.Identity), "sc")
        b.scalar(_)

        def _(v, s0=s0):
            for i in range(NT):
                C.w(v, "sc", s0 + i * (DC + 1) + 1)
                C.inc(v.tensor_scalar(out=xqtm[:, i % 2, :], in0=qtile[:, i % 2, :],
                                      scalar1=sc16[:, 0:1], scalar2=None, op0=ALU.mult), "vc")
        b.vector(_)

        def _(t, v0=v0, s0=s0):
            for i in range(NT):
                C.w(t, "vc", v0 + i + 1)
                for dc in range(DC):
                    k = i * DC + dc
                    if k >= 2:
                        ii, dd = divmod(k - 2, DC)
                        C.w(t, "sc", s0 + ii * (DC + 1) + 1 + dd + 1)
                    C.inc(t.transpose(psT2[k % 2][:, 0:128],
                                      xqtm[:, i % 2, dc * 128:(dc + 1) * 128], identb[:]), "tc")
        b.tensor(_)

        full_barrier(C)

        if SUBSTAGE == 2:
            def _(g2):
                for s in ("tc", "vc", "sc"):
                    C.w(g2, s, C.n[s])
                for j, buf in enumerate((bufs["s1c"], bufs["s2c"], mxc, mnc, mu_c, r_c, a_c, bq_c)):
                    C.dinc(g2.dma_start(out=xdst[0, :, j * 32:(j + 1) * 32], in_=buf[:]), "px")
                C.dinc(g2.dma_start(out=xdst[0, :, 256:264], in_=colv[:, 0:8]), "px")
                C.dinc(g2.dma_start(out=xdst[0, :, 264:280], in_=bc16[:, :]), "px")
                C.dinc(g2.dma_start(out=xdst[0, :, 280:296], in_=sc16[:, :]), "px")
            b.gpsimd(_)
            full_barrier(C)
            return

        if SUBSTAGE in (3, 4, 5):
            wh_d = {3: io["wupT_h"], 4: io["wgT_h"], 5: io["wdT_h"]}[SUBSTAGE]
            nrows = wh_d.shape[0]
            wcols = wh_d.shape[1]
            nch = nrows // 128
            ntc = wcols // D
            wtmp_b = PE(nc.sbuf_tensor("dbg_wb", [128, wcols], BF16))
            wtmp_f = PE(nc.sbuf_tensor("dbg_wf", [128, wcols], F32))
            wh_t = wh_d.rearrange("(n p) c -> n p c", p=128)
            lxw = C.n["lx"]
            vw = C.n["vc"]
            pxw = C.n["px"]

            def _(sync, lxw=lxw, pxw=pxw):
                for i in range(nch):
                    if i >= 1:
                        C.w(sync, "px", pxw + 16 * ntc * i)
                    C.dinc(sync.dma_start(out=wtmp_b[:], in_=wh_t[i, :, :]), "lx")
            b.sync(_)

            def _(v, lxw=lxw):
                for i in range(nch):
                    C.w(v, "lx", lxw + 16 * (i + 1))
                    C.inc(v.tensor_copy(wtmp_f[:], wtmp_b[:]), "vc")
            b.vector(_)

            def _(g2, vw=vw):
                for i in range(nch):
                    C.w(g2, "vc", vw + i + 1)
                    for j in range(ntc):
                        C.dinc(g2.dma_start(out=xdst[i * ntc + j, :, :],
                                            in_=wtmp_f[:, j * D:(j + 1) * D]), "px")
            b.gpsimd(_)
            full_barrier(C)
            return

        if SUBSTAGE == 1:
            dbg = PE(nc.sbuf_tensor(f"f{call}_dbg", [128, 2, D], F32))
            st_d = C.stream("l")
            v0d = C.n["vc"]

            def _(sync, st_d=st_d, v0d=v0d):
                for i in range(NT):
                    if i >= 2:
                        C.w(sync, "vc", v0d + (i - 1))
                    st_d.issue(sync.dma_start(out=xa[:, i % 2, :], in_=xsrc[i, :, :]), i)
            b.sync(_)

            def _(s, st_d=st_d):
                for i in range(NT):
                    st_d.wait(s, i)
                    C.inc(s.activation(out=qtile[:, i % 2, :], in_=xa[:, i % 2, :],
                                       func=AF.Identity,
                                       scale=a_c[:, i:i + 1], bias=bq_c[:, i:i + 1]), "sc")
            b.scalar(_)
            s0d = C.n["sc"] - NT

            def _(v, s0d=s0d, v0d=v0d):
                for i in range(NT):
                    C.w(v, "sc", s0d + i + 1)
                    C.inc(v.tensor_scalar(out=dbg[:, i % 2, :], in0=qtile[:, i % 2, :],
                                          scalar1=sc16[:, 0:1], scalar2=None,
                                          op0=ALU.mult), "vc")
            b.vector(_)
            v1d = C.n["vc"] - NT

            def _(g2, v1d=v1d):
                for i in range(NT):
                    C.w(g2, "vc", v1d + i + 1)
                    C.dinc(g2.dma_start(out=xdst[i, :, :], in_=dbg[:, i % 2, :]), "px")
            b.gpsimd(_)
            full_barrier(C)
            return

        w_hbm = (io["wupT_h"], io["wgT_h"])
        w_sp = (sp_up, sp_gate)
        for quarter in range(NG // 2):
            groups = [2 * quarter, 2 * quarter + 1]
            for wi in range(2):
                wh = w_hbm[wi].rearrange("(c p) e -> p c e", p=128)
                stwm = C.stream("l2")
                stsp = C.stream("p")
                t0n = C.n["tc"]
                v0 = C.n["vc"]
                c10 = C.n["c1"]

                def _(sync, wh=wh, t0n=t0n, stwm=stwm):
                    for e in range(IC):
                        if e < 2:
                            C.w(sync, "tc", t0n)
                        else:
                            C.w(sync, "tc", t0n + (e - 1))
                        stwm.issue(sync.dma_start(out=wst[:, e % 2, :, :],
                                                  in_=wh[:, :, e * 128:(e + 1) * 128]), e)
                b.sync(_)

                def _(t, v0=v0, groups=groups, stwm=stwm):
                    for e in range(IC):
                        stwm.wait(t, e)
                        if e >= 2:
                            C.w(t, "vc", v0 + (e - 1) * 2)
                        for d in range(DC):
                            for gi in range(2):
                                ins = t.matmul(ps[2 * (e % 2) + gi][:],
                                               wst[:, e % 2, d, :],
                                               xqT[:, d, groups[gi] * GT:(groups[gi] + 1) * GT],
                                               start=(d == 0), stop=(d == DC - 1))
                        C.inc(ins, "tc")
                b.tensor(_)

                def _(v, t0n=t0n, wi=wi, groups=groups, stsp=stsp):
                    for e in range(IC):
                        C.w(v, "tc", t0n + e + 1)
                        if e >= 2:
                            stsp.wait(v, e - 2, ndma=2)
                        for gi in range(2):
                            g = groups[gi]
                            col = e * NG + g
                            ncol = IC * NG + col
                            v.tensor_copy(ev[:, 2 * (e % 2) + gi, :], ps[2 * (e % 2) + gi][:])
                            v.tensor_scalar(out=junk[:], in0=ev[:, 2 * (e % 2) + gi, :],
                                            scalar1=1.0, scalar2=None, op0=ALU.mult,
                                            op1=ALU.max, accum_out=amx[:, wi, col:col + 1])
                            C.inc(v.tensor_scalar(out=junk[:], in0=ev[:, 2 * (e % 2) + gi, :],
                                                  scalar1=-1.0, scalar2=None, op0=ALU.mult,
                                                  op1=ALU.max,
                                                  accum_out=amx[:, wi, ncol:ncol + 1]), "vc")
                b.vector(_)

                def _(g2, v0=v0, wi=wi, groups=groups, stsp=stsp):
                    for e in range(IC):
                        for gi in range(2):
                            C.w(g2, "vc", v0 + e * 2 + gi + 1)
                            stsp.issue(g2.dma_start(out=w_sp[wi][e, groups[gi], :, :],
                                                    in_=ev[:, 2 * (e % 2) + gi, :]), e)
                b.gpsimd(_)

        g_pe, g_po = C.n["pe"], C.n["po"]

        def _(v, g_pe=g_pe, g_po=g_po):
            C.w(v, "pe", g_pe)
            C.w(v, "po", g_po)
            C.w(v, "vc", C.n["vc"])
            C.inc(v.tensor_reduce(out=colv[:, 0:1], in_=amx[:, 0, :], axis=AX, op=ALU.max), "vc")
            C.inc(v.tensor_reduce(out=colv[:, 1:2], in_=amx[:, 1, :], axis=AX, op=ALU.max), "vc")
        b.vector(_)

        emit_allreduce_max(C, colv, 2)
        sp_bc = C.n["px"]

        def _(v, sp_bc=sp_bc):
            C.w(v, "px", sp_bc)
            C.inc(v.tensor_scalar(out=sc16[:, 2:3], in0=bc16[:, 0:1], scalar1=1.0 / QMAX,
                                  scalar2=1e-12, op0=ALU.mult, op1=ALU.max), "vc")
            C.w(v, "vc", C.n["vc"])
            v.reciprocal(sc16[:, 3:4], sc16[:, 2:3])
            C.inc(v.tensor_scalar(out=sc16[:, 4:5], in0=bc16[:, 1:2], scalar1=1.0 / QMAX,
                                  scalar2=1e-12, op0=ALU.mult, op1=ALU.max), "vc")
            C.w(v, "vc", C.n["vc"])
            C.inc(v.reciprocal(sc16[:, 5:6], sc16[:, 4:5]), "vc")
        b.vector(_)
        full_barrier(C)

        if SUBSTAGE == 10:
            dxf = PE(nc.sbuf_tensor("dbg_xf", [128, 2, 1024], F32))
            vd = C.n["vc"]
            pxd = C.n["px"]

            def _(v, pxd=pxd):
                for k in range(32):
                    d, q = divmod(k, 4)
                    if k >= 2:
                        C.w(v, "px", pxd + 16 * (k - 1))
                    C.inc(v.tensor_copy(dxf[:, k % 2, :],
                                        xqT[:, d, q * 1024:(q + 1) * 1024]), "vc")
            b.vector(_)

            def _(g2, vd=vd):
                for k in range(32):
                    C.w(g2, "vc", vd + k + 1)
                    C.dinc(g2.dma_start(out=xdst[k, :, :], in_=dxf[:, k % 2, :]), "px")
            b.gpsimd(_)
            full_barrier(C)
            return

        if SUBSTAGE in (7, 8, 9):
            e_base = {7: 0, 8: 8, 9: 16}[SUBSTAGE]
            dzb = PE(nc.sbuf_tensor("dbg_zb", [128, 2, GT], BF16))
            dzf = PE(nc.sbuf_tensor("dbg_zf", [128, 2, GT], F32))
            lxd = C.n["lx"]
            vd = C.n["vc"]
            pxd = C.n["px"]

            def _(sync, lxd=lxd, pxd=pxd, e_base=e_base):
                for k in range(64):
                    e, g = divmod(k, 8)
                    if k >= 2:
                        C.w(sync, "px", pxd + 16 * (k - 1))
                    C.dinc(sync.dma_start(out=dzb[:, k % 2, :],
                                          in_=sp_up[e_base + e, g, :, :]), "lx")
            b.sync(_)

            def _(v, lxd=lxd):
                for k in range(64):
                    C.w(v, "lx", lxd + 16 * (k + 1))
                    C.inc(v.tensor_copy(dzf[:, k % 2, :], dzb[:, k % 2, :]), "vc")
            b.vector(_)

            def _(g2, vd=vd):
                for k in range(64):
                    C.w(g2, "vc", vd + k + 1)
                    ti, half = divmod(k, 2)
                    C.dinc(g2.dma_start(out=xdst[ti, :, half * GT:(half + 1) * GT],
                                        in_=dzf[:, k % 2, :]), "px")
            b.gpsimd(_)
            full_barrier(C)
            return

        if SUBSTAGE == 6:
            def _(g2):
                for s in ("vc",):
                    C.w(g2, s, C.n[s])
                C.dinc(g2.dma_start(out=xdst[0, :, 0:384], in_=amx[:, 0, :]), "px")
                C.dinc(g2.dma_start(out=xdst[0, :, 384:768], in_=amx[:, 1, :]), "px")
                C.dinc(g2.dma_start(out=xdst[0, :, 768:776], in_=colv[:, 0:8]), "px")
                C.dinc(g2.dma_start(out=xdst[0, :, 776:792], in_=bc16[:, :]), "px")
                C.dinc(g2.dma_start(out=xdst[0, :, 792:808], in_=sc16[:, :]), "px")
            b.gpsimd(_)
            full_barrier(C)
            return

    ntile = IC * NG
    with contextlib.ExitStack() as pes:
        PE = pes.enter_context
        gtile = PE(nc.sbuf_tensor(f"f{call}_gt", [128, 2, GT], BF16))
        gi_t = PE(nc.sbuf_tensor(f"f{call}_gi", [128, 2, GT], I16))
        sig0 = PE(nc.sbuf_tensor(f"f{call}_sg0", [128, 2, GT], F32))
        sigu = PE(nc.sbuf_tensor(f"f{call}_sgu", [128, 2, GT], U16))
        tp = PE(nc.sbuf_tensor(f"f{call}_tp", [128, 2, GT], F32))
        tmaxc = PE(nc.sbuf_tensor(f"f{call}_tmx", [128, ntile], F32))
        uptile = PE(nc.sbuf_tensor(f"f{call}_up", [128, 2, GT], BF16))
        g2t = PE(nc.sbuf_tensor(f"f{call}_g2", [128, 2, GT], I16))
        uit = PE(nc.sbuf_tensor(f"f{call}_ui", [128, 2, GT], I16))
        ptile = PE(nc.sbuf_tensor(f"f{call}_p", [128, 2, GT], F32))

        stga = C.stream("l")
        stgb = C.stream("l2")
        stspc = C.stream("p")
        v0 = C.n["vc"]
        s0 = C.n["sc"]

        def _(sync, v0=v0, stga=stga, stgb=stgb):
            for k in range(ntile):
                e, g = divmod(k, NG)
                if k >= 2:
                    C.w(sync, "vc", v0 + 2 * k - 1)
                (stga, stgb)[k % 2].issue(
                    sync.dma_start(out=gtile[:, k % 2, :], in_=sp_gate[e, g, :, :]), k // 2)
        b.sync(_)

        def _(v, s0=s0, stga=stga, stgb=stgb, stspc=stspc):
            for i in range(ntile + 1):
                if i < ntile:
                    (stga, stgb)[i % 2].wait(v, i // 2)
                    if i >= 2:
                        stspc.wait(v, i - 2, ndma=2)
                    C.inc(v.tensor_scalar(out=gi_t[:, i % 2, :], in0=gtile[:, i % 2, :],
                                          scalar1=sc16[:, 5:6], scalar2=None,
                                          op0=ALU.mult), "vc")
                if i >= 1:
                    k = i - 1
                    sl = k % 2
                    C.w(v, "sc", s0 + k + 1)
                    v.tensor_scalar(out=sigu[:, sl, :], in0=sig0[:, sl, :], scalar1=65536.0,
                                    scalar2=65535.0, op0=ALU.mult, op1=ALU.min)
                    v.tensor_tensor(out=tp[:, sl, :], in0=gi_t[:, sl, :], in1=sigu[:, sl, :],
                                    op=ALU.mult)
                    C.inc(v.tensor_reduce(out=tmaxc[:, k:k + 1], in_=tp[:, sl, :], axis=AX,
                                          op=ALU.max, apply_absolute_value=True), "vc")
        b.vector(_)

        def _(s, v0=v0):
            for k in range(ntile):
                C.w(s, "vc", v0 + (1 if k == 0 else 2 * k))
                C.inc(s.activation(out=sig0[:, k % 2, :], in_=gi_t[:, k % 2, :],
                                   func=AF.Sigmoid, scale=sc16[:, 4:5]), "sc")
        b.scalar(_)

        def _(g2, v0=v0, stspc=stspc):
            for k in range(ntile):
                e, g = divmod(k, NG)
                C.w(g2, "vc", v0 + min(2 * k + 3, 2 * ntile))
                stspc.issue(g2.dma_start(out=sp_gi[e, g, :, :], in_=gi_t[:, k % 2, :]), k)
                stspc.issue(g2.dma_start(out=sp_sig[e, g, :, :], in_=sigu[:, k % 2, :]), k)
        b.gpsimd(_)

        def _(v, stspc=stspc):
            stspc.wait_all(v)
            C.w(v, "vc", C.n["vc"])
            C.inc(v.tensor_reduce(out=colv[:, 0:1], in_=tmaxc[:], axis=AX, op=ALU.max), "vc")
        b.vector(_)
        emit_allreduce_max(C, colv, 1)
        sp_bc = C.n["px"]

        def _(v, sp_bc=sp_bc):
            C.w(v, "px", sp_bc)
            C.inc(v.tensor_scalar(out=sc16[:, 6:7], in0=sc16[:, 4:5],
                                  scalar1=float(2.0 ** -16),
                                  scalar2=None, op0=ALU.mult), "vc")
            C.w(v, "vc", C.n["vc"])
            C.inc(v.tensor_mul(sc16[:, 7:8], bc16[:, 0:1], sc16[:, 6:7]), "vc")
            C.w(v, "vc", C.n["vc"])
            C.inc(v.tensor_scalar(out=sc16[:, 7:8], in0=sc16[:, 7:8], scalar1=1.0 / QMAX,
                                  scalar2=1e-12, op0=ALU.mult, op1=ALU.max), "vc")
            C.w(v, "vc", C.n["vc"])
            C.inc(v.reciprocal(sc16[:, 8:9], sc16[:, 7:8]), "vc")
            C.w(v, "vc", C.n["vc"])
            C.inc(v.tensor_mul(sc16[:, 8:9], sc16[:, 8:9], sc16[:, 6:7]), "vc")
        b.vector(_)
        full_barrier(C)

        stda = C.stream("l")
        stdb = C.stream("l2")
        stdc = C.stream("w")
        stspd = C.stream("p")
        v0 = C.n["vc"]

        def _(sync, v0=v0, stda=stda, stdb=stdb, stdc=stdc):
            for k in range(ntile):
                e, g = divmod(k, NG)
                if k >= 2:
                    C.w(sync, "vc", v0 + (k - 1))
                stda.issue(sync.dma_start(out=gi_t[:, k % 2, :], in_=sp_gi[e, g, :, :]), k)
                stdb.issue(sync.dma_start(out=sigu[:, k % 2, :], in_=sp_sig[e, g, :, :]), k)
                stdc.issue(sync.dma_start(out=uptile[:, k % 2, :], in_=sp_up[e, g, :, :]), k)
        b.sync(_)

        def _(v, stda=stda, stdb=stdb, stdc=stdc, stspd=stspd):
            for k in range(ntile):
                sl = k % 2
                stda.wait(v, k)
                stdb.wait(v, k)
                stdc.wait(v, k)
                if k >= 2:
                    stspd.wait(v, k - 2)
                v.tensor_tensor(out=tp[:, sl, :], in0=gi_t[:, sl, :], in1=sigu[:, sl, :],
                                op=ALU.mult)
                v.tensor_scalar(out=g2t[:, sl, :], in0=tp[:, sl, :], scalar1=sc16[:, 8:9],
                                scalar2=None, op0=ALU.mult)
                v.tensor_scalar(out=uit[:, sl, :], in0=uptile[:, sl, :], scalar1=sc16[:, 3:4],
                                scalar2=None, op0=ALU.mult)
                v.tensor_tensor(out=ptile[:, sl, :], in0=g2t[:, sl, :], in1=uit[:, sl, :],
                                op=ALU.mult)
                C.inc(v.tensor_reduce(out=tmaxc[:, k:k + 1], in_=ptile[:, sl, :], axis=AX,
                                      op=ALU.max, apply_absolute_value=True), "vc")
        b.vector(_)

        def _(g2, v0=v0, stspd=stspd):
            for k in range(ntile):
                e, g = divmod(k, NG)
                C.w(g2, "vc", v0 + k + 1)
                stspd.issue(g2.dma_start(out=sp_p[e, g, :, :], in_=ptile[:, k % 2, :]), k)
        b.gpsimd(_)

        def _(v, stspd=stspd):
            stspd.wait_all(v)
            C.w(v, "vc", C.n["vc"])
            C.inc(v.tensor_reduce(out=colv[:, 0:1], in_=tmaxc[:], axis=AX, op=ALU.max), "vc")
        b.vector(_)
        emit_allreduce_max(C, colv, 1)
        sp_bc = C.n["px"]

        def _(v, sp_bc=sp_bc):
            C.w(v, "px", sp_bc)
            C.inc(v.tensor_mul(sc16[:, 9:10], sc16[:, 7:8], sc16[:, 2:3]), "vc")
            C.w(v, "vc", C.n["vc"])
            C.inc(v.tensor_mul(sc16[:, 10:11], sc16[:, 9:10], bc16[:, 0:1]), "vc")
            C.w(v, "vc", C.n["vc"])
            C.inc(v.tensor_scalar(out=sc16[:, 10:11], in0=sc16[:, 10:11], scalar1=1.0 / QMAX,
                                  scalar2=1e-12, op0=ALU.mult, op1=ALU.max), "vc")
            C.w(v, "vc", C.n["vc"])
            C.inc(v.reciprocal(sc16[:, 11:12], sc16[:, 10:11]), "vc")
            C.w(v, "vc", C.n["vc"])
            C.inc(v.tensor_mul(sc16[:, 11:12], sc16[:, 11:12], sc16[:, 9:10]), "vc")
        b.vector(_)
        full_barrier(C)

        with contextlib.ExitStack() as pes2:
            PE2 = pes2.enter_context
            ogrp = PE2(nc.sbuf_tensor(f"f{call}_og", [128, IC, GT], BF16))
            wdsb = PE2(nc.sbuf_tensor(f"f{call}_wd", [128, IC, D], BF16))
            oi_t = PE2(nc.sbuf_tensor(f"f{call}_oi", [128, 2, GT], I16))
            xrt = PE2(nc.sbuf_tensor(f"f{call}_xr", [128, 4, D], F32))
            xnew = PE2(nc.sbuf_tensor(f"f{call}_xn", [128, 4, D], F32))
            wdh = io["wdT_h"].rearrange("(c p) e -> c p e", p=128)

            def _(sync):
                for e in range(IC):
                    C.dinc(sync.dma_start(out=wdsb[:, e, :], in_=wdh[e, :, :]), "lx")
                C.fence(sync, "lx")
            b.sync(_)
            ld_wd = C.n["lx"]

            for g in range(NG):
                stpe = C.stream("l")
                v0 = C.n["vc"]
                t0n = C.n["tc"]
                c10 = C.n["c1"]
                lx0 = C.n["lx"]

                def _(sync, g=g, v0=v0, stpe=stpe):
                    for e in range(IC):
                        if e >= 2:
                            C.w(sync, "vc", v0 + (e - 1))
                        stpe.issue(sync.dma_start(out=ptile[:, e % 2, :], in_=sp_p[e, g, :, :]), e)
                    for tt in range(4):
                        C.dinc(sync.dma_start(out=xrt[:, tt, :], in_=xsrc[g * 4 + tt, :, :]), "lx")
                b.sync(_)

                def _(v, stpe=stpe):
                    for e in range(IC):
                        stpe.wait(v, e)
                        v.tensor_scalar(out=oi_t[:, e % 2, :], in0=ptile[:, e % 2, :],
                                        scalar1=sc16[:, 11:12], scalar2=None, op0=ALU.mult)
                        C.inc(v.tensor_scalar(out=ogrp[:, e, :], in0=oi_t[:, e % 2, :],
                                              scalar1=sc16[:, 10:11], scalar2=None,
                                              op0=ALU.mult), "vc")
                b.vector(_)
                v_o = C.n["vc"]
                lx_x = lx0 + 16 * 4

                def _(t, g=g, v_o=v_o):
                    C.w(t, "vc", v_o)
                    C.w(t, "lx", ld_wd)
                    for pair in range(2):
                        if pair == 1:
                            C.w(t, "vc", v_o + 2)
                        for e in range(IC):
                            for t2 in range(2):
                                tt = pair * 2 + t2
                                t.matmul(ps[2 * t2][:], ogrp[:, e, tt * 128:(tt + 1) * 128],
                                         wdsb[:, e, 0:GT],
                                         start=(e == 0), stop=(e == IC - 1))
                                ins = t.matmul(ps[2 * t2 + 1][:],
                                               ogrp[:, e, tt * 128:(tt + 1) * 128],
                                               wdsb[:, e, GT:D],
                                               start=(e == 0), stop=(e == IC - 1))
                            C.inc(ins, "c1")
                b.tensor(_)

                def _(v, g=g, c10=c10, lx_x=lx_x):
                    for pair in range(2):
                        C.w(v, "c1", c10 + IC * (pair + 1))
                        C.w(v, "lx", lx_x)
                        for t2 in range(2):
                            tt = pair * 2 + t2
                            v.scalar_tensor_tensor(out=xnew[:, tt, 0:GT], in0=ps[2 * t2][:],
                                                   scalar=0.5, in1=xrt[:, tt, 0:GT],
                                                   op0=ALU.mult, op1=ALU.add)
                            C.inc(v.scalar_tensor_tensor(out=xnew[:, tt, GT:D],
                                                         in0=ps[2 * t2 + 1][:],
                                                         scalar=0.5, in1=xrt[:, tt, GT:D],
                                                         op0=ALU.mult, op1=ALU.add), "vc")
                b.vector(_)

                def _(g2, g=g):
                    for tt in range(4):
                        C.w(g2, "vc", C.n["vc"] - (3 - tt))
                        C.dinc(g2.dma_start(out=xdst[g * 4 + tt, :, :], in_=xnew[:, tt, :]), "px")
                b.gpsimd(_)
                full_barrier(C)

    full_barrier(C)



def emit_attn(C, xsrc, xdst, io):
    nc, b = C.nc, C.block
    bufs = C.bufs
    ps, psT2, psS = bufs["ps"], bufs["psT2"], bufs["psS"]
    identb = bufs["identb"]
    mu0_c, r0_c = bufs["mu0_c"], bufs["r0_c"]
    b0_c, r1_c = bufs["b0_c"], bufs["r1_c"]
    exp_ls = bufs["exp_ls"]
    bom_row, b2_row = bufs["bom_row"], bufs["b2_row"]
    ones_row, ones_colb = bufs["ones_row"], bufs["ones_colb"]
    const_bf, colsum_bf = bufs["const_bf"], bufs["colsum_bf"]
    carry, carry_r, carry_i = bufs["carry"], bufs["carry_r"], bufs["carry_i"]
    off_phi, off_r, off_i = bufs["off_phi"], bufs["off_r"], bufs["off_i"]
    colv = bufs["colv"]
    sp_phi, sp_xln = io["sp_phi"], io["sp_xln"]
    sp_cos, sp_sin = io["sp_cos"], io["sp_sin"]
    sp_cmr, sp_cmi = io["sp_cmr"], io["sp_cmi"]
    sp_Sr, sp_Si = io["sp_Sr"], io["sp_Si"]

    emit_ln0_stats(C, xsrc, False, mu0_c, r0_c)

    def _(v):
        C.w(v, "vc", C.n["vc"])
        C.inc(v.tensor_mul(b0_c[:], mu0_c[:], r0_c[:]), "vc")
        C.w(v, "vc", C.n["vc"])
        v.tensor_scalar(out=b0_c[:], in0=b0_c[:], scalar1=-1.0, scalar2=None, op0=ALU.mult)
        v.tensor_scalar(out=r1_c[:], in0=r0_c[:], scalar1=1.0, scalar2=None, op0=ALU.add)
        v.memset(carry[:], 0.0)
        v.memset(carry_r[:], 0.0)
        C.inc(v.memset(carry_i[:], 0.0), "vc")
    b.vector(_)
    full_barrier(C)

    with contextlib.ExitStack() as pes:
        PE = pes.enter_context
        xa = bufs["xstats"]
        xlntm = PE(nc.sbuf_tensor("a_xlntm", [128, 2, D], BF16))
        xlnT = PE(nc.sbuf_tensor("a_xlnT", [128, DC, GT], BF16))
        womsb = PE(nc.sbuf_tensor("a_wom", [128, DC, D], BF16))
        oma = PE(nc.sbuf_tensor("a_oma", [128, DC, GT], F32))
        invsqt = PE(nc.sbuf_tensor("a_invsq", [128, GT], F32))

        womt = io["womT_h"].rearrange("(c p) e -> c p e", p=128)

        def _(sync):
            for d in range(DC):
                C.dinc(sync.dma_start(out=womsb[:, d, :], in_=womt[d, :, :]), "lx")
            C.fence(sync, "lx")
        b.sync(_)
        ld_wom = C.n["lx"]

        for g in range(NG):
            stxa = C.stream("l")
            lx0 = C.n["lx"]
            s0 = C.n["sc"]
            v0 = C.n["vc"]
            t0n = C.n["tc"]

            def _(sync, g=g, v0=v0, stxa=stxa):
                C.dinc(sync.dma_start(out=invsqt[:],
                                      in_=io["invsq_in"][0:1, g * GT:(g + 1) * GT]
                                      .broadcast_to((128, GT))), "lx")
                for tt in range(4):
                    if tt >= 2:
                        C.w(sync, "vc", v0 + (tt - 1) * DC)
                    stxa.issue(sync.dma_start(out=xa[:, tt % 2, :], in_=xsrc[g * 4 + tt, :, :]), tt)
            b.sync(_)

            def _(s, g=g, stxa=stxa):
                for tt in range(4):
                    i = g * 4 + tt
                    stxa.wait(s, tt)
                    C.inc(s.activation(out=xlntm[:, tt % 2, :], in_=xa[:, tt % 2, :],
                                       func=AF.Identity,
                                       scale=r0_c[:, i:i + 1], bias=b0_c[:, i:i + 1]), "sc")
            b.scalar(_)

            def _(t, s0=s0, v0=v0):
                for tt in range(4):
                    C.w(t, "sc", s0 + tt + 1)
                    for dc in range(DC):
                        k = tt * DC + dc
                        if k >= 2:
                            C.w(t, "vc", v0 + k - 1)
                        C.inc(t.transpose(psT2[k % 2][:, 0:128],
                                          xlntm[:, tt % 2, dc * 128:(dc + 1) * 128],
                                          identb[:]), "tc")
            b.tensor(_)

            def _(v, t0n=t0n):
                for tt in range(4):
                    for dc in range(DC):
                        k = tt * DC + dc
                        C.w(v, "tc", t0n + k + 1)
                        C.inc(v.tensor_copy(xlnT[:, dc, tt * 128:(tt + 1) * 128],
                                            psT2[k % 2][:, 0:128]), "vc")
            b.vector(_)
            v_xlnT = C.n["vc"]
            t_tr = C.n["tc"]

            def _(g2, g=g, v_xlnT=v_xlnT):
                C.w(g2, "vc", v_xlnT)
                C.dinc(g2.dma_start(out=sp_xln[g, :, :, :], in_=xlnT[:]), "px")
            b.gpsimd(_)

            s_ev0 = C.n["sc"]

            def _(t, v_xlnT=v_xlnT, t_tr=t_tr, s_ev0=s_ev0):
                C.w(t, "vc", v_xlnT)
                C.w(t, "lx", ld_wom)
                for ec in range(DC):
                    bank = 4 + (ec % 2)
                    if ec >= 2:
                        C.w(t, "sc", s_ev0 + ec - 1)
                    for d in range(DC):
                        t.matmul(ps[bank][:], womsb[:, d, ec * 128:(ec + 1) * 128],
                                 xlnT[:, d, :], start=(d == 0), stop=False)
                    C.inc(t.matmul(ps[bank][:], bom_row[0:1, ec * 128:(ec + 1) * 128],
                                   ones_row[0:1, :], start=False, stop=True), "tc")
            b.tensor(_)
            t_om0 = t_tr

            def _(s, t_om0=t_om0):
                for ec in range(DC):
                    C.w(s, "tc", t_om0 + ec + 1)
                    C.inc(s.activation(out=oma[:, ec, :], in_=ps[4 + (ec % 2)][:],
                                       func=AF.Identity, scale=exp_ls[:, ec:ec + 1]), "sc")
            b.scalar(_)

            def _(v, g=g, lx0=lx0):
                C.w(v, "sc", C.n["sc"])
                C.w(v, "lx", lx0 + 16)
                v.tensor_tensor(out=oma[:], in0=oma[:],
                                in1=invsqt[:].rearrange("p (o t) -> p o t", o=1)
                                .broadcast_to((128, DC, GT)),
                                op=ALU.mult)
                for ec in range(DC):
                    ins_scan = v.tensor_tensor_scan(out=oma[:, ec, :], data0=oma[:, ec, :],
                                                    data1=oma[:, ec, :],
                                                    initial=carry[:, ec:ec + 1],
                                                    op0=ALU.add, op1=ALU.bypass)
                C.w(v, "vc", C.inc(ins_scan, "vc"))
                C.inc(v.tensor_copy(carry[:], oma[:, :, GT - 1]), "vc")
            b.vector(_)

            def _(g2, g=g):
                C.w(g2, "vc", C.n["vc"])
                C.dinc(g2.dma_start(out=sp_phi[g, :, :, :], in_=oma[:]), "px")
            b.gpsimd(_)
            full_barrier(C)

        emit_carry_gather(C, [carry], [off_phi])

        kint = PE(nc.sbuf_tensor("a_k", [128, DC, GT], I32))
        phi1 = PE(nc.sbuf_tensor("a_phi1", [128, DC, GT], F32))
        cosb = PE(nc.sbuf_tensor("a_cos", [128, DC, GT], BF16))
        sinb = PE(nc.sbuf_tensor("a_sin", [128, DC, GT], BF16))
        cmr = PE(nc.sbuf_tensor("a_cmr", [128, DC, GT], BF16))
        cmi = PE(nc.sbuf_tensor("a_cmi", [128, DC, GT], BF16))
        Srb = PE(nc.sbuf_tensor("a_Sr", [128, DC, GT], F32))
        Sib = PE(nc.sbuf_tensor("a_Si", [128, DC, GT], F32))

        for g in range(NG):
            lx0 = C.n["lx"]

            def _(sync, g=g):
                C.dinc(sync.dma_start(out=oma[:], in_=sp_phi[g, :, :, :]), "lx")
                C.dinc(sync.dma_start(out=xlnT[:], in_=sp_xln[g, :, :, :]), "lx")
            b.sync(_)

            def _(v, lx0=lx0):
                C.w(v, "lx", lx0 + 32)
                for ec in range(DC):
                    v.tensor_scalar(out=oma[:, ec, :], in0=oma[:, ec, :],
                                    scalar1=off_phi[:, ec:ec + 1], scalar2=None, op0=ALU.add)
                C.inc(v.tensor_copy(junk_one(C), oma[:, 0, 0:1]), "vc")
            b.vector(_)
            v_phi = C.n["vc"]

            def _(s, v_phi=v_phi):
                C.w(s, "vc", v_phi)
                C.inc(s.activation(out=kint[:], in_=oma[:], func=AF.Identity,
                                   scale=float(1.0 / TWO_PI)), "sc")
            b.scalar(_)
            s_k = C.n["sc"]

            def _(v, s_k=s_k):
                C.w(v, "sc", s_k)
                v.scalar_tensor_tensor(out=phi1[:], in0=kint[:], scalar=-RC1, in1=oma[:],
                                       op0=ALU.mult, op1=ALU.add)
                v.scalar_tensor_tensor(out=phi1[:], in0=kint[:], scalar=-RC2, in1=phi1[:],
                                       op0=ALU.mult, op1=ALU.add)
                v.tensor_scalar(out=Srb[:], in0=phi1[:], scalar1=PIF, scalar2=float(-TWO_PI),
                                op0=ALU.is_gt, op1=ALU.mult)
                v.tensor_add(phi1[:], phi1[:], Srb[:])
                v.tensor_scalar(out=Srb[:], in0=phi1[:], scalar1=-PIF, scalar2=float(TWO_PI),
                                op0=ALU.is_lt, op1=ALU.mult)
                v.tensor_add(phi1[:], phi1[:], Srb[:])
                C.inc(v.tensor_scalar(out=phi1[:], in0=phi1[:], scalar1=PIF,
                                      scalar2=-PIF, op0=ALU.min, op1=ALU.max), "vc")
            b.vector(_)
            v_ph1 = C.n["vc"]

            def _(s, v_ph1=v_ph1):
                C.w(s, "vc", v_ph1)
                C.inc(s.activation(out=sinb[:], in_=phi1[:], func=AF.Sin), "sc")
            b.scalar(_)
            s_sin = C.n["sc"]

            def _(v, s_sin=s_sin):
                C.w(v, "sc", s_sin)
                v.tensor_scalar(out=oma[:], in0=phi1[:], scalar1=PIH,
                                scalar2=float(-TWO_PI), op0=ALU.is_gt, op1=ALU.mult)
                C.inc(v.scalar_tensor_tensor(out=oma[:], in0=phi1[:], scalar=PIH,
                                             in1=oma[:], op0=ALU.add, op1=ALU.add), "vc")
            b.vector(_)
            v_psi = C.n["vc"]

            def _(s, v_psi=v_psi):
                C.w(s, "vc", v_psi)
                C.inc(s.activation(out=cosb[:], in_=oma[:], func=AF.Sin), "sc")
            b.scalar(_)
            s_cos = C.n["sc"]

            def _(v, s_cos=s_cos):
                C.w(v, "sc", s_cos)
                v.tensor_tensor(out=cmr[:], in0=xlnT[:], in1=cosb[:], op=ALU.mult)
                v.tensor_tensor(out=cmi[:], in0=xlnT[:], in1=sinb[:], op=ALU.mult)
                for ec in range(DC):
                    v.tensor_tensor_scan(out=Srb[:, ec, :], data0=cmr[:, ec, :],
                                         data1=cmr[:, ec, :], initial=carry_r[:, ec:ec + 1],
                                         op0=ALU.add, op1=ALU.bypass)
                    ins_scan2 = v.tensor_tensor_scan(out=Sib[:, ec, :], data0=cmi[:, ec, :],
                                                     data1=cmi[:, ec, :],
                                                     initial=carry_i[:, ec:ec + 1],
                                                     op0=ALU.add, op1=ALU.bypass)
                C.w(v, "vc", C.inc(ins_scan2, "vc"))
                v.tensor_copy(carry_r[:], Srb[:, :, GT - 1])
                C.inc(v.tensor_copy(carry_i[:], Sib[:, :, GT - 1]), "vc")
            b.vector(_)

            def _(g2, g=g):
                C.w(g2, "vc", C.n["vc"])
                C.dinc(g2.dma_start(out=sp_cos[g, :, :, :], in_=cosb[:]), "px")
                C.dinc(g2.dma_start(out=sp_sin[g, :, :, :], in_=sinb[:]), "px")
                C.dinc(g2.dma_start(out=sp_cmr[g, :, :, :], in_=cmr[:]), "px")
                C.dinc(g2.dma_start(out=sp_cmi[g, :, :, :], in_=cmi[:]), "px")
                C.dinc(g2.dma_start(out=sp_Sr[g, :, :, :], in_=Srb[:]), "px")
                C.dinc(g2.dma_start(out=sp_Si[g, :, :, :], in_=Sib[:]), "px")
            b.gpsimd(_)
            full_barrier(C)

        emit_carry_gather(C, [carry_r, carry_i], [off_r, off_i])

    with contextlib.ExitStack() as pes:
        PE = pes.enter_context
        ctxb = PE(nc.sbuf_tensor("a3_ctx", [128, FC, GT], BF16))
        Srb = PE(nc.sbuf_tensor("a3_Sr", [128, DC, GT], F32))
        Sib = PE(nc.sbuf_tensor("a3_Si", [128, DC, GT], F32))
        cosb = PE(nc.sbuf_tensor("a3_cos", [128, DC, GT], BF16))
        sinb = PE(nc.sbuf_tensor("a3_sin", [128, DC, GT], BF16))
        tmp1 = PE(nc.sbuf_tensor("a3_t1", [128, GT], BF16))
        tmp2 = PE(nc.sbuf_tensor("a3_t2", [128, GT], BF16))
        invpt = PE(nc.sbuf_tensor("a3_invp", [128, GT], F32))
        w1st = PE(nc.sbuf_tensor("a3_w1", [128, 4, GT], BF16))
        w2sb = PE(nc.sbuf_tensor("a3_w2", [128, EC2, D], BF16))
        sqc = PE(nc.sbuf_tensor("a3_sq", [128, 2, GT], BF16))
        hT = PE(nc.sbuf_tensor("a3_hT", [128, 4, EC2, 128], BF16))
        stt4 = PE(nc.sbuf_tensor("a3_st", [128, 16], F32))
        rows_f = PE(nc.sbuf_tensor("a3_rowf", [1, GT], F32))
        negmu_rb = PE(nc.sbuf_tensor("a3_nmu", [1, GT], BF16))
        invr_rb = PE(nc.sbuf_tensor("a3_ivr", [1, GT], BF16))
        rows_f2 = PE(nc.sbuf_tensor("a3_rowf2", [1, GT], F32))
        xrt = PE(nc.sbuf_tensor("a3_xr", [128, 2, D], F32))
        xnew = PE(nc.sbuf_tensor("a3_xn", [128, 2, D], F32))

        w1gt = io["w1g_h"].rearrange("(c p) e -> c p e", p=128)
        w2t = io["w2T_h"].rearrange("(c p) e -> c p e", p=128)

        def _(sync):
            for e2c in range(EC2):
                C.dinc(sync.dma_start(out=w2sb[:, e2c, :], in_=w2t[e2c, :, :]), "lx")
            C.fence(sync, "lx")
        b.sync(_)
        ld_w2 = C.n["lx"]

        for g in range(NG):
            def _(sync, g=g):
                C.dinc(sync.dma_start(out=Srb[:], in_=sp_Sr[g, :, :, :]), "lx")
                C.dinc(sync.dma_start(out=Sib[:], in_=sp_Si[g, :, :, :]), "lx")
                C.dinc(sync.dma_start(out=cosb[:], in_=sp_cos[g, :, :, :]), "lx")
                C.dinc(sync.dma_start(out=sinb[:], in_=sp_sin[g, :, :, :]), "lx")
                C.dinc(sync.dma_start(out=ctxb[:, 0:DC, :], in_=sp_cmr[g, :, :, :]), "lx")
                C.dinc(sync.dma_start(out=ctxb[:, DC:2 * DC, :], in_=sp_cmi[g, :, :, :]), "lx")
                C.dinc(sync.dma_start(out=invpt[:],
                                      in_=io["invp_in"][0:1, g * GT:(g + 1) * GT]
                                      .broadcast_to((128, GT))), "lx")
                C.fence(sync, "lx")
            b.sync(_)
            ld_sp = C.n["lx"]

            def _(v, ld_sp=ld_sp):
                C.w(v, "lx", ld_sp)
                for ec in range(DC):
                    v.tensor_scalar(out=Srb[:, ec, :], in0=Srb[:, ec, :],
                                    scalar1=off_r[:, ec:ec + 1], scalar2=None, op0=ALU.add)
                    v.tensor_scalar(out=Sib[:, ec, :], in0=Sib[:, ec, :],
                                    scalar1=off_i[:, ec:ec + 1], scalar2=None, op0=ALU.add)
                for ec in range(DC):
                    v.tensor_tensor(out=tmp1[:], in0=Srb[:, ec, :], in1=cosb[:, ec, :], op=ALU.mult)
                    v.tensor_tensor(out=tmp2[:], in0=Sib[:, ec, :], in1=sinb[:, ec, :], op=ALU.mult)
                    v.tensor_add(tmp1[:], tmp1[:], tmp2[:])
                    v.tensor_tensor(out=ctxb[:, 2 * DC + ec, :], in0=tmp1[:], in1=invpt[:], op=ALU.mult)
                    v.tensor_tensor(out=tmp1[:], in0=Sib[:, ec, :], in1=cosb[:, ec, :], op=ALU.mult)
                    v.tensor_tensor(out=tmp2[:], in0=Srb[:, ec, :], in1=sinb[:, ec, :], op=ALU.mult)
                    v.tensor_sub(tmp1[:], tmp1[:], tmp2[:])
                    v.tensor_tensor(out=ctxb[:, 3 * DC + ec, :], in0=tmp1[:], in1=invpt[:], op=ALU.mult)
                C.inc(v.tensor_copy(junk_one(C), ctxb[:, FC - 1, 0:1]), "vc")
            b.vector(_)
            v_ctx = C.n["vc"]

            s_sq0 = C.n["sc"]
            c20 = C.n["c2"]

            def _(s, v_ctx=v_ctx, c20=c20):
                C.w(s, "vc", v_ctx)
                for fc in range(FC):
                    if fc >= 2:
                        C.w(s, "c2", c20 + (fc - 1))
                    C.inc(s.activation(out=sqc[:, fc % 2, :], in_=ctxb[:, fc, :],
                                       func=AF.Square), "sc")
            b.scalar(_)

            stw1a = C.stream("l")
            stw1b = C.stream("w")
            c10 = C.n["c1"]

            def _(sync, g=g, v_ctx=v_ctx, c10=c10, stw1a=stw1a, stw1b=stw1b):
                C.w(sync, "vc", v_ctx)
                for e2g in range(4):
                    for fc in range(FC):
                        k = e2g * FC + fc
                        if k >= 4:
                            C.w(sync, "c1", c10 + (k - 3))
                        st = (stw1a, stw1b)[k % 2]
                        st.issue(sync.dma_start(out=w1st[:, k % 4, :],
                                                in_=w1gt[fc, :, e2g * GT:(e2g + 1) * GT]), k // 2)
            b.sync(_)

            t_g0 = C.n["tc"]
            c10_t = c10
            c30 = C.n["c3"]
            v_rows = v_ctx + 12
            s_gelu0 = s_sq0 + FC + 1

            def _(t, v_ctx=v_ctx, stw1a=stw1a, stw1b=stw1b, s_sq0=s_sq0, v_rows=v_rows,
                  s_gelu0=s_gelu0, c30=c30):
                C.w(t, "vc", v_ctx)
                for e2g in range(4):
                    if e2g > 0:
                        C.w(t, "sc", s_gelu0 + 4 * e2g)
                    for fc in range(FC):
                        k = e2g * FC + fc
                        (stw1a, stw1b)[k % 2].wait(t, k // 2)
                        if e2g == 0:
                            C.w(t, "sc", s_sq0 + fc + 1)
                        for tt in range(4):
                            mm_w1 = t.matmul(ps[tt][:], ctxb[:, fc, tt * 128:(tt + 1) * 128],
                                             w1st[:, k % 4, :], start=(fc == 0), stop=False)
                            if e2g == 0:
                                t.matmul(psS[:, 2 * tt:2 * tt + 1],
                                         ctxb[:, fc, tt * 128:(tt + 1) * 128],
                                         ones_colb[:, 0:1],
                                         start=(fc == 0 and tt == 0), stop=False)
                                mm_sq = t.matmul(psS[:, 2 * tt + 1:2 * tt + 2],
                                                 sqc[:, fc % 2, tt * 128:(tt + 1) * 128],
                                                 ones_colb[:, 0:1], start=False,
                                                 stop=(fc == FC - 1 and tt == 3))
                        C.inc(mm_w1, "c1")
                        if e2g == 0:
                            C.inc(mm_sq, "c2")
                    if e2g == 0:
                        C.w(t, "vc", v_rows)
                    for tt in range(4):
                        t.matmul(ps[tt][:], negmu_rb[0:1, tt * 128:(tt + 1) * 128],
                                 colsum_bf[0:1, e2g * GT:(e2g + 1) * GT],
                                 start=False, stop=False)
                        last_r1 = t.matmul(ps[tt][:], invr_rb[0:1, tt * 128:(tt + 1) * 128],
                                           const_bf[0:1, e2g * GT:(e2g + 1) * GT],
                                           start=False, stop=True)
                    C.inc(last_r1, "c3")
            b.tensor(_)

            def _(v, c20=c20):
                C.w(v, "c2", c20 + FC)
                v.tensor_scalar(out=stt4[:, 0:4], in0=psS[:, 0:8:2], scalar1=1.0 / D4,
                                scalar2=None, op0=ALU.mult)
                C.inc(v.tensor_scalar(out=stt4[:, 4:8], in0=psS[:, 1:8:2], scalar1=1.0 / D4,
                                      scalar2=None, op0=ALU.mult), "vc")
                C.w(v, "vc", C.n["vc"])
                C.inc(v.tensor_mul(stt4[:, 8:12], stt4[:, 0:4], stt4[:, 0:4]), "vc")
                C.w(v, "vc", C.n["vc"])
                C.inc(v.tensor_sub(stt4[:, 4:8], stt4[:, 4:8], stt4[:, 8:12]), "vc")
                C.w(v, "vc", C.n["vc"])
                C.inc(v.tensor_scalar(out=stt4[:, 4:8], in0=stt4[:, 4:8], scalar1=LN_EPS,
                                      scalar2=None, op0=ALU.add), "vc")
            b.vector(_)
            v_ve = C.n["vc"]

            def _(s, v_ve=v_ve):
                C.w(s, "vc", v_ve)
                C.inc(s.sqrt(stt4[:, 8:12], stt4[:, 4:8]), "sc")
            b.scalar(_)
            s_sv = C.n["sc"]

            def _(v, s_sv=s_sv):
                C.w(v, "sc", s_sv)
                C.inc(v.reciprocal(stt4[:, 12:16], stt4[:, 8:12]), "vc")
                C.w(v, "vc", C.n["vc"])
                C.inc(v.tensor_mul(stt4[:, 0:4], stt4[:, 12:16], stt4[:, 12:16]), "vc")
                C.w(v, "vc", C.n["vc"])
                C.inc(v.tensor_mul(stt4[:, 0:4], stt4[:, 0:4], stt4[:, 4:8]), "vc")
                C.w(v, "vc", C.n["vc"])
                C.inc(v.tensor_scalar(out=stt4[:, 0:4], in0=stt4[:, 0:4], scalar1=-0.5,
                                      scalar2=1.5, op0=ALU.mult, op1=ALU.add), "vc")
                C.w(v, "vc", C.n["vc"])
                C.inc(v.tensor_mul(stt4[:, 12:16], stt4[:, 12:16], stt4[:, 0:4]), "vc")
                C.w(v, "vc", C.n["vc"])
                C.inc(v.tensor_scalar(out=stt4[:, 0:4], in0=psS[:, 0:8:2],
                                      scalar1=-1.0 / D4, scalar2=None, op0=ALU.mult), "vc")
            b.vector(_)
            v_nm = C.n["vc"]

            def _(g2, v_nm=v_nm):
                C.w(g2, "vc", v_nm)
                C.fence(g2, "px")
                for tt in range(4):
                    C.dinc(g2.dma_start(out=rows_f[0:1, tt * 128:(tt + 1) * 128],
                                        in_=stt4[:, tt:tt + 1]), "px")
                    C.dinc(g2.dma_start(out=rows_f2[0:1, tt * 128:(tt + 1) * 128],
                                        in_=stt4[:, 8 + tt:9 + tt]), "px")
            b.gpsimd(_)
            g_rows = C.n["px"]

            def _(v, g_rows=g_rows):
                C.w(v, "px", g_rows)
                C.inc(v.tensor_copy(negmu_rb[0:1, :], rows_f[0:1, :]), "vc")
                C.inc(v.tensor_copy(invr_rb[0:1, :], rows_f2[0:1, :]), "vc")
            b.vector(_)
            assert C.n["vc"] == v_rows, (C.n["vc"], v_rows)

            def hslot(k):
                return cosb[:, k, :] if k < DC else sinb[:, k - DC, :]

            def _(s, c30=c30, s_gelu0=s_gelu0):
                assert C.n["sc"] == s_gelu0, (C.n["sc"], s_gelu0)
                for e2g in range(4):
                    C.w(s, "c3", c30 + e2g + 1)
                    for tt in range(4):
                        C.inc(s.activation(out=hslot(e2g * 4 + tt), in_=ps[tt][:],
                                           func=AF.Gelu,
                                           scale=stt4[:, 12 + tt:13 + tt]), "sc")
            b.scalar(_)

            t_tr0 = C.n["tc"]
            v_tr0 = v_rows

            def _(t, s_gelu0=s_gelu0, v_tr0=v_tr0):
                C.w(t, "sc", s_gelu0 + 16)
                n = 0
                for e2g in range(4):
                    for tt in range(4):
                        for q in range(4):
                            if n >= 2:
                                C.w(t, "vc", v_tr0 + n - 1)
                            C.inc(t.transpose(
                                psT2[n % 2][:, 0:128],
                                hslot(e2g * 4 + tt)[:, q * 128:(q + 1) * 128],
                                identb[:]), "tc")
                            n += 1
            b.tensor(_)

            def _(v, t_tr0=t_tr0):
                n = 0
                for e2g in range(4):
                    for tt in range(4):
                        for q in range(4):
                            n += 1
                            C.w(v, "tc", t_tr0 + n)
                            C.inc(v.tensor_copy(hT[:, tt, e2g * 4 + q, :],
                                                psT2[(n - 1) % 2][:, 0:128]), "vc")
            b.vector(_)
            v_resid0 = C.n["vc"]
            assert v_resid0 == v_tr0 + 64

            t_w20 = C.n["tc"]
            stx3 = C.stream("l2")
            stst3 = C.stream("p")
            v_resid_base = v_resid0

            def _(t, v_resid0=v_resid0, t_w20=t_w20):
                C.w(t, "vc", v_resid0)
                C.w(t, "lx", ld_w2)
                for tt in range(4):
                    bank0 = (tt % 2) * 2
                    if tt >= 2:
                        C.w(t, "vc", v_resid0 + (tt - 1))
                    for e2c in range(EC2):
                        t.matmul(ps[bank0][:], hT[:, tt, e2c, :], w2sb[:, e2c, 0:GT],
                                 start=(e2c == 0), stop=False)
                        t.matmul(ps[bank0 + 1][:], hT[:, tt, e2c, :], w2sb[:, e2c, GT:D],
                                 start=(e2c == 0), stop=False)
                    t.matmul(ps[bank0][:], ones_row[0:1, 0:128], b2_row[0:1, 0:GT],
                             start=False, stop=True)
                    C.inc(t.matmul(ps[bank0 + 1][:], ones_row[0:1, 0:128],
                                   b2_row[0:1, GT:D], start=False, stop=True), "tc")
            b.tensor(_)

            def _(v, g=g, t_w20=t_w20, stx3=stx3):
                for tt in range(4):
                    i = g * 4 + tt
                    bank0 = (tt % 2) * 2
                    C.w(v, "tc", t_w20 + tt + 1)
                    stx3.wait(v, tt)
                    v.tensor_scalar(out=xnew[:, tt % 2, :], in0=xrt[:, tt % 2, :],
                                    scalar1=r1_c[:, i:i + 1], scalar2=b0_c[:, i:i + 1],
                                    op0=ALU.mult, op1=ALU.add)
                    v.tensor_add(xnew[:, tt % 2, 0:GT], xnew[:, tt % 2, 0:GT], ps[bank0][:])
                    C.inc(v.tensor_add(xnew[:, tt % 2, GT:D], xnew[:, tt % 2, GT:D],
                                       ps[bank0 + 1][:]), "vc")
            b.vector(_)
            v_res_end = C.n["vc"]

            def _(sync, g=g, v_resid_base=v_resid_base, stx3=stx3, stst3=stst3):
                for tt in range(4):
                    if tt >= 2:
                        C.w(sync, "vc", v_resid_base + (tt - 1))
                        stst3.wait(sync, tt - 2)
                    stx3.issue(sync.dma_start(out=xrt[:, tt % 2, :], in_=xsrc[g * 4 + tt, :, :]), tt)
            b.sync(_)

            def _(g2, g=g, v_res_end=v_res_end, stst3=stst3):
                for tt in range(4):
                    C.w(g2, "vc", v_res_end - (3 - tt))
                    stst3.issue(g2.dma_start(out=xdst[g * 4 + tt, :, :], in_=xnew[:, tt % 2, :]), tt)
            b.gpsimd(_)
            full_barrier(C)

    full_barrier(C)


def junk_one(C):
    return C.bufs["junkb"][:, 0:1]



DEBUG_BUILD = False
STAGES = 3
SUBSTAGE = 0


def build_nc():
    nc = bass.Bass("TRN2", target_bir_lowering=False, debug=DEBUG_BUILD,
                   detect_race_conditions=False)
    io = {}
    io["x_in"] = nc.dram_tensor("x", [TPC, D], F32, kind="ExternalInput").ap()
    io["out"] = nc.dram_tensor("out", [TPC, D], F32, kind="ExternalOutput").ap()
    io["wup_raw"] = nc.dram_tensor("wup", [I3, D], F32, kind="ExternalInput").ap()
    io["wgate_raw"] = nc.dram_tensor("wgate", [I3, D], F32, kind="ExternalInput").ap()
    io["wdown_raw"] = nc.dram_tensor("wdown", [D, I3], F32, kind="ExternalInput").ap()
    io["womT_in"] = nc.dram_tensor("womT", [D, D], F32, kind="ExternalInput").ap()
    io["w1T_in"] = nc.dram_tensor("w1T", [D4, D2], F32, kind="ExternalInput").ap()
    io["w2T_in"] = nc.dram_tensor("w2T", [D2, D], F32, kind="ExternalInput").ap()
    io["bom_in"] = nc.dram_tensor("bom", [1, D], F32, kind="ExternalInput").ap()
    io["b1_in"] = nc.dram_tensor("b1", [1, D2], F32, kind="ExternalInput").ap()
    io["b2_in"] = nc.dram_tensor("b2", [1, D], F32, kind="ExternalInput").ap()
    io["gamma_in"] = nc.dram_tensor("gammac", [128, FC], F32, kind="ExternalInput").ap()
    io["beta_in"] = nc.dram_tensor("betac", [128, FC], F32, kind="ExternalInput").ap()
    io["ls_in"] = nc.dram_tensor("lsc", [128, DC], F32, kind="ExternalInput").ap()
    io["identb_in"] = nc.dram_tensor("identb", [128, 128], BF16, kind="ExternalInput").ap()
    io["invsq_in"] = nc.dram_tensor("invsq", [1, TPC], F32, kind="ExternalInput").ap()
    io["invp_in"] = nc.dram_tensor("invp", [1, TPC], F32, kind="ExternalInput").ap()
    io["cmask_in"] = nc.dram_tensor("cmask", [1, NCORES], F32, kind="ExternalInput").ap()

    io["xres"] = nc.dram_tensor("xres", [NT, 128, D], F32).ap()
    io["wupT_h"] = nc.dram_tensor("wupT_h", [D, I3], BF16).ap()
    io["wgT_h"] = nc.dram_tensor("wgT_h", [D, I3], BF16).ap()
    io["wdT_h"] = nc.dram_tensor("wdT_h", [I3, D], BF16).ap()
    io["womT_h"] = nc.dram_tensor("womT_h", [D, D], BF16).ap()
    io["w1g_h"] = nc.dram_tensor("w1g_h", [D4, D2], BF16).ap()
    io["w2T_h"] = nc.dram_tensor("w2T_h", [D2, D], BF16).ap()
    io["sp_up"] = nc.dram_tensor("sp_up", [IC, NG, 128, GT], BF16).ap()
    io["sp_gate"] = nc.dram_tensor("sp_gate", [IC, NG, 128, GT], BF16).ap()
    io["sp_gi"] = nc.dram_tensor("sp_gi", [IC, NG, 128, GT], I16).ap()
    io["sp_sig"] = nc.dram_tensor("sp_sig", [IC, NG, 128, GT], U16).ap()
    io["sp_p"] = nc.dram_tensor("sp_p", [IC, NG, 128, GT], F32).ap()
    io["sp_phi"] = nc.dram_tensor("sp_phi", [NG, 128, DC, GT], F32).ap()
    io["sp_xln"] = nc.dram_tensor("sp_xln", [NG, 128, DC, GT], BF16).ap()
    io["sp_cos"] = nc.dram_tensor("sp_cos", [NG, 128, DC, GT], BF16).ap()
    io["sp_sin"] = nc.dram_tensor("sp_sin", [NG, 128, DC, GT], BF16).ap()
    io["sp_cmr"] = nc.dram_tensor("sp_cmr", [NG, 128, DC, GT], BF16).ap()
    io["sp_cmi"] = nc.dram_tensor("sp_cmi", [NG, 128, DC, GT], BF16).ap()
    io["sp_Sr"] = nc.dram_tensor("sp_Sr", [NG, 128, DC, GT], F32).ap()
    io["sp_Si"] = nc.dram_tensor("sp_Si", [NG, 128, DC, GT], F32).ap()

    es = contextlib.ExitStack()
    with es:
        E = es.enter_context
        block = E(nc.Block())
        C = Ctx(nc, block, es)
        bufs = C.bufs
        bufs["cc_in"] = nc.dram_tensor("cc_in", [128, 2], F32).ap()
        bufs["cc_out"] = nc.dram_tensor("cc_out", [128, 2], F32).ap()
        bufs["rowb_h"] = nc.dram_tensor("rowb_h", [16, 8], F32).ap()
        bufs["ag_in8"] = nc.dram_tensor("ag_in8", [128, 8], F32).ap()
        bufs["ag_out8"] = nc.dram_tensor("ag_out8", [NCORES * 128, 8], F32).ap()
        bufs["ag_in16"] = nc.dram_tensor("ag_in16", [128, 16], F32).ap()
        bufs["ag_out16"] = nc.dram_tensor("ag_out16", [NCORES * 128, 16], F32).ap()

        def sb(name, shape, dt):
            t = E(nc.sbuf_tensor("sb_" + name, shape, dt))
            bufs[name] = t
            return t

        sb("identb", [128, 128], BF16)
        sb("ones_row", [1, GT], F32)
        sb("ones_colb", [128, 1], BF16)
        sb("bom_row", [1, D], F32)
        sb("b2_row", [1, D], F32)
        sb("const_bf", [1, D2], BF16)
        sb("colsum_bf", [1, D2], BF16)
        sb("const_f", [1, D2], F32)
        sb("gamma_c", [128, FC], F32)
        sb("beta_c", [128, FC], F32)
        sb("exp_ls", [128, DC], F32)
        sb("cmaskt", [128, NCORES], F32)
        for nm in ("s1c", "s2c", "mxc", "mnc", "mu_c", "r_c", "a_c", "bq_c",
                   "mu0_c", "r0_c", "b0_c", "r1_c", "t0_c", "t1_c", "t2_c"):
            sb(nm, [128, NT], F32)
        sb("colv", [128, 8], F32)
        sb("rowf", [16, 256], F32)
        sb("bcp", [128, 2], F32)
        sb("colk", [16, 8], F32)
        sb("bc16", [128, 16], F32)
        sb("sc16", [128, 16], F32)
        sb("junkb", [128, 1024], BF16)
        sb("carry", [128, DC], F32)
        sb("carry_r", [128, DC], F32)
        sb("carry_i", [128, DC], F32)
        sb("off_phi", [128, DC], F32)
        sb("off_r", [128, DC], F32)
        sb("off_i", [128, DC], F32)
        sb("gat", [128, NCORES, 16], F32)
        sb("xstats", [128, 2, D], F32)
        bufs["ps"] = [E(nc.psum_tensor(f"ps{i}", [128, GT], F32)) for i in range(6)]
        bufs["psS"] = bufs["ps"][5]
        bufs["psT2"] = [E(nc.psum_tensor("psT_a", [128, 1024], BF16)),
                        E(nc.psum_tensor("psT_b", [128, 1024], BF16))]

        emit_prologue(C, io)
        x_tiles_in = io["x_in"].rearrange("(n p) d -> n p d", p=128)
        out_tiles = io["out"].rearrange("(n p) d -> n p d", p=128)
        if STAGES == 0:
            pass
        elif STAGES == 1:
            emit_ffn(C, 0, x_tiles_in, out_tiles, io)
        elif STAGES == 2:
            emit_ffn(C, 0, x_tiles_in, io["xres"], io)
            emit_attn(C, io["xres"], out_tiles, io)
        else:
            emit_ffn(C, 0, x_tiles_in, io["xres"], io)
            emit_attn(C, io["xres"], io["xres"], io)
            emit_ffn(C, 1, io["xres"], out_tiles, io)

        snap = C.snap()

        def _(g):
            for s in SEMS:
                if snap[s]:
                    g.wait_ge(C.sem[s], snap[s])
        block.gpsimd(_)
    return nc



_CACHE = {}


def kernel(**inputs):
    x = np.ascontiguousarray(np.asarray(inputs["L0_attn_x"], np.float32)).reshape(B * S, D)
    womT = np.ascontiguousarray(np.asarray(inputs["L0_attn_W_omega"], np.float32).T)
    w1T = np.ascontiguousarray(np.asarray(inputs["L0_attn_W1"], np.float32).T)
    w2T = np.ascontiguousarray(np.asarray(inputs["L0_attn_W2"], np.float32).T)
    bom = np.ascontiguousarray(np.asarray(inputs["L0_attn_b_omega"], np.float32).reshape(1, D))
    b1 = np.ascontiguousarray(np.asarray(inputs["L0_attn_b1"], np.float32).reshape(1, D2))
    b2 = np.ascontiguousarray(np.asarray(inputs["L0_attn_b2"], np.float32).reshape(1, D))
    gamma = np.ascontiguousarray(np.asarray(inputs["L0_attn_ln_gamma"], np.float32).reshape(FC, 128).T)
    beta = np.ascontiguousarray(np.asarray(inputs["L0_attn_ln_beta"], np.float32).reshape(FC, 128).T)
    lsc = np.ascontiguousarray(np.asarray(inputs["L0_attn_log_scale"], np.float32).reshape(DC, 128).T)
    wup = np.ascontiguousarray(np.asarray(inputs["L0_ffn_w_up"], np.float32))
    wgate = np.ascontiguousarray(np.asarray(inputs["L0_ffn_w_gate"], np.float32))
    wdown = np.ascontiguousarray(np.asarray(inputs["L0_ffn_w_down"], np.float32))
    identb = np.eye(128, dtype=NPBF)
    pos = np.arange(1, S + 1, dtype=np.float64)
    invsq_full = (1.0 / np.sqrt(pos)).astype(np.float32)
    invp_full = (1.0 / pos).astype(np.float32)

    if "nc" not in _CACHE:
        _CACHE["nc"] = build_nc()
    nc = _CACHE["nc"]

    in_maps = []
    for c in range(NCORES):
        sl = slice(c * TPC, (c + 1) * TPC)
        so = (c % 4) * TPC
        cmask = np.zeros((1, NCORES), np.float32)
        for cp in range((c // 4) * 4, c):
            cmask[0, cp] = 1.0
        in_maps.append({
            "x": np.ascontiguousarray(x[sl]),
            "womT": womT, "w1T": w1T, "w2T": w2T,
            "bom": bom, "b1": b1, "b2": b2,
            "gammac": gamma, "betac": beta, "lsc": lsc,
            "wup": wup, "wgate": wgate, "wdown": wdown,
            "identb": identb,
            "invsq": np.ascontiguousarray(invsq_full[so:so + TPC].reshape(1, TPC)),
            "invp": np.ascontiguousarray(invp_full[so:so + TPC].reshape(1, TPC)),
            "cmask": cmask,
        })

    res = run_bass_kernel_spmd(nc, in_maps, core_ids=list(range(NCORES)))
    outp = np.concatenate([np.asarray(res.results[c]["out"], np.float32)
                           for c in range(NCORES)], axis=0)
    return outp.reshape(B, S, D)


if __name__ == "__main__":
    nc = build_nc()
    n = sum(1 for _ in nc.all_instructions())
    print("built, instructions:", n)


# version 07
# speedup vs baseline: 1.0871x; 1.0871x over previous
import contextlib
import math

import numpy as np
import ml_dtypes

import concourse.bass as bass
import concourse.mybir as mybir
from concourse.bass_utils import run_bass_kernel_spmd

F32 = mybir.dt.float32
BF16 = mybir.dt.bfloat16
I16 = mybir.dt.int16
U16 = mybir.dt.uint16
I32 = mybir.dt.int32
AF = mybir.ActivationFunctionType
ALU = mybir.AluOpType
AX = mybir.AxisListType.X
NPBF = ml_dtypes.bfloat16

NCORES = 8
B, S, D = 2, 16384, 1024
TPC = 4096
NT = 32
NG = 8
GT = 512
DC = 8
I3, IC = 3072, 24
D4, FC = 4096, 32
D2, EC2 = 2048, 16
LN_EPS = 1e-5
TWO_PI = 2.0 * math.pi
RC1 = float(np.float32(TWO_PI))
RC2 = float(np.float32(TWO_PI - RC1))
QMAX = 32767.0
PIH = float(np.pi / 2)
PIF = float(np.pi)

SEMS = ("le", "lo", "l2e", "l2o", "wae", "wao", "lx", "pe", "po", "px", "tc", "vc", "sc", "cc", "c1", "c2", "c3")


class Ctx:
    def __init__(self, nc, block, es):
        self.nc = nc
        self.block = block
        self.es = es
        self.n = {}
        self.sem = {}
        for name in SEMS:
            self.sem[name] = es.enter_context(nc.semaphore(name))
            self.n[name] = 0
        self.bufs = {}

    def inc(self, instr, name, k=1):
        instr.then_inc(self.sem[name], k)
        self.n[name] += k
        return self.n[name]

    def dinc(self, instr, name):
        return self.inc(instr, name, 16)

    def w(self, eng, name, val):
        if val > 0:
            eng.wait_ge(self.sem[name], val)

    def snap(self):
        return dict(self.n)

    def stream(self, kind):
        return DStream(self, kind)

    def fence(self, eng, name):
        if self.n[name] > 0:
            eng.wait_ge(self.sem[name], self.n[name])


class DStream:

    def __init__(self, C, kind):
        self.C = C
        self.names = {"l": ("le", "lo"), "l2": ("l2e", "l2o"), "w": ("wae", "wao"),
                      "p": ("pe", "po")}[kind]
        self.base = [C.n[self.names[0]], C.n[self.names[1]]]
        self.cnt = [0, 0]

    def issue(self, instr, j):
        nm = self.names[j % 2]
        self.C.inc(instr, nm, 16)
        self.cnt[j % 2] += 16

    def wait(self, eng, j, ndma=1):
        nm = self.names[j % 2]
        eng.wait_ge(self.C.sem[nm], self.base[j % 2] + 16 * ndma * (j // 2 + 1))

    def wait_all(self, eng):
        for i in (0, 1):
            if self.cnt[i]:
                eng.wait_ge(self.C.sem[self.names[i]], self.base[i] + self.cnt[i])


def full_barrier(C):
    snap = C.snap()

    def _(eng):
        for s in SEMS:
            if snap[s]:
                eng.wait_ge(C.sem[s], snap[s])
    for sec in (C.block.sync, C.block.gpsimd, C.block.tensor, C.block.vector, C.block.scalar):
        sec(_)



def emit_allreduce_max(C, cols, nvals):
    nc, b = C.nc, C.block
    bufs = C.bufs
    rowf, colk, bc16, bcp = bufs["rowf"], bufs["colk"], bufs["bc16"], bufs["bcp"]
    cc_in, cc_out, rowb_h = bufs["cc_in"], bufs["cc_out"], bufs["rowb_h"]
    v0 = C.n["vc"]

    def _(g):
        C.w(g, "vc", v0)
        C.fence(g, "px")
        C.dinc(g.dma_start(out=cc_in, in_=cols[:, 0:2]), "px")
        C.fence(g, "px")
        C.inc(g.collective_compute(
            "AllReduce", ALU.max, replica_groups=[list(range(NCORES))],
            ins=[cc_in.opt()], outs=[cc_out.opt()]), "cc")
        C.w(g, "cc", C.n["cc"])
        C.dinc(g.dma_start(out=bcp[:], in_=cc_out), "px")
        C.fence(g, "px")
        C.dinc(g.dma_start(out=rowf[0:1, 0:256], in_=bcp[:, 0:2]), "px")
    b.gpsimd(_)
    sp1 = C.n["px"]

    def _(v):
        C.w(v, "px", sp1)
        C.inc(v.tensor_reduce(out=colk[0:1, 0:2],
                              in_=rowf[0:1, 0:256].rearrange("r (p c) -> r c p", c=2),
                              axis=AX, op=ALU.max), "vc")
    b.vector(_)
    v1 = C.n["vc"]

    def _(g):
        C.w(g, "vc", v1)
        C.dinc(g.dma_start(out=rowb_h[0:1, 0:8], in_=colk[0:1, :]), "px")
        C.fence(g, "px")
        for j in range(nvals):
            C.dinc(g.dma_start(out=bc16[:, j:j + 1],
                               in_=rowb_h[0:1, j:j + 1].broadcast_to((128, 1))), "px")
    b.gpsimd(_)


def emit_carry_gather(C, carry_list, off_list):
    nc, b = C.nc, C.block
    bufs = C.bufs
    gat, cmaskt = bufs["gat"], bufs["cmaskt"]
    k = len(carry_list)
    v0 = C.n["vc"]
    w = k * DC
    ag_in = bufs["ag_in8"] if k == 1 else bufs["ag_in16"]
    ag_out = bufs["ag_out8"] if k == 1 else bufs["ag_out16"]

    def _(g):
        C.w(g, "vc", v0)
        C.fence(g, "px")
        for j, car in enumerate(carry_list):
            C.dinc(g.dma_start(out=ag_in[:, j * DC:(j + 1) * DC], in_=car[:]), "px")
        C.fence(g, "px")
        C.inc(g.collective_compute(
            "AllGather", ALU.bypass, replica_groups=[list(range(NCORES))],
            ins=[ag_in[:, 0:w].opt()], outs=[ag_out[:, 0:w].opt()]), "cc")
        C.w(g, "cc", C.n["cc"])
        C.dinc(g.dma_start(out=gat[:, :, 0:w],
                           in_=ag_out[:, 0:w].rearrange("(c p) n -> p c n", p=128)), "px")
    b.gpsimd(_)
    sp1 = C.n["px"]

    def _(v):
        C.w(v, "px", sp1)
        C.inc(v.tensor_tensor(out=gat[:, :, 0:w], in0=gat[:, :, 0:w],
                              in1=cmaskt[:, 0:NCORES].rearrange("p (c o) -> p c o", o=1)
                              .broadcast_to((128, NCORES, w)),
                              op=ALU.mult), "vc")
        C.w(v, "vc", C.n["vc"])
        for j, off in enumerate(off_list):
            C.inc(v.tensor_reduce(out=off[:],
                                  in_=gat[:, :, j * DC:(j + 1) * DC].rearrange("p c n -> p n c"),
                                  axis=AX, op=ALU.add), "vc")
    b.vector(_)
    full_barrier(C)


def emit_ln0_stats(C, xsrc, with_minmax, mu_dst, r_dst):
    nc, b = C.nc, C.block
    bufs = C.bufs
    s1c, s2c, mxc, mnc = bufs["s1c"], bufs["s2c"], bufs["mxc"], bufs["mnc"]
    junkb = bufs["junkb"]
    t0, t1, t2 = bufs["t0_c"], bufs["t1_c"], bufs["t2_c"]
    xa = bufs["xstats"]

    stx = C.stream("l")
    v0 = C.n["vc"]
    s0 = C.n["sc"]

    def _(sync):
        for i in range(NT):
            if i >= 2:
                C.w(sync, "vc", v0 + (i - 1))
                C.w(sync, "sc", s0 + (i - 1))
            stx.issue(sync.dma_start(out=xa[:, i % 2, :], in_=xsrc[i, :, :]), i)
    b.sync(_)

    def _(v):
        for i in range(NT):
            stx.wait(v, i)
            xt = xa[:, i % 2, :]
            v.tensor_scalar(out=junkb[:, 0:D], in0=xt, scalar1=1.0, scalar2=None,
                            op0=ALU.mult, op1=ALU.add, accum_out=s1c[:, i:i + 1])
            if with_minmax:
                v.tensor_scalar(out=junkb[:, 0:D], in0=xt, scalar1=1.0, scalar2=None,
                                op0=ALU.mult, op1=ALU.max, accum_out=mxc[:, i:i + 1])
                ins = v.tensor_scalar(out=junkb[:, 0:D], in0=xt, scalar1=-1.0,
                                      scalar2=None, op0=ALU.mult, op1=ALU.max,
                                      accum_out=mnc[:, i:i + 1])
            else:
                ins = v.tensor_copy(junkb[:, 0:1], xt[:, 0:1])
            C.inc(ins, "vc")
    b.vector(_)

    def _(s):
        for i in range(NT):
            stx.wait(s, i)
            C.inc(s.activation(out=junkb[:, 0:D], in_=xa[:, i % 2, :], func=AF.Square,
                               accum_out=s2c[:, i:i + 1]), "sc")
    b.scalar(_)

    v1, s1 = C.n["vc"], C.n["sc"]

    def _(v):
        C.w(v, "vc", v1)
        C.w(v, "sc", s1)
        v.tensor_scalar(out=mu_dst[:], in0=s1c[:], scalar1=1.0 / D, scalar2=None, op0=ALU.mult)
        C.inc(v.tensor_scalar(out=t0[:], in0=s2c[:], scalar1=1.0 / D, scalar2=None,
                              op0=ALU.mult), "vc")
        C.w(v, "vc", C.n["vc"])
        C.inc(v.tensor_mul(t1[:], mu_dst[:], mu_dst[:]), "vc")
        C.w(v, "vc", C.n["vc"])
        C.inc(v.tensor_sub(t0[:], t0[:], t1[:]), "vc")
        C.w(v, "vc", C.n["vc"])
        C.inc(v.tensor_scalar(out=t0[:], in0=t0[:], scalar1=LN_EPS, scalar2=None,
                              op0=ALU.add), "vc")
    b.vector(_)
    v2 = C.n["vc"]

    def _(s):
        C.w(s, "vc", v2)
        C.inc(s.sqrt(t1[:], t0[:]), "sc")
    b.scalar(_)
    s2 = C.n["sc"]

    def _(v):
        C.w(v, "sc", s2)
        C.inc(v.reciprocal(t2[:], t1[:]), "vc")
        C.w(v, "vc", C.n["vc"])
        C.inc(v.tensor_mul(t1[:], t2[:], t2[:]), "vc")
        C.w(v, "vc", C.n["vc"])
        C.inc(v.tensor_mul(t1[:], t1[:], t0[:]), "vc")
        C.w(v, "vc", C.n["vc"])
        C.inc(v.tensor_scalar(out=t1[:], in0=t1[:], scalar1=-0.5, scalar2=1.5,
                              op0=ALU.mult, op1=ALU.add), "vc")
        C.w(v, "vc", C.n["vc"])
        C.inc(v.tensor_mul(r_dst[:], t2[:], t1[:]), "vc")
    b.vector(_)



def emit_prologue(C, io):
    nc, b = C.nc, C.block
    bufs = C.bufs
    identb = bufs["identb"]
    ones_row, ones_colb = bufs["ones_row"], bufs["ones_colb"]
    bom_row, b2_row = bufs["bom_row"], bufs["b2_row"]
    const_bf, colsum_bf, const_f = bufs["const_bf"], bufs["colsum_bf"], bufs["const_f"]
    gamma_c, beta_c, exp_ls, cmaskt = bufs["gamma_c"], bufs["beta_c"], bufs["exp_ls"], bufs["cmaskt"]
    ps, psT2 = bufs["ps"], bufs["psT2"]

    with contextlib.ExitStack() as pes:
        PE = pes.enter_context
        wbuf = PE(nc.sbuf_tensor("p_wbuf", [128, 2, I3], F32))
        ls_raw = PE(nc.sbuf_tensor("p_lsraw", [128, DC], F32))
        b1_rowf = PE(nc.sbuf_tensor("p_b1row", [1, D2], F32))
        bmax = PE(nc.sbuf_tensor("p_bmax", [128, 2, 96], F32))
        f7 = PE(nc.sbuf_tensor("p_f7", [128, 2, 96], F32))
        smul = PE(nc.sbuf_tensor("p_smul", [128, 2, 96], F32))
        qi = PE(nc.sbuf_tensor("p_qi", [128, I3], I32))
        wq = PE(nc.sbuf_tensor("p_wq", [128, 2, I3], BF16))
        wtr = PE(nc.sbuf_tensor("p_wtr", [128, 2, 128], BF16))
        w1chunk = PE(nc.sbuf_tensor("p_w1c", [128, 2, D2], F32))
        w1bf = PE(nc.sbuf_tensor("p_w1bf", [128, 2, D2], BF16))

        def _(sync):
            for dst, src in ((identb[:], io["identb_in"]), (bom_row[:], io["bom_in"]),
                             (b1_rowf[:], io["b1_in"]), (b2_row[:], io["b2_in"]),
                             (gamma_c[:], io["gamma_in"]), (beta_c[:], io["beta_in"]),
                             (ls_raw[:], io["ls_in"])):
                C.dinc(sync.dma_start(out=dst, in_=src), "lx")
            C.dinc(sync.dma_start(out=cmaskt[:, 0:NCORES],
                                  in_=io["cmask_in"][0:1, :].broadcast_to((128, NCORES))), "lx")
            C.fence(sync, "lx")
        b.sync(_)
        ld_consts = C.n["lx"]

        def _(v):
            v.memset(ones_row[:], 1.0)
            v.memset(bufs["colk"][:], 0.0)
            v.memset(bufs["colv"][:], 0.0)
            C.inc(v.memset(ones_colb[:], 1.0), "vc")
        b.vector(_)



        def _(s):
            C.w(s, "lx", ld_consts)
            C.inc(s.activation(out=exp_ls[:], in_=ls_raw[:], func=AF.Exp), "sc")
        b.scalar(_)

        jobs = [(io["wup_raw"], io["wupT_h"], IC, D),
                (io["wgate_raw"], io["wgT_h"], IC, D),
                (io["wdown_raw"], io["wdT_h"], DC, I3)]
        for raw, dstT, nrc, ncols in jobs:
            nb = ncols // 32
            ncc = ncols // 128
            raw_t = raw.rearrange("(n p) c -> n p c", p=128)
            dst_t = dstT.rearrange("(n p) c -> n p c", p=128)
            stw = C.stream("l")
            stp = C.stream("p")
            v0 = C.n["vc"]
            t0n = C.n["tc"]
            s0 = C.n["sc"]

            def _(sync, raw_t=raw_t, nrc=nrc, ncols=ncols, v0=v0, stw=stw):
                for i in range(nrc):
                    if i >= 2:
                        C.w(sync, "vc", v0 + 5 * (i - 1))
                    stw.issue(sync.dma_start(out=wbuf[:, i % 2, 0:ncols], in_=raw_t[i, :, :]), i)
            b.sync(_)

            def _(v, nrc=nrc, ncols=ncols, nb=nb, t0n=t0n, ncc=ncc, stw=stw):
                for i in range(nrc):
                    sl = i % 2
                    stw.wait(v, i)
                    if i >= 2:
                        C.w(v, "tc", t0n + (i - 1) * ncc)
                    C.inc(v.tensor_reduce(out=bmax[:, sl, 0:nb],
                                          in_=wbuf[:, sl, 0:ncols]
                                          .rearrange("p (b k) -> p b k", k=32),
                                          axis=AX, op=ALU.max,
                                          apply_absolute_value=True), "vc")
                    C.w(v, "vc", C.n["vc"])
                    C.inc(v.tensor_scalar(out=bmax[:, sl, 0:nb], in0=bmax[:, sl, 0:nb],
                                           scalar1=7e-12, scalar2=None, op0=ALU.max), "vc")
                    C.w(v, "vc", C.n["vc"])
                    C.inc(v.reciprocal(f7[:, sl, 0:nb], bmax[:, sl, 0:nb]), "vc")
                    C.w(v, "vc", C.n["vc"])
                    v.tensor_scalar(out=f7[:, sl, 0:nb], in0=f7[:, sl, 0:nb],
                                    scalar1=7.0, scalar2=None, op0=ALU.mult)
                    C.inc(v.tensor_scalar(out=smul[:, sl, 0:nb], in0=bmax[:, sl, 0:nb],
                                          scalar1=1.0 / 7.0, scalar2=None,
                                          op0=ALU.mult), "vc")
                    C.w(v, "vc", C.n["vc"])
                    v.tensor_tensor(out=qi[:, 0:ncols].rearrange("p (b k) -> p b k", k=32),
                                    in0=wbuf[:, sl, 0:ncols].rearrange("p (b k) -> p b k", k=32),
                                    in1=f7[:, sl, 0:nb].rearrange("p (b o) -> p b o", o=1)
                                    .broadcast_to((128, nb, 32)),
                                    op=ALU.mult)
                    C.inc(v.tensor_tensor(
                        out=wq[:, sl, 0:ncols].rearrange("p (b k) -> p b k", k=32),
                        in0=qi[:, 0:ncols].rearrange("p (b k) -> p b k", k=32),
                        in1=smul[:, sl, 0:nb].rearrange("p (b o) -> p b o", o=1)
                        .broadcast_to((128, nb, 32)),
                        op=ALU.mult), "vc")
            b.vector(_)

            def _(t, nrc=nrc, ncc=ncc, v0=v0, s0=s0):
                for i in range(nrc):
                    C.w(t, "vc", v0 + 5 * (i + 1))
                    for cc_ in range(ncc):
                        k = i * ncc + cc_
                        if k >= 2:
                            C.w(t, "sc", s0 + k - 1)
                        C.inc(t.transpose(psT2[k % 2][:, 0:128],
                                          wq[:, i % 2, cc_ * 128:(cc_ + 1) * 128],
                                          identb[:]), "tc")
            b.tensor(_)

            def _(s, nrc=nrc, ncc=ncc, t0n=t0n, stp=stp):
                for k in range(nrc * ncc):
                    C.w(s, "tc", t0n + k + 1)
                    if k >= 2:
                        stp.wait(s, k - 2)
                    C.inc(s.activation(out=wtr[:, k % 2, :], in_=psT2[k % 2][:, 0:128],
                                       func=AF.Identity), "sc")
            b.scalar(_)

            def _(g, nrc=nrc, ncc=ncc, s0=s0, dst_t=dst_t, stp=stp):
                for k in range(nrc * ncc):
                    i, cc_ = divmod(k, ncc)
                    C.w(g, "sc", s0 + k + 1)
                    stp.issue(g.dma_start(out=dst_t[cc_, :, i * 128:(i + 1) * 128],
                                          in_=wtr[:, k % 2, :]), k)
            b.gpsimd(_)
            full_barrier(C)

        for src_in, dst_h, nch, ncols in ((io["womT_in"], io["womT_h"], DC, D),
                                          (io["w2T_in"], io["w2T_h"], EC2, D)):
            src_t = src_in.rearrange("(n p) c -> n p c", p=128)
            dst_t = dst_h.rearrange("(n p) c -> n p c", p=128)
            stw = C.stream("l")
            stp = C.stream("p")
            v0 = C.n["vc"]

            def _(sync, src_t=src_t, nch=nch, ncols=ncols, v0=v0, stw=stw):
                for i in range(nch):
                    if i >= 2:
                        C.w(sync, "vc", v0 + (i - 1))
                    stw.issue(sync.dma_start(out=wbuf[:, i % 2, 0:ncols], in_=src_t[i, :, :]), i)
            b.sync(_)

            t_start = C.n["tc"]

            def _(v, nch=nch, ncols=ncols, t_start=t_start, stw=stw, stp=stp):
                for i in range(nch):
                    stw.wait(v, i)
                    if i < 2:
                        C.w(v, "tc", t_start)
                    if i >= 2:
                        stp.wait(v, i - 2)
                    C.inc(v.tensor_copy(wq[:, i % 2, 0:ncols], wbuf[:, i % 2, 0:ncols]), "vc")
            b.vector(_)

            def _(g, nch=nch, ncols=ncols, v0=v0, dst_t=dst_t, stp=stp):
                for i in range(nch):
                    C.w(g, "vc", v0 + i + 1)
                    stp.issue(g.dma_start(out=dst_t[i, :, :], in_=wq[:, i % 2, 0:ncols]), i)
            b.gpsimd(_)
            full_barrier(C)

        w1T_t = io["w1T_in"].rearrange("(n p) c -> n p c", p=128)
        w1g_t = io["w1g_h"].rearrange("(n p) c -> n p c", p=128)
        stw1 = C.stream("l")
        stp1 = C.stream("p")
        v0 = C.n["vc"]
        t0n = C.n["tc"]

        def _(sync, v0=v0, t0n=t0n, stw1=stw1):
            for i in range(FC):
                if i >= 2:
                    C.w(sync, "vc", v0 + (i - 1))
                    C.w(sync, "tc", t0n + 4 * (i - 1))
                stw1.issue(sync.dma_start(out=w1chunk[:, i % 2, :], in_=w1T_t[i, :, :]), i)
        b.sync(_)

        def _(v, stw1=stw1, stp1=stp1):
            for i in range(FC):
                stw1.wait(v, i)
                if i >= 2:
                    stp1.wait(v, i - 2)
                C.inc(v.tensor_scalar(out=w1bf[:, i % 2, :], in0=w1chunk[:, i % 2, :],
                                      scalar1=gamma_c[:, i:i + 1], scalar2=None,
                                      op0=ALU.mult), "vc")
        b.vector(_)

        def _(g, v0=v0, stp1=stp1):
            for i in range(FC):
                C.w(g, "vc", v0 + i + 1)
                stp1.issue(g.dma_start(out=w1g_t[i, :, :], in_=w1bf[:, i % 2, :]), i)
        b.gpsimd(_)

        def _(t, stw1=stw1):
            for i in range(FC):
                stw1.wait(t, i)
                for j in range(4):
                    C.inc(t.matmul(ps[j][0:1, :], beta_c[:, i:i + 1],
                                   w1chunk[:, i % 2, j * 512:(j + 1) * 512],
                                   start=(i == 0), stop=(i == FC - 1)), "tc")
        b.tensor(_)
        t_beta = C.n["tc"]

        def _(v, t_beta=t_beta, stw1=stw1):
            C.w(v, "tc", t_beta)
            stw1.wait_all(v)
            for j in range(4):
                v.tensor_add(const_f[0:1, j * 512:(j + 1) * 512], ps[j][0:1, :],
                             b1_rowf[0:1, j * 512:(j + 1) * 512])
            C.inc(v.tensor_copy(const_bf[0:1, :], const_f[0:1, :]), "vc")
        b.vector(_)
        full_barrier(C)

        stc = C.stream("l")
        v0 = C.n["vc"]
        t0n = C.n["tc"]

        def _(sync, t0n=t0n, stc=stc, stp1=stp1):
            stp1.wait_all(sync)
            C.w(sync, "vc", C.n["vc"])
            for i in range(FC):
                if i >= 2:
                    C.w(sync, "tc", t0n + 4 * (i - 1))
                stc.issue(sync.dma_start(out=w1bf[:, i % 2, :], in_=w1g_t[i, :, :]), i)
        b.sync(_)

        def _(t, stc=stc):
            for i in range(FC):
                stc.wait(t, i)
                for j in range(4):
                    C.inc(t.matmul(ps[j][0:1, :], ones_colb[:, 0:1],
                                   w1bf[:, i % 2, j * 512:(j + 1) * 512],
                                   start=(i == 0), stop=(i == FC - 1)), "tc")
        b.tensor(_)

        def _(v):
            C.w(v, "tc", C.n["tc"])
            for j in range(4):
                C.inc(v.tensor_copy(colsum_bf[0:1, j * 512:(j + 1) * 512], ps[j][0:1, :]), "vc")
        b.vector(_)

        full_barrier(C)



def emit_ffn(C, call, xsrc, xdst, io):
    nc, b = C.nc, C.block
    bufs = C.bufs
    ps, psT2 = bufs["ps"], bufs["psT2"]
    identb = bufs["identb"]
    bc16, sc16 = bufs["bc16"], bufs["sc16"]
    mu_c, r_c, a_c, bq_c = bufs["mu_c"], bufs["r_c"], bufs["a_c"], bufs["bq_c"]
    mxc, mnc = bufs["mxc"], bufs["mnc"]
    t0, t1 = bufs["t0_c"], bufs["t1_c"]
    colv = bufs["colv"]
    sp_up, sp_gate, sp_gi, sp_sig, sp_p = (io["sp_up"], io["sp_gate"], io["sp_gi"],
                                           io["sp_sig"], io["sp_p"])

    emit_ln0_stats(C, xsrc, True, mu_c, r_c)

    def _(v):
        C.w(v, "vc", C.n["vc"])
        v.tensor_sub(t0[:], mxc[:], mu_c[:])
        C.inc(v.tensor_add(t1[:], mnc[:], mu_c[:]), "vc")
        C.w(v, "vc", C.n["vc"])
        C.inc(v.tensor_max(t0[:], t0[:], t1[:]), "vc")
        C.w(v, "vc", C.n["vc"])
        C.inc(v.tensor_mul(t0[:], t0[:], r_c[:]), "vc")
        C.w(v, "vc", C.n["vc"])
        C.inc(v.tensor_reduce(out=colv[:, 0:1], in_=t0[:], axis=AX, op=ALU.max), "vc")
    b.vector(_)

    emit_allreduce_max(C, colv, 1)
    sp_bc = C.n["px"]

    def _(v):
        C.w(v, "px", sp_bc)
        C.inc(v.tensor_scalar(out=sc16[:, 0:1], in0=bc16[:, 0:1], scalar1=1.0 / QMAX,
                              scalar2=1e-12, op0=ALU.mult, op1=ALU.max), "vc")
        C.w(v, "vc", C.n["vc"])
        C.inc(v.reciprocal(sc16[:, 1:2], sc16[:, 0:1]), "vc")
        C.w(v, "vc", C.n["vc"])
        C.inc(v.tensor_scalar(out=a_c[:], in0=r_c[:], scalar1=sc16[:, 1:2], scalar2=None,
                              op0=ALU.mult), "vc")
        C.w(v, "vc", C.n["vc"])
        C.inc(v.tensor_mul(bq_c[:], mu_c[:], a_c[:]), "vc")
        C.w(v, "vc", C.n["vc"])
        C.inc(v.tensor_scalar(out=bq_c[:], in0=bq_c[:], scalar1=-1.0, scalar2=None,
                              op0=ALU.mult), "vc")
    b.vector(_)
    full_barrier(C)

    with contextlib.ExitStack() as pes:
        PE = pes.enter_context
        xa = bufs["xstats"]
        qtile = PE(nc.sbuf_tensor(f"f{call}_q", [128, 2, D], I16))
        xqtm = PE(nc.sbuf_tensor(f"f{call}_xqtm", [128, 2, D], BF16))
        xqT = PE(nc.sbuf_tensor(f"f{call}_xqT", [128, DC, TPC], BF16))
        wst = PE(nc.sbuf_tensor(f"f{call}_wst", [128, 2, DC, 128], BF16))
        ev = PE(nc.sbuf_tensor(f"f{call}_ev", [128, 8, GT], BF16))
        amx = PE(nc.sbuf_tensor(f"f{call}_amx", [128, 2, 2 * IC * NG], F32))
        junk = PE(nc.sbuf_tensor(f"f{call}_junk", [128, GT], BF16))

        stx = C.stream("l")
        s0 = C.n["sc"]
        v0 = C.n["vc"]
        t0n = C.n["tc"]

        def _(sync, v0=v0, stx=stx):
            for i in range(NT):
                if i >= 2:
                    C.w(sync, "vc", v0 + (i - 1))
                stx.issue(sync.dma_start(out=xa[:, i % 2, :], in_=xsrc[i, :, :]), i)
        b.sync(_)

        def _(s, v0=v0, t0n=t0n, stx=stx):
            for i in range(NT):
                stx.wait(s, i)
                C.inc(s.activation(out=qtile[:, i % 2, :], in_=xa[:, i % 2, :],
                                   func=AF.Identity,
                                   scale=a_c[:, i:i + 1], bias=bq_c[:, i:i + 1]), "sc")
                for dc in range(DC):
                    k = i * DC + dc
                    C.w(s, "tc", t0n + k + 1)
                    C.inc(s.activation(out=xqT[:, dc, i * 128:(i + 1) * 128],
                                       in_=psT2[k % 2][:, 0:128], func=AF.Identity), "sc")
        b.scalar(_)

        def _(v, s0=s0):
            for i in range(NT):
                C.w(v, "sc", s0 + i * (DC + 1) + 1)
                C.inc(v.tensor_scalar(out=xqtm[:, i % 2, :], in0=qtile[:, i % 2, :],
                                      scalar1=sc16[:, 0:1], scalar2=None, op0=ALU.mult), "vc")
        b.vector(_)

        def _(t, v0=v0, s0=s0):
            for i in range(NT):
                C.w(t, "vc", v0 + i + 1)
                for dc in range(DC):
                    k = i * DC + dc
                    if k >= 2:
                        ii, dd = divmod(k - 2, DC)
                        C.w(t, "sc", s0 + ii * (DC + 1) + 1 + dd + 1)
                    C.inc(t.transpose(psT2[k % 2][:, 0:128],
                                      xqtm[:, i % 2, dc * 128:(dc + 1) * 128], identb[:]), "tc")
        b.tensor(_)

        full_barrier(C)

        if SUBSTAGE == 2:
            def _(g2):
                for s in ("tc", "vc", "sc"):
                    C.w(g2, s, C.n[s])
                for j, buf in enumerate((bufs["s1c"], bufs["s2c"], mxc, mnc, mu_c, r_c, a_c, bq_c)):
                    C.dinc(g2.dma_start(out=xdst[0, :, j * 32:(j + 1) * 32], in_=buf[:]), "px")
                C.dinc(g2.dma_start(out=xdst[0, :, 256:264], in_=colv[:, 0:8]), "px")
                C.dinc(g2.dma_start(out=xdst[0, :, 264:280], in_=bc16[:, :]), "px")
                C.dinc(g2.dma_start(out=xdst[0, :, 280:296], in_=sc16[:, :]), "px")
            b.gpsimd(_)
            full_barrier(C)
            return

        if SUBSTAGE in (3, 4, 5):
            wh_d = {3: io["wupT_h"], 4: io["wgT_h"], 5: io["wdT_h"]}[SUBSTAGE]
            nrows = wh_d.shape[0]
            wcols = wh_d.shape[1]
            nch = nrows // 128
            ntc = wcols // D
            wtmp_b = PE(nc.sbuf_tensor("dbg_wb", [128, wcols], BF16))
            wtmp_f = PE(nc.sbuf_tensor("dbg_wf", [128, wcols], F32))
            wh_t = wh_d.rearrange("(n p) c -> n p c", p=128)
            lxw = C.n["lx"]
            vw = C.n["vc"]
            pxw = C.n["px"]

            def _(sync, lxw=lxw, pxw=pxw):
                for i in range(nch):
                    if i >= 1:
                        C.w(sync, "px", pxw + 16 * ntc * i)
                    C.dinc(sync.dma_start(out=wtmp_b[:], in_=wh_t[i, :, :]), "lx")
            b.sync(_)

            def _(v, lxw=lxw):
                for i in range(nch):
                    C.w(v, "lx", lxw + 16 * (i + 1))
                    C.inc(v.tensor_copy(wtmp_f[:], wtmp_b[:]), "vc")
            b.vector(_)

            def _(g2, vw=vw):
                for i in range(nch):
                    C.w(g2, "vc", vw + i + 1)
                    for j in range(ntc):
                        C.dinc(g2.dma_start(out=xdst[i * ntc + j, :, :],
                                            in_=wtmp_f[:, j * D:(j + 1) * D]), "px")
            b.gpsimd(_)
            full_barrier(C)
            return

        if SUBSTAGE == 1:
            dbg = PE(nc.sbuf_tensor(f"f{call}_dbg", [128, 2, D], F32))
            st_d = C.stream("l")
            v0d = C.n["vc"]

            def _(sync, st_d=st_d, v0d=v0d):
                for i in range(NT):
                    if i >= 2:
                        C.w(sync, "vc", v0d + (i - 1))
                    st_d.issue(sync.dma_start(out=xa[:, i % 2, :], in_=xsrc[i, :, :]), i)
            b.sync(_)

            def _(s, st_d=st_d):
                for i in range(NT):
                    st_d.wait(s, i)
                    C.inc(s.activation(out=qtile[:, i % 2, :], in_=xa[:, i % 2, :],
                                       func=AF.Identity,
                                       scale=a_c[:, i:i + 1], bias=bq_c[:, i:i + 1]), "sc")
            b.scalar(_)
            s0d = C.n["sc"] - NT

            def _(v, s0d=s0d, v0d=v0d):
                for i in range(NT):
                    C.w(v, "sc", s0d + i + 1)
                    C.inc(v.tensor_scalar(out=dbg[:, i % 2, :], in0=qtile[:, i % 2, :],
                                          scalar1=sc16[:, 0:1], scalar2=None,
                                          op0=ALU.mult), "vc")
            b.vector(_)
            v1d = C.n["vc"] - NT

            def _(g2, v1d=v1d):
                for i in range(NT):
                    C.w(g2, "vc", v1d + i + 1)
                    C.dinc(g2.dma_start(out=xdst[i, :, :], in_=dbg[:, i % 2, :]), "px")
            b.gpsimd(_)
            full_barrier(C)
            return

        w_hbm = (io["wupT_h"], io["wgT_h"])
        w_sp = (sp_up, sp_gate)
        for quarter in range(NG // 2):
            groups = [2 * quarter, 2 * quarter + 1]
            for wi in range(2):
                wh = w_hbm[wi].rearrange("(c p) e -> p c e", p=128)
                stwm = C.stream("l2")
                stsp = C.stream("p")
                t0n = C.n["tc"]
                v0 = C.n["vc"]
                c10 = C.n["c1"]

                def _(sync, wh=wh, t0n=t0n, stwm=stwm):
                    for e in range(IC):
                        if e < 2:
                            C.w(sync, "tc", t0n)
                        else:
                            C.w(sync, "tc", t0n + (e - 1))
                        stwm.issue(sync.dma_start(out=wst[:, e % 2, :, :],
                                                  in_=wh[:, :, e * 128:(e + 1) * 128]), e)
                b.sync(_)

                def _(t, v0=v0, groups=groups, stwm=stwm):
                    for e in range(IC):
                        stwm.wait(t, e)
                        if e >= 2:
                            C.w(t, "vc", v0 + (e - 1) * 2)
                        for d in range(DC):
                            for gi in range(2):
                                ins = t.matmul(ps[2 * (e % 2) + gi][:],
                                               wst[:, e % 2, d, :],
                                               xqT[:, d, groups[gi] * GT:(groups[gi] + 1) * GT],
                                               start=(d == 0), stop=(d == DC - 1))
                        C.inc(ins, "tc")
                b.tensor(_)

                def _(v, t0n=t0n, wi=wi, groups=groups, stsp=stsp):
                    for e in range(IC):
                        C.w(v, "tc", t0n + e + 1)
                        if e >= 2:
                            stsp.wait(v, e - 2, ndma=2)
                        for gi in range(2):
                            g = groups[gi]
                            col = e * NG + g
                            ncol = IC * NG + col
                            v.tensor_copy(ev[:, 2 * (e % 2) + gi, :], ps[2 * (e % 2) + gi][:])
                            v.tensor_scalar(out=junk[:], in0=ev[:, 2 * (e % 2) + gi, :],
                                            scalar1=1.0, scalar2=None, op0=ALU.mult,
                                            op1=ALU.max, accum_out=amx[:, wi, col:col + 1])
                            C.inc(v.tensor_scalar(out=junk[:], in0=ev[:, 2 * (e % 2) + gi, :],
                                                  scalar1=-1.0, scalar2=None, op0=ALU.mult,
                                                  op1=ALU.max,
                                                  accum_out=amx[:, wi, ncol:ncol + 1]), "vc")
                b.vector(_)

                def _(g2, v0=v0, wi=wi, groups=groups, stsp=stsp):
                    for e in range(IC):
                        for gi in range(2):
                            C.w(g2, "vc", v0 + e * 2 + gi + 1)
                            stsp.issue(g2.dma_start(out=w_sp[wi][e, groups[gi], :, :],
                                                    in_=ev[:, 2 * (e % 2) + gi, :]), e)
                b.gpsimd(_)

        g_pe, g_po = C.n["pe"], C.n["po"]

        def _(v, g_pe=g_pe, g_po=g_po):
            C.w(v, "pe", g_pe)
            C.w(v, "po", g_po)
            C.w(v, "vc", C.n["vc"])
            C.inc(v.tensor_reduce(out=colv[:, 0:1], in_=amx[:, 0, :], axis=AX, op=ALU.max), "vc")
            C.inc(v.tensor_reduce(out=colv[:, 1:2], in_=amx[:, 1, :], axis=AX, op=ALU.max), "vc")
        b.vector(_)

        emit_allreduce_max(C, colv, 2)
        sp_bc = C.n["px"]

        def _(v, sp_bc=sp_bc):
            C.w(v, "px", sp_bc)
            C.inc(v.tensor_scalar(out=sc16[:, 2:3], in0=bc16[:, 0:1], scalar1=1.0 / QMAX,
                                  scalar2=1e-12, op0=ALU.mult, op1=ALU.max), "vc")
            C.w(v, "vc", C.n["vc"])
            v.reciprocal(sc16[:, 3:4], sc16[:, 2:3])
            C.inc(v.tensor_scalar(out=sc16[:, 4:5], in0=bc16[:, 1:2], scalar1=1.0 / QMAX,
                                  scalar2=1e-12, op0=ALU.mult, op1=ALU.max), "vc")
            C.w(v, "vc", C.n["vc"])
            C.inc(v.reciprocal(sc16[:, 5:6], sc16[:, 4:5]), "vc")
        b.vector(_)
        full_barrier(C)

        if SUBSTAGE == 10:
            dxf = PE(nc.sbuf_tensor("dbg_xf", [128, 2, 1024], F32))
            vd = C.n["vc"]
            pxd = C.n["px"]

            def _(v, pxd=pxd):
                for k in range(32):
                    d, q = divmod(k, 4)
                    if k >= 2:
                        C.w(v, "px", pxd + 16 * (k - 1))
                    C.inc(v.tensor_copy(dxf[:, k % 2, :],
                                        xqT[:, d, q * 1024:(q + 1) * 1024]), "vc")
            b.vector(_)

            def _(g2, vd=vd):
                for k in range(32):
                    C.w(g2, "vc", vd + k + 1)
                    C.dinc(g2.dma_start(out=xdst[k, :, :], in_=dxf[:, k % 2, :]), "px")
            b.gpsimd(_)
            full_barrier(C)
            return

        if SUBSTAGE in (7, 8, 9):
            e_base = {7: 0, 8: 8, 9: 16}[SUBSTAGE]
            dzb = PE(nc.sbuf_tensor("dbg_zb", [128, 2, GT], BF16))
            dzf = PE(nc.sbuf_tensor("dbg_zf", [128, 2, GT], F32))
            lxd = C.n["lx"]
            vd = C.n["vc"]
            pxd = C.n["px"]

            def _(sync, lxd=lxd, pxd=pxd, e_base=e_base):
                for k in range(64):
                    e, g = divmod(k, 8)
                    if k >= 2:
                        C.w(sync, "px", pxd + 16 * (k - 1))
                    C.dinc(sync.dma_start(out=dzb[:, k % 2, :],
                                          in_=sp_up[e_base + e, g, :, :]), "lx")
            b.sync(_)

            def _(v, lxd=lxd):
                for k in range(64):
                    C.w(v, "lx", lxd + 16 * (k + 1))
                    C.inc(v.tensor_copy(dzf[:, k % 2, :], dzb[:, k % 2, :]), "vc")
            b.vector(_)

            def _(g2, vd=vd):
                for k in range(64):
                    C.w(g2, "vc", vd + k + 1)
                    ti, half = divmod(k, 2)
                    C.dinc(g2.dma_start(out=xdst[ti, :, half * GT:(half + 1) * GT],
                                        in_=dzf[:, k % 2, :]), "px")
            b.gpsimd(_)
            full_barrier(C)
            return

        if SUBSTAGE == 6:
            def _(g2):
                for s in ("vc",):
                    C.w(g2, s, C.n[s])
                C.dinc(g2.dma_start(out=xdst[0, :, 0:384], in_=amx[:, 0, :]), "px")
                C.dinc(g2.dma_start(out=xdst[0, :, 384:768], in_=amx[:, 1, :]), "px")
                C.dinc(g2.dma_start(out=xdst[0, :, 768:776], in_=colv[:, 0:8]), "px")
                C.dinc(g2.dma_start(out=xdst[0, :, 776:792], in_=bc16[:, :]), "px")
                C.dinc(g2.dma_start(out=xdst[0, :, 792:808], in_=sc16[:, :]), "px")
            b.gpsimd(_)
            full_barrier(C)
            return

    ntile = IC * NG
    with contextlib.ExitStack() as pes:
        PE = pes.enter_context
        gtile = PE(nc.sbuf_tensor(f"f{call}_gt", [128, 2, GT], BF16))
        gi_t = PE(nc.sbuf_tensor(f"f{call}_gi", [128, 2, GT], I16))
        sig0 = PE(nc.sbuf_tensor(f"f{call}_sg0", [128, 2, GT], F32))
        sigu = PE(nc.sbuf_tensor(f"f{call}_sgu", [128, 2, GT], U16))
        tp = PE(nc.sbuf_tensor(f"f{call}_tp", [128, 2, GT], F32))
        tmaxc = PE(nc.sbuf_tensor(f"f{call}_tmx", [128, ntile], F32))
        uptile = PE(nc.sbuf_tensor(f"f{call}_up", [128, 2, GT], BF16))
        g2t = PE(nc.sbuf_tensor(f"f{call}_g2", [128, 2, GT], I16))
        uit = PE(nc.sbuf_tensor(f"f{call}_ui", [128, 2, GT], I16))
        ptile = PE(nc.sbuf_tensor(f"f{call}_p", [128, 2, GT], F32))

        stg = C.stream("l")
        stspc = C.stream("p")
        v0 = C.n["vc"]
        s0 = C.n["sc"]

        def _(sync, v0=v0, stg=stg):
            for k in range(ntile):
                e, g = divmod(k, NG)
                if k >= 2:
                    C.w(sync, "vc", v0 + (k - 1) * 2)
                stg.issue(sync.dma_start(out=gtile[:, k % 2, :], in_=sp_gate[e, g, :, :]), k)
        b.sync(_)

        def _(v, s0=s0, stg=stg, stspc=stspc):
            for k in range(ntile):
                sl = k % 2
                stg.wait(v, k)
                if k >= 2:
                    stspc.wait(v, k - 2, ndma=2)
                C.inc(v.tensor_scalar(out=gi_t[:, sl, :], in0=gtile[:, sl, :],
                                      scalar1=sc16[:, 5:6], scalar2=None, op0=ALU.mult), "vc")
                C.w(v, "sc", s0 + k + 1)
                v.tensor_scalar(out=sigu[:, sl, :], in0=sig0[:, sl, :], scalar1=65536.0,
                                scalar2=65535.0, op0=ALU.mult, op1=ALU.min)
                v.tensor_tensor(out=tp[:, sl, :], in0=gi_t[:, sl, :], in1=sigu[:, sl, :],
                                op=ALU.mult)
                C.inc(v.tensor_reduce(out=tmaxc[:, k:k + 1], in_=tp[:, sl, :], axis=AX,
                                      op=ALU.max, apply_absolute_value=True), "vc")
        b.vector(_)

        def _(s, v0=v0):
            for k in range(ntile):
                C.w(s, "vc", v0 + 2 * k + 1)
                C.inc(s.activation(out=sig0[:, k % 2, :], in_=gi_t[:, k % 2, :],
                                   func=AF.Sigmoid, scale=sc16[:, 4:5]), "sc")
        b.scalar(_)

        def _(g2, v0=v0, stspc=stspc):
            for k in range(ntile):
                e, g = divmod(k, NG)
                C.w(g2, "vc", v0 + 2 * (k + 1))
                stspc.issue(g2.dma_start(out=sp_gi[e, g, :, :], in_=gi_t[:, k % 2, :]), k)
                stspc.issue(g2.dma_start(out=sp_sig[e, g, :, :], in_=sigu[:, k % 2, :]), k)
        b.gpsimd(_)

        def _(v, stspc=stspc):
            stspc.wait_all(v)
            C.w(v, "vc", C.n["vc"])
            C.inc(v.tensor_reduce(out=colv[:, 0:1], in_=tmaxc[:], axis=AX, op=ALU.max), "vc")
        b.vector(_)
        emit_allreduce_max(C, colv, 1)
        sp_bc = C.n["px"]

        def _(v, sp_bc=sp_bc):
            C.w(v, "px", sp_bc)
            C.inc(v.tensor_scalar(out=sc16[:, 6:7], in0=sc16[:, 4:5],
                                  scalar1=float(2.0 ** -16),
                                  scalar2=None, op0=ALU.mult), "vc")
            C.w(v, "vc", C.n["vc"])
            C.inc(v.tensor_mul(sc16[:, 7:8], bc16[:, 0:1], sc16[:, 6:7]), "vc")
            C.w(v, "vc", C.n["vc"])
            C.inc(v.tensor_scalar(out=sc16[:, 7:8], in0=sc16[:, 7:8], scalar1=1.0 / QMAX,
                                  scalar2=1e-12, op0=ALU.mult, op1=ALU.max), "vc")
            C.w(v, "vc", C.n["vc"])
            C.inc(v.reciprocal(sc16[:, 8:9], sc16[:, 7:8]), "vc")
            C.w(v, "vc", C.n["vc"])
            C.inc(v.tensor_mul(sc16[:, 8:9], sc16[:, 8:9], sc16[:, 6:7]), "vc")
        b.vector(_)
        full_barrier(C)

        std_ = C.stream("l")
        stspd = C.stream("p")
        v0 = C.n["vc"]

        def _(sync, v0=v0, std_=std_):
            for k in range(ntile):
                e, g = divmod(k, NG)
                if k >= 2:
                    C.w(sync, "vc", v0 + (k - 1))
                std_.issue(sync.dma_start(out=gi_t[:, k % 2, :], in_=sp_gi[e, g, :, :]), k)
                std_.issue(sync.dma_start(out=sigu[:, k % 2, :], in_=sp_sig[e, g, :, :]), k)
                std_.issue(sync.dma_start(out=uptile[:, k % 2, :], in_=sp_up[e, g, :, :]), k)
        b.sync(_)

        def _(v, std_=std_, stspd=stspd):
            for k in range(ntile):
                sl = k % 2
                std_.wait(v, k, ndma=3)
                if k >= 2:
                    stspd.wait(v, k - 2)
                v.tensor_tensor(out=tp[:, sl, :], in0=gi_t[:, sl, :], in1=sigu[:, sl, :],
                                op=ALU.mult)
                v.tensor_scalar(out=g2t[:, sl, :], in0=tp[:, sl, :], scalar1=sc16[:, 8:9],
                                scalar2=None, op0=ALU.mult)
                v.tensor_scalar(out=uit[:, sl, :], in0=uptile[:, sl, :], scalar1=sc16[:, 3:4],
                                scalar2=None, op0=ALU.mult)
                v.tensor_tensor(out=ptile[:, sl, :], in0=g2t[:, sl, :], in1=uit[:, sl, :],
                                op=ALU.mult)
                C.inc(v.tensor_reduce(out=tmaxc[:, k:k + 1], in_=ptile[:, sl, :], axis=AX,
                                      op=ALU.max, apply_absolute_value=True), "vc")
        b.vector(_)

        def _(g2, v0=v0, stspd=stspd):
            for k in range(ntile):
                e, g = divmod(k, NG)
                C.w(g2, "vc", v0 + k + 1)
                stspd.issue(g2.dma_start(out=sp_p[e, g, :, :], in_=ptile[:, k % 2, :]), k)
        b.gpsimd(_)

        def _(v, stspd=stspd):
            stspd.wait_all(v)
            C.w(v, "vc", C.n["vc"])
            C.inc(v.tensor_reduce(out=colv[:, 0:1], in_=tmaxc[:], axis=AX, op=ALU.max), "vc")
        b.vector(_)
        emit_allreduce_max(C, colv, 1)
        sp_bc = C.n["px"]

        def _(v, sp_bc=sp_bc):
            C.w(v, "px", sp_bc)
            C.inc(v.tensor_mul(sc16[:, 9:10], sc16[:, 7:8], sc16[:, 2:3]), "vc")
            C.w(v, "vc", C.n["vc"])
            C.inc(v.tensor_mul(sc16[:, 10:11], sc16[:, 9:10], bc16[:, 0:1]), "vc")
            C.w(v, "vc", C.n["vc"])
            C.inc(v.tensor_scalar(out=sc16[:, 10:11], in0=sc16[:, 10:11], scalar1=1.0 / QMAX,
                                  scalar2=1e-12, op0=ALU.mult, op1=ALU.max), "vc")
            C.w(v, "vc", C.n["vc"])
            C.inc(v.reciprocal(sc16[:, 11:12], sc16[:, 10:11]), "vc")
            C.w(v, "vc", C.n["vc"])
            C.inc(v.tensor_mul(sc16[:, 11:12], sc16[:, 11:12], sc16[:, 9:10]), "vc")
        b.vector(_)
        full_barrier(C)

        with contextlib.ExitStack() as pes2:
            PE2 = pes2.enter_context
            eptile = PE2(nc.sbuf_tensor(f"f{call}_ep", [128, 2, GT], F32))
            ogrp = PE2(nc.sbuf_tensor(f"f{call}_og", [128, 2, IC, GT], BF16))
            wdsb = PE2(nc.sbuf_tensor(f"f{call}_wd", [128, IC, D], BF16))
            oi_t = PE2(nc.sbuf_tensor(f"f{call}_oi", [128, 2, GT], I16))
            xrt = PE2(nc.sbuf_tensor(f"f{call}_xr", [128, 4, D], F32))
            xnew = PE2(nc.sbuf_tensor(f"f{call}_xn", [128, 4, D], F32))
            wdh = io["wdT_h"].rearrange("(c p) e -> c p e", p=128)

            def _(sync):
                for e in range(IC):
                    C.dinc(sync.dma_start(out=wdsb[:, e, :], in_=wdh[e, :, :]), "lx")
                C.fence(sync, "lx")
            b.sync(_)
            ld_wd = C.n["lx"]

            stpe = C.stream("l")
            v0 = C.n["vc"]
            c1_0 = C.n["c1"]
            px0 = C.n["px"]
            lx0 = C.n["lx"]

            vdone = {}
            S = {}
            cnt = v0
            for gi in range(NG + 1):
                if gi < NG:
                    for e in range(IC):
                        cnt += 2
                        vdone[gi * IC + e] = cnt
                if gi >= 1:
                    S[gi - 1] = cnt
                    cnt += 4

            def _(sync):
                for g in range(NG):
                    for e in range(IC):
                        k = g * IC + e
                        if k >= 2:
                            C.w(sync, "vc", vdone[k - 2])
                        stpe.issue(sync.dma_start(out=eptile[:, k % 2, :],
                                                  in_=sp_p[e, g, :, :]), k)
                    if g >= 1:
                        C.w(sync, "vc", S[g - 1] + 4)
                    for tt in range(4):
                        C.dinc(sync.dma_start(out=xrt[:, tt, :], in_=xsrc[g * 4 + tt, :, :]),
                               "lx")
            b.sync(_)

            def _(v):
                for gi in range(NG + 1):
                    if gi < NG:
                        for e in range(IC):
                            k = gi * IC + e
                            stpe.wait(v, k)
                            if e == 0 and gi >= 2:
                                C.w(v, "c1", c1_0 + 2 * IC * (gi - 1))
                            C.inc(v.tensor_scalar(out=oi_t[:, k % 2, :],
                                                  in0=eptile[:, k % 2, :],
                                                  scalar1=sc16[:, 11:12], scalar2=None,
                                                  op0=ALU.mult), "vc")
                            C.inc(v.tensor_scalar(out=ogrp[:, gi % 2, e, :],
                                                  in0=oi_t[:, k % 2, :],
                                                  scalar1=sc16[:, 10:11], scalar2=None,
                                                  op0=ALU.mult), "vc")
                    if gi >= 1:
                        g = gi - 1
                        if g >= 1:
                            C.w(v, "px", px0 + 64 * g)
                        C.w(v, "lx", lx0 + 64 * (g + 1))
                        for pair in range(2):
                            C.w(v, "c1", c1_0 + 2 * IC * g + IC * (pair + 1))
                            for t2 in range(2):
                                tt = pair * 2 + t2
                                v.scalar_tensor_tensor(out=xnew[:, tt, 0:GT],
                                                       in0=ps[2 * t2][:], scalar=0.5,
                                                       in1=xrt[:, tt, 0:GT],
                                                       op0=ALU.mult, op1=ALU.add)
                                C.inc(v.scalar_tensor_tensor(out=xnew[:, tt, GT:D],
                                                             in0=ps[2 * t2 + 1][:],
                                                             scalar=0.5,
                                                             in1=xrt[:, tt, GT:D],
                                                             op0=ALU.mult, op1=ALU.add),
                                      "vc")
            b.vector(_)

            def _(t):
                C.w(t, "lx", ld_wd)
                for g in range(NG):
                    C.w(t, "vc", vdone[g * IC + IC - 1])
                    if g >= 1:
                        C.w(t, "vc", S[g - 1] + 4)
                    for pair in range(2):
                        if pair == 1:
                            C.w(t, "vc", S[g] + 2)
                        for e in range(IC):
                            for t2 in range(2):
                                tt = pair * 2 + t2
                                t.matmul(ps[2 * t2][:],
                                         ogrp[:, g % 2, e, tt * 128:(tt + 1) * 128],
                                         wdsb[:, e, 0:GT],
                                         start=(e == 0), stop=(e == IC - 1))
                                ins = t.matmul(ps[2 * t2 + 1][:],
                                               ogrp[:, g % 2, e, tt * 128:(tt + 1) * 128],
                                               wdsb[:, e, GT:D],
                                               start=(e == 0), stop=(e == IC - 1))
                            C.inc(ins, "c1")
            b.tensor(_)

            def _(g2):
                for g in range(NG):
                    for tt in range(4):
                        C.w(g2, "vc", S[g] + tt + 1)
                        C.dinc(g2.dma_start(out=xdst[g * 4 + tt, :, :],
                                            in_=xnew[:, tt, :]), "px")
            b.gpsimd(_)
            full_barrier(C)

    full_barrier(C)



def emit_attn(C, xsrc, xdst, io):
    nc, b = C.nc, C.block
    bufs = C.bufs
    ps, psT2, psS = bufs["ps"], bufs["psT2"], bufs["psS"]
    identb = bufs["identb"]
    mu0_c, r0_c = bufs["mu0_c"], bufs["r0_c"]
    b0_c, r1_c = bufs["b0_c"], bufs["r1_c"]
    exp_ls = bufs["exp_ls"]
    bom_row, b2_row = bufs["bom_row"], bufs["b2_row"]
    ones_row, ones_colb = bufs["ones_row"], bufs["ones_colb"]
    const_bf, colsum_bf = bufs["const_bf"], bufs["colsum_bf"]
    carry, carry_r, carry_i = bufs["carry"], bufs["carry_r"], bufs["carry_i"]
    off_phi, off_r, off_i = bufs["off_phi"], bufs["off_r"], bufs["off_i"]
    colv = bufs["colv"]
    sp_phi, sp_xln = io["sp_phi"], io["sp_xln"]
    sp_cos, sp_sin = io["sp_cos"], io["sp_sin"]
    sp_cmr, sp_cmi = io["sp_cmr"], io["sp_cmi"]
    sp_Sr, sp_Si = io["sp_Sr"], io["sp_Si"]

    emit_ln0_stats(C, xsrc, False, mu0_c, r0_c)

    def _(v):
        C.w(v, "vc", C.n["vc"])
        C.inc(v.tensor_mul(b0_c[:], mu0_c[:], r0_c[:]), "vc")
        C.w(v, "vc", C.n["vc"])
        v.tensor_scalar(out=b0_c[:], in0=b0_c[:], scalar1=-1.0, scalar2=None, op0=ALU.mult)
        v.tensor_scalar(out=r1_c[:], in0=r0_c[:], scalar1=1.0, scalar2=None, op0=ALU.add)
        v.memset(carry[:], 0.0)
        v.memset(carry_r[:], 0.0)
        C.inc(v.memset(carry_i[:], 0.0), "vc")
    b.vector(_)
    full_barrier(C)

    with contextlib.ExitStack() as pes:
        PE = pes.enter_context
        xa = bufs["xstats"]
        xlntm = PE(nc.sbuf_tensor("a_xlntm", [128, 2, D], BF16))
        xlnT = PE(nc.sbuf_tensor("a_xlnT", [128, DC, GT], BF16))
        womsb = PE(nc.sbuf_tensor("a_wom", [128, DC, D], BF16))
        oma = PE(nc.sbuf_tensor("a_oma", [128, DC, GT], F32))
        invsqt = PE(nc.sbuf_tensor("a_invsq", [128, GT], F32))

        womt = io["womT_h"].rearrange("(c p) e -> c p e", p=128)

        def _(sync):
            for d in range(DC):
                C.dinc(sync.dma_start(out=womsb[:, d, :], in_=womt[d, :, :]), "lx")
            C.fence(sync, "lx")
        b.sync(_)
        ld_wom = C.n["lx"]

        for g in range(NG):
            stxa = C.stream("l")
            lx0 = C.n["lx"]
            s0 = C.n["sc"]
            v0 = C.n["vc"]
            t0n = C.n["tc"]

            def _(sync, g=g, v0=v0, stxa=stxa):
                C.dinc(sync.dma_start(out=invsqt[:],
                                      in_=io["invsq_in"][0:1, g * GT:(g + 1) * GT]
                                      .broadcast_to((128, GT))), "lx")
                for tt in range(4):
                    if tt >= 2:
                        C.w(sync, "vc", v0 + (tt - 1) * DC)
                    stxa.issue(sync.dma_start(out=xa[:, tt % 2, :], in_=xsrc[g * 4 + tt, :, :]), tt)
            b.sync(_)

            def _(s, g=g, stxa=stxa):
                for tt in range(4):
                    i = g * 4 + tt
                    stxa.wait(s, tt)
                    C.inc(s.activation(out=xlntm[:, tt % 2, :], in_=xa[:, tt % 2, :],
                                       func=AF.Identity,
                                       scale=r0_c[:, i:i + 1], bias=b0_c[:, i:i + 1]), "sc")
            b.scalar(_)

            def _(t, s0=s0, v0=v0):
                for tt in range(4):
                    C.w(t, "sc", s0 + tt + 1)
                    for dc in range(DC):
                        k = tt * DC + dc
                        if k >= 2:
                            C.w(t, "vc", v0 + k - 1)
                        C.inc(t.transpose(psT2[k % 2][:, 0:128],
                                          xlntm[:, tt % 2, dc * 128:(dc + 1) * 128],
                                          identb[:]), "tc")
            b.tensor(_)

            def _(v, t0n=t0n):
                for tt in range(4):
                    for dc in range(DC):
                        k = tt * DC + dc
                        C.w(v, "tc", t0n + k + 1)
                        C.inc(v.tensor_copy(xlnT[:, dc, tt * 128:(tt + 1) * 128],
                                            psT2[k % 2][:, 0:128]), "vc")
            b.vector(_)
            v_xlnT = C.n["vc"]
            t_tr = C.n["tc"]

            def _(g2, g=g, v_xlnT=v_xlnT):
                C.w(g2, "vc", v_xlnT)
                C.dinc(g2.dma_start(out=sp_xln[g, :, :, :], in_=xlnT[:]), "px")
            b.gpsimd(_)

            s_ev0 = C.n["sc"]

            def _(t, v_xlnT=v_xlnT, t_tr=t_tr, s_ev0=s_ev0):
                C.w(t, "vc", v_xlnT)
                C.w(t, "lx", ld_wom)
                for ec in range(DC):
                    bank = 4 + (ec % 2)
                    if ec >= 2:
                        C.w(t, "sc", s_ev0 + ec - 1)
                    for d in range(DC):
                        t.matmul(ps[bank][:], womsb[:, d, ec * 128:(ec + 1) * 128],
                                 xlnT[:, d, :], start=(d == 0), stop=False)
                    C.inc(t.matmul(ps[bank][:], bom_row[0:1, ec * 128:(ec + 1) * 128],
                                   ones_row[0:1, :], start=False, stop=True), "tc")
            b.tensor(_)
            t_om0 = t_tr

            def _(s, t_om0=t_om0):
                for ec in range(DC):
                    C.w(s, "tc", t_om0 + ec + 1)
                    C.inc(s.activation(out=oma[:, ec, :], in_=ps[4 + (ec % 2)][:],
                                       func=AF.Identity, scale=exp_ls[:, ec:ec + 1]), "sc")
            b.scalar(_)

            def _(v, g=g, lx0=lx0):
                C.w(v, "sc", C.n["sc"])
                C.w(v, "lx", lx0 + 16)
                v.tensor_tensor(out=oma[:], in0=oma[:],
                                in1=invsqt[:].rearrange("p (o t) -> p o t", o=1)
                                .broadcast_to((128, DC, GT)),
                                op=ALU.mult)
                for ec in range(DC):
                    ins_scan = v.tensor_tensor_scan(out=oma[:, ec, :], data0=oma[:, ec, :],
                                                    data1=oma[:, ec, :],
                                                    initial=carry[:, ec:ec + 1],
                                                    op0=ALU.add, op1=ALU.bypass)
                C.w(v, "vc", C.inc(ins_scan, "vc"))
                C.inc(v.tensor_copy(carry[:], oma[:, :, GT - 1]), "vc")
            b.vector(_)

            def _(g2, g=g):
                C.w(g2, "vc", C.n["vc"])
                C.dinc(g2.dma_start(out=sp_phi[g, :, :, :], in_=oma[:]), "px")
            b.gpsimd(_)
            full_barrier(C)

        emit_carry_gather(C, [carry], [off_phi])

        kint = PE(nc.sbuf_tensor("a_k", [128, DC, GT], I32))
        phi1 = PE(nc.sbuf_tensor("a_phi1", [128, DC, GT], F32))
        cosb = PE(nc.sbuf_tensor("a_cos", [128, DC, GT], BF16))
        sinb = PE(nc.sbuf_tensor("a_sin", [128, DC, GT], BF16))
        cmr = PE(nc.sbuf_tensor("a_cmr", [128, DC, GT], BF16))
        cmi = PE(nc.sbuf_tensor("a_cmi", [128, DC, GT], BF16))
        Srb = PE(nc.sbuf_tensor("a_Sr", [128, DC, GT], F32))
        Sib = PE(nc.sbuf_tensor("a_Si", [128, DC, GT], F32))

        for g in range(NG):
            lx0 = C.n["lx"]

            def _(sync, g=g):
                C.dinc(sync.dma_start(out=oma[:], in_=sp_phi[g, :, :, :]), "lx")
                C.dinc(sync.dma_start(out=xlnT[:], in_=sp_xln[g, :, :, :]), "lx")
            b.sync(_)

            def _(v, lx0=lx0):
                C.w(v, "lx", lx0 + 32)
                for ec in range(DC):
                    v.tensor_scalar(out=oma[:, ec, :], in0=oma[:, ec, :],
                                    scalar1=off_phi[:, ec:ec + 1], scalar2=None, op0=ALU.add)
                C.inc(v.tensor_copy(junk_one(C), oma[:, 0, 0:1]), "vc")
            b.vector(_)
            v_phi = C.n["vc"]

            def _(s, v_phi=v_phi):
                C.w(s, "vc", v_phi)
                C.inc(s.activation(out=kint[:], in_=oma[:], func=AF.Identity,
                                   scale=float(1.0 / TWO_PI)), "sc")
            b.scalar(_)
            s_k = C.n["sc"]

            def _(v, s_k=s_k):
                C.w(v, "sc", s_k)
                v.scalar_tensor_tensor(out=phi1[:], in0=kint[:], scalar=-RC1, in1=oma[:],
                                       op0=ALU.mult, op1=ALU.add)
                v.scalar_tensor_tensor(out=phi1[:], in0=kint[:], scalar=-RC2, in1=phi1[:],
                                       op0=ALU.mult, op1=ALU.add)
                v.tensor_scalar(out=Srb[:], in0=phi1[:], scalar1=PIF, scalar2=float(-TWO_PI),
                                op0=ALU.is_gt, op1=ALU.mult)
                v.tensor_add(phi1[:], phi1[:], Srb[:])
                v.tensor_scalar(out=Srb[:], in0=phi1[:], scalar1=-PIF, scalar2=float(TWO_PI),
                                op0=ALU.is_lt, op1=ALU.mult)
                v.tensor_add(phi1[:], phi1[:], Srb[:])
                C.inc(v.tensor_scalar(out=phi1[:], in0=phi1[:], scalar1=PIF,
                                      scalar2=-PIF, op0=ALU.min, op1=ALU.max), "vc")
            b.vector(_)
            v_ph1 = C.n["vc"]

            def _(s, v_ph1=v_ph1):
                C.w(s, "vc", v_ph1)
                C.inc(s.activation(out=sinb[:], in_=phi1[:], func=AF.Sin), "sc")
            b.scalar(_)
            s_sin = C.n["sc"]

            def _(v, s_sin=s_sin):
                C.w(v, "sc", s_sin)
                v.tensor_scalar(out=oma[:], in0=phi1[:], scalar1=PIH,
                                scalar2=float(-TWO_PI), op0=ALU.is_gt, op1=ALU.mult)
                C.inc(v.scalar_tensor_tensor(out=oma[:], in0=phi1[:], scalar=PIH,
                                             in1=oma[:], op0=ALU.add, op1=ALU.add), "vc")
            b.vector(_)
            v_psi = C.n["vc"]

            def _(s, v_psi=v_psi):
                C.w(s, "vc", v_psi)
                C.inc(s.activation(out=cosb[:], in_=oma[:], func=AF.Sin), "sc")
            b.scalar(_)
            s_cos = C.n["sc"]

            def _(v, s_cos=s_cos):
                C.w(v, "sc", s_cos)
                v.tensor_tensor(out=cmr[:], in0=xlnT[:], in1=cosb[:], op=ALU.mult)
                v.tensor_tensor(out=cmi[:], in0=xlnT[:], in1=sinb[:], op=ALU.mult)
                for ec in range(DC):
                    v.tensor_tensor_scan(out=Srb[:, ec, :], data0=cmr[:, ec, :],
                                         data1=cmr[:, ec, :], initial=carry_r[:, ec:ec + 1],
                                         op0=ALU.add, op1=ALU.bypass)
                    ins_scan2 = v.tensor_tensor_scan(out=Sib[:, ec, :], data0=cmi[:, ec, :],
                                                     data1=cmi[:, ec, :],
                                                     initial=carry_i[:, ec:ec + 1],
                                                     op0=ALU.add, op1=ALU.bypass)
                C.w(v, "vc", C.inc(ins_scan2, "vc"))
                v.tensor_copy(carry_r[:], Srb[:, :, GT - 1])
                C.inc(v.tensor_copy(carry_i[:], Sib[:, :, GT - 1]), "vc")
            b.vector(_)

            def _(g2, g=g):
                C.w(g2, "vc", C.n["vc"])
                C.dinc(g2.dma_start(out=sp_cos[g, :, :, :], in_=cosb[:]), "px")
                C.dinc(g2.dma_start(out=sp_sin[g, :, :, :], in_=sinb[:]), "px")
                C.dinc(g2.dma_start(out=sp_cmr[g, :, :, :], in_=cmr[:]), "px")
                C.dinc(g2.dma_start(out=sp_cmi[g, :, :, :], in_=cmi[:]), "px")
                C.dinc(g2.dma_start(out=sp_Sr[g, :, :, :], in_=Srb[:]), "px")
                C.dinc(g2.dma_start(out=sp_Si[g, :, :, :], in_=Sib[:]), "px")
            b.gpsimd(_)
            full_barrier(C)

        emit_carry_gather(C, [carry_r, carry_i], [off_r, off_i])

    with contextlib.ExitStack() as pes:
        PE = pes.enter_context
        ctxb = PE(nc.sbuf_tensor("a3_ctx", [128, FC, GT], BF16))
        Srb = PE(nc.sbuf_tensor("a3_Sr", [128, DC, GT], F32))
        Sib = PE(nc.sbuf_tensor("a3_Si", [128, DC, GT], F32))
        cosb = PE(nc.sbuf_tensor("a3_cos", [128, DC, GT], BF16))
        sinb = PE(nc.sbuf_tensor("a3_sin", [128, DC, GT], BF16))
        tmp1 = PE(nc.sbuf_tensor("a3_t1", [128, GT], BF16))
        tmp2 = PE(nc.sbuf_tensor("a3_t2", [128, GT], BF16))
        invpt = PE(nc.sbuf_tensor("a3_invp", [128, GT], F32))
        w1st = PE(nc.sbuf_tensor("a3_w1", [128, 4, GT], BF16))
        w2sb = PE(nc.sbuf_tensor("a3_w2", [128, EC2, D], BF16))
        sqc = PE(nc.sbuf_tensor("a3_sq", [128, 2, GT], BF16))
        hT = PE(nc.sbuf_tensor("a3_hT", [128, 4, EC2, 128], BF16))
        stt4 = PE(nc.sbuf_tensor("a3_st", [128, 16], F32))
        rows_f = PE(nc.sbuf_tensor("a3_rowf", [1, GT], F32))
        negmu_rb = PE(nc.sbuf_tensor("a3_nmu", [1, GT], BF16))
        invr_rb = PE(nc.sbuf_tensor("a3_ivr", [1, GT], BF16))
        rows_f2 = PE(nc.sbuf_tensor("a3_rowf2", [1, GT], F32))
        xrt = PE(nc.sbuf_tensor("a3_xr", [128, 2, D], F32))
        xnew = PE(nc.sbuf_tensor("a3_xn", [128, 2, D], F32))

        w1gt = io["w1g_h"].rearrange("(c p) e -> c p e", p=128)
        w2t = io["w2T_h"].rearrange("(c p) e -> c p e", p=128)

        def _(sync):
            for e2c in range(EC2):
                C.dinc(sync.dma_start(out=w2sb[:, e2c, :], in_=w2t[e2c, :, :]), "lx")
            C.fence(sync, "lx")
        b.sync(_)
        ld_w2 = C.n["lx"]

        for g in range(NG):
            def _(sync, g=g):
                C.dinc(sync.dma_start(out=Srb[:], in_=sp_Sr[g, :, :, :]), "lx")
                C.dinc(sync.dma_start(out=Sib[:], in_=sp_Si[g, :, :, :]), "lx")
                C.dinc(sync.dma_start(out=cosb[:], in_=sp_cos[g, :, :, :]), "lx")
                C.dinc(sync.dma_start(out=sinb[:], in_=sp_sin[g, :, :, :]), "lx")
                C.dinc(sync.dma_start(out=ctxb[:, 0:DC, :], in_=sp_cmr[g, :, :, :]), "lx")
                C.dinc(sync.dma_start(out=ctxb[:, DC:2 * DC, :], in_=sp_cmi[g, :, :, :]), "lx")
                C.dinc(sync.dma_start(out=invpt[:],
                                      in_=io["invp_in"][0:1, g * GT:(g + 1) * GT]
                                      .broadcast_to((128, GT))), "lx")
                C.fence(sync, "lx")
            b.sync(_)
            ld_sp = C.n["lx"]

            def _(v, ld_sp=ld_sp):
                C.w(v, "lx", ld_sp)
                for ec in range(DC):
                    v.tensor_scalar(out=Srb[:, ec, :], in0=Srb[:, ec, :],
                                    scalar1=off_r[:, ec:ec + 1], scalar2=None, op0=ALU.add)
                    v.tensor_scalar(out=Sib[:, ec, :], in0=Sib[:, ec, :],
                                    scalar1=off_i[:, ec:ec + 1], scalar2=None, op0=ALU.add)
                for ec in range(DC):
                    v.tensor_tensor(out=tmp1[:], in0=Srb[:, ec, :], in1=cosb[:, ec, :], op=ALU.mult)
                    v.tensor_tensor(out=tmp2[:], in0=Sib[:, ec, :], in1=sinb[:, ec, :], op=ALU.mult)
                    v.tensor_add(tmp1[:], tmp1[:], tmp2[:])
                    v.tensor_tensor(out=ctxb[:, 2 * DC + ec, :], in0=tmp1[:], in1=invpt[:], op=ALU.mult)
                    v.tensor_tensor(out=tmp1[:], in0=Sib[:, ec, :], in1=cosb[:, ec, :], op=ALU.mult)
                    v.tensor_tensor(out=tmp2[:], in0=Srb[:, ec, :], in1=sinb[:, ec, :], op=ALU.mult)
                    v.tensor_sub(tmp1[:], tmp1[:], tmp2[:])
                    v.tensor_tensor(out=ctxb[:, 3 * DC + ec, :], in0=tmp1[:], in1=invpt[:], op=ALU.mult)
                C.inc(v.tensor_copy(junk_one(C), ctxb[:, FC - 1, 0:1]), "vc")
            b.vector(_)
            v_ctx = C.n["vc"]

            s_sq0 = C.n["sc"]
            c20 = C.n["c2"]

            def _(s, v_ctx=v_ctx, c20=c20):
                C.w(s, "vc", v_ctx)
                for fc in range(FC):
                    if fc >= 2:
                        C.w(s, "c2", c20 + (fc - 1))
                    C.inc(s.activation(out=sqc[:, fc % 2, :], in_=ctxb[:, fc, :],
                                       func=AF.Square), "sc")
            b.scalar(_)

            stw1a = C.stream("l")
            stw1b = C.stream("w")
            c10 = C.n["c1"]

            def _(sync, g=g, v_ctx=v_ctx, c10=c10, stw1a=stw1a, stw1b=stw1b):
                C.w(sync, "vc", v_ctx)
                for e2g in range(4):
                    for fc in range(FC):
                        k = e2g * FC + fc
                        if k >= 4:
                            C.w(sync, "c1", c10 + (k - 3))
                        st = (stw1a, stw1b)[k % 2]
                        st.issue(sync.dma_start(out=w1st[:, k % 4, :],
                                                in_=w1gt[fc, :, e2g * GT:(e2g + 1) * GT]), k // 2)
            b.sync(_)

            t_g0 = C.n["tc"]
            c10_t = c10
            c30 = C.n["c3"]
            v_rows = v_ctx + 12
            s_gelu0 = s_sq0 + FC + 1

            def _(t, v_ctx=v_ctx, stw1a=stw1a, stw1b=stw1b, s_sq0=s_sq0, v_rows=v_rows,
                  s_gelu0=s_gelu0, c30=c30):
                C.w(t, "vc", v_ctx)
                for e2g in range(4):
                    if e2g > 0:
                        C.w(t, "sc", s_gelu0 + 4 * e2g)
                    for fc in range(FC):
                        k = e2g * FC + fc
                        (stw1a, stw1b)[k % 2].wait(t, k // 2)
                        if e2g == 0:
                            C.w(t, "sc", s_sq0 + fc + 1)
                        for tt in range(4):
                            mm_w1 = t.matmul(ps[tt][:], ctxb[:, fc, tt * 128:(tt + 1) * 128],
                                             w1st[:, k % 4, :], start=(fc == 0), stop=False)
                            if e2g == 0:
                                t.matmul(psS[:, 2 * tt:2 * tt + 1],
                                         ctxb[:, fc, tt * 128:(tt + 1) * 128],
                                         ones_colb[:, 0:1],
                                         start=(fc == 0 and tt == 0), stop=False)
                                mm_sq = t.matmul(psS[:, 2 * tt + 1:2 * tt + 2],
                                                 sqc[:, fc % 2, tt * 128:(tt + 1) * 128],
                                                 ones_colb[:, 0:1], start=False,
                                                 stop=(fc == FC - 1 and tt == 3))
                        C.inc(mm_w1, "c1")
                        if e2g == 0:
                            C.inc(mm_sq, "c2")
                    if e2g == 0:
                        C.w(t, "vc", v_rows)
                    for tt in range(4):
                        t.matmul(ps[tt][:], negmu_rb[0:1, tt * 128:(tt + 1) * 128],
                                 colsum_bf[0:1, e2g * GT:(e2g + 1) * GT],
                                 start=False, stop=False)
                        last_r1 = t.matmul(ps[tt][:], invr_rb[0:1, tt * 128:(tt + 1) * 128],
                                           const_bf[0:1, e2g * GT:(e2g + 1) * GT],
                                           start=False, stop=True)
                    C.inc(last_r1, "c3")
            b.tensor(_)

            def _(v, c20=c20):
                C.w(v, "c2", c20 + FC)
                v.tensor_scalar(out=stt4[:, 0:4], in0=psS[:, 0:8:2], scalar1=1.0 / D4,
                                scalar2=None, op0=ALU.mult)
                C.inc(v.tensor_scalar(out=stt4[:, 4:8], in0=psS[:, 1:8:2], scalar1=1.0 / D4,
                                      scalar2=None, op0=ALU.mult), "vc")
                C.w(v, "vc", C.n["vc"])
                C.inc(v.tensor_mul(stt4[:, 8:12], stt4[:, 0:4], stt4[:, 0:4]), "vc")
                C.w(v, "vc", C.n["vc"])
                C.inc(v.tensor_sub(stt4[:, 4:8], stt4[:, 4:8], stt4[:, 8:12]), "vc")
                C.w(v, "vc", C.n["vc"])
                C.inc(v.tensor_scalar(out=stt4[:, 4:8], in0=stt4[:, 4:8], scalar1=LN_EPS,
                                      scalar2=None, op0=ALU.add), "vc")
            b.vector(_)
            v_ve = C.n["vc"]

            def _(s, v_ve=v_ve):
                C.w(s, "vc", v_ve)
                C.inc(s.sqrt(stt4[:, 8:12], stt4[:, 4:8]), "sc")
            b.scalar(_)
            s_sv = C.n["sc"]

            def _(v, s_sv=s_sv):
                C.w(v, "sc", s_sv)
                C.inc(v.reciprocal(stt4[:, 12:16], stt4[:, 8:12]), "vc")
                C.w(v, "vc", C.n["vc"])
                C.inc(v.tensor_mul(stt4[:, 0:4], stt4[:, 12:16], stt4[:, 12:16]), "vc")
                C.w(v, "vc", C.n["vc"])
                C.inc(v.tensor_mul(stt4[:, 0:4], stt4[:, 0:4], stt4[:, 4:8]), "vc")
                C.w(v, "vc", C.n["vc"])
                C.inc(v.tensor_scalar(out=stt4[:, 0:4], in0=stt4[:, 0:4], scalar1=-0.5,
                                      scalar2=1.5, op0=ALU.mult, op1=ALU.add), "vc")
                C.w(v, "vc", C.n["vc"])
                C.inc(v.tensor_mul(stt4[:, 12:16], stt4[:, 12:16], stt4[:, 0:4]), "vc")
                C.w(v, "vc", C.n["vc"])
                C.inc(v.tensor_scalar(out=stt4[:, 0:4], in0=psS[:, 0:8:2],
                                      scalar1=-1.0 / D4, scalar2=None, op0=ALU.mult), "vc")
            b.vector(_)
            v_nm = C.n["vc"]

            def _(g2, v_nm=v_nm):
                C.w(g2, "vc", v_nm)
                C.fence(g2, "px")
                for tt in range(4):
                    C.dinc(g2.dma_start(out=rows_f[0:1, tt * 128:(tt + 1) * 128],
                                        in_=stt4[:, tt:tt + 1]), "px")
                    C.dinc(g2.dma_start(out=rows_f2[0:1, tt * 128:(tt + 1) * 128],
                                        in_=stt4[:, 8 + tt:9 + tt]), "px")
            b.gpsimd(_)
            g_rows = C.n["px"]

            def _(v, g_rows=g_rows):
                C.w(v, "px", g_rows)
                C.inc(v.tensor_copy(negmu_rb[0:1, :], rows_f[0:1, :]), "vc")
                C.inc(v.tensor_copy(invr_rb[0:1, :], rows_f2[0:1, :]), "vc")
            b.vector(_)
            assert C.n["vc"] == v_rows, (C.n["vc"], v_rows)

            def hslot(k):
                return cosb[:, k, :] if k < DC else sinb[:, k - DC, :]

            def _(s, c30=c30, s_gelu0=s_gelu0):
                assert C.n["sc"] == s_gelu0, (C.n["sc"], s_gelu0)
                for e2g in range(4):
                    C.w(s, "c3", c30 + e2g + 1)
                    for tt in range(4):
                        C.inc(s.activation(out=hslot(e2g * 4 + tt), in_=ps[tt][:],
                                           func=AF.Gelu,
                                           scale=stt4[:, 12 + tt:13 + tt]), "sc")
            b.scalar(_)

            t_tr0 = C.n["tc"]
            v_tr0 = v_rows

            def _(t, s_gelu0=s_gelu0, v_tr0=v_tr0):
                C.w(t, "sc", s_gelu0 + 16)
                n = 0
                for e2g in range(4):
                    for tt in range(4):
                        for q in range(4):
                            if n >= 2:
                                C.w(t, "vc", v_tr0 + n - 1)
                            C.inc(t.transpose(
                                psT2[n % 2][:, 0:128],
                                hslot(e2g * 4 + tt)[:, q * 128:(q + 1) * 128],
                                identb[:]), "tc")
                            n += 1
            b.tensor(_)

            def _(v, t_tr0=t_tr0):
                n = 0
                for e2g in range(4):
                    for tt in range(4):
                        for q in range(4):
                            n += 1
                            C.w(v, "tc", t_tr0 + n)
                            C.inc(v.tensor_copy(hT[:, tt, e2g * 4 + q, :],
                                                psT2[(n - 1) % 2][:, 0:128]), "vc")
            b.vector(_)
            v_resid0 = C.n["vc"]
            assert v_resid0 == v_tr0 + 64

            t_w20 = C.n["tc"]
            stx3 = C.stream("l2")
            stst3 = C.stream("p")
            v_resid_base = v_resid0

            def _(t, v_resid0=v_resid0, t_w20=t_w20):
                C.w(t, "vc", v_resid0)
                C.w(t, "lx", ld_w2)
                for tt in range(4):
                    bank0 = (tt % 2) * 2
                    if tt >= 2:
                        C.w(t, "vc", v_resid0 + (tt - 1))
                    for e2c in range(EC2):
                        t.matmul(ps[bank0][:], hT[:, tt, e2c, :], w2sb[:, e2c, 0:GT],
                                 start=(e2c == 0), stop=False)
                        t.matmul(ps[bank0 + 1][:], hT[:, tt, e2c, :], w2sb[:, e2c, GT:D],
                                 start=(e2c == 0), stop=False)
                    t.matmul(ps[bank0][:], ones_row[0:1, 0:128], b2_row[0:1, 0:GT],
                             start=False, stop=True)
                    C.inc(t.matmul(ps[bank0 + 1][:], ones_row[0:1, 0:128],
                                   b2_row[0:1, GT:D], start=False, stop=True), "tc")
            b.tensor(_)

            def _(v, g=g, t_w20=t_w20, stx3=stx3):
                for tt in range(4):
                    i = g * 4 + tt
                    bank0 = (tt % 2) * 2
                    C.w(v, "tc", t_w20 + tt + 1)
                    stx3.wait(v, tt)
                    v.tensor_scalar(out=xnew[:, tt % 2, :], in0=xrt[:, tt % 2, :],
                                    scalar1=r1_c[:, i:i + 1], scalar2=b0_c[:, i:i + 1],
                                    op0=ALU.mult, op1=ALU.add)
                    v.tensor_add(xnew[:, tt % 2, 0:GT], xnew[:, tt % 2, 0:GT], ps[bank0][:])
                    C.inc(v.tensor_add(xnew[:, tt % 2, GT:D], xnew[:, tt % 2, GT:D],
                                       ps[bank0 + 1][:]), "vc")
            b.vector(_)
            v_res_end = C.n["vc"]

            def _(sync, g=g, v_resid_base=v_resid_base, stx3=stx3, stst3=stst3):
                for tt in range(4):
                    if tt >= 2:
                        C.w(sync, "vc", v_resid_base + (tt - 1))
                        stst3.wait(sync, tt - 2)
                    stx3.issue(sync.dma_start(out=xrt[:, tt % 2, :], in_=xsrc[g * 4 + tt, :, :]), tt)
            b.sync(_)

            def _(g2, g=g, v_res_end=v_res_end, stst3=stst3):
                for tt in range(4):
                    C.w(g2, "vc", v_res_end - (3 - tt))
                    stst3.issue(g2.dma_start(out=xdst[g * 4 + tt, :, :], in_=xnew[:, tt % 2, :]), tt)
            b.gpsimd(_)
            full_barrier(C)

    full_barrier(C)


def junk_one(C):
    return C.bufs["junkb"][:, 0:1]



DEBUG_BUILD = False
STAGES = 3
SUBSTAGE = 0


def build_nc():
    nc = bass.Bass("TRN2", target_bir_lowering=False, debug=DEBUG_BUILD,
                   detect_race_conditions=False)
    io = {}
    io["x_in"] = nc.dram_tensor("x", [TPC, D], F32, kind="ExternalInput").ap()
    io["out"] = nc.dram_tensor("out", [TPC, D], F32, kind="ExternalOutput").ap()
    io["wup_raw"] = nc.dram_tensor("wup", [I3, D], F32, kind="ExternalInput").ap()
    io["wgate_raw"] = nc.dram_tensor("wgate", [I3, D], F32, kind="ExternalInput").ap()
    io["wdown_raw"] = nc.dram_tensor("wdown", [D, I3], F32, kind="ExternalInput").ap()
    io["womT_in"] = nc.dram_tensor("womT", [D, D], F32, kind="ExternalInput").ap()
    io["w1T_in"] = nc.dram_tensor("w1T", [D4, D2], F32, kind="ExternalInput").ap()
    io["w2T_in"] = nc.dram_tensor("w2T", [D2, D], F32, kind="ExternalInput").ap()
    io["bom_in"] = nc.dram_tensor("bom", [1, D], F32, kind="ExternalInput").ap()
    io["b1_in"] = nc.dram_tensor("b1", [1, D2], F32, kind="ExternalInput").ap()
    io["b2_in"] = nc.dram_tensor("b2", [1, D], F32, kind="ExternalInput").ap()
    io["gamma_in"] = nc.dram_tensor("gammac", [128, FC], F32, kind="ExternalInput").ap()
    io["beta_in"] = nc.dram_tensor("betac", [128, FC], F32, kind="ExternalInput").ap()
    io["ls_in"] = nc.dram_tensor("lsc", [128, DC], F32, kind="ExternalInput").ap()
    io["identb_in"] = nc.dram_tensor("identb", [128, 128], BF16, kind="ExternalInput").ap()
    io["invsq_in"] = nc.dram_tensor("invsq", [1, TPC], F32, kind="ExternalInput").ap()
    io["invp_in"] = nc.dram_tensor("invp", [1, TPC], F32, kind="ExternalInput").ap()
    io["cmask_in"] = nc.dram_tensor("cmask", [1, NCORES], F32, kind="ExternalInput").ap()

    io["xres"] = nc.dram_tensor("xres", [NT, 128, D], F32).ap()
    io["wupT_h"] = nc.dram_tensor("wupT_h", [D, I3], BF16).ap()
    io["wgT_h"] = nc.dram_tensor("wgT_h", [D, I3], BF16).ap()
    io["wdT_h"] = nc.dram_tensor("wdT_h", [I3, D], BF16).ap()
    io["womT_h"] = nc.dram_tensor("womT_h", [D, D], BF16).ap()
    io["w1g_h"] = nc.dram_tensor("w1g_h", [D4, D2], BF16).ap()
    io["w2T_h"] = nc.dram_tensor("w2T_h", [D2, D], BF16).ap()
    io["sp_up"] = nc.dram_tensor("sp_up", [IC, NG, 128, GT], BF16).ap()
    io["sp_gate"] = nc.dram_tensor("sp_gate", [IC, NG, 128, GT], BF16).ap()
    io["sp_gi"] = nc.dram_tensor("sp_gi", [IC, NG, 128, GT], I16).ap()
    io["sp_sig"] = nc.dram_tensor("sp_sig", [IC, NG, 128, GT], U16).ap()
    io["sp_p"] = nc.dram_tensor("sp_p", [IC, NG, 128, GT], F32).ap()
    io["sp_phi"] = nc.dram_tensor("sp_phi", [NG, 128, DC, GT], F32).ap()
    io["sp_xln"] = nc.dram_tensor("sp_xln", [NG, 128, DC, GT], BF16).ap()
    io["sp_cos"] = nc.dram_tensor("sp_cos", [NG, 128, DC, GT], BF16).ap()
    io["sp_sin"] = nc.dram_tensor("sp_sin", [NG, 128, DC, GT], BF16).ap()
    io["sp_cmr"] = nc.dram_tensor("sp_cmr", [NG, 128, DC, GT], BF16).ap()
    io["sp_cmi"] = nc.dram_tensor("sp_cmi", [NG, 128, DC, GT], BF16).ap()
    io["sp_Sr"] = nc.dram_tensor("sp_Sr", [NG, 128, DC, GT], F32).ap()
    io["sp_Si"] = nc.dram_tensor("sp_Si", [NG, 128, DC, GT], F32).ap()

    es = contextlib.ExitStack()
    with es:
        E = es.enter_context
        block = E(nc.Block())
        C = Ctx(nc, block, es)
        bufs = C.bufs
        bufs["cc_in"] = nc.dram_tensor("cc_in", [128, 2], F32).ap()
        bufs["cc_out"] = nc.dram_tensor("cc_out", [128, 2], F32).ap()
        bufs["rowb_h"] = nc.dram_tensor("rowb_h", [16, 8], F32).ap()
        bufs["ag_in8"] = nc.dram_tensor("ag_in8", [128, 8], F32).ap()
        bufs["ag_out8"] = nc.dram_tensor("ag_out8", [NCORES * 128, 8], F32).ap()
        bufs["ag_in16"] = nc.dram_tensor("ag_in16", [128, 16], F32).ap()
        bufs["ag_out16"] = nc.dram_tensor("ag_out16", [NCORES * 128, 16], F32).ap()

        def sb(name, shape, dt):
            t = E(nc.sbuf_tensor("sb_" + name, shape, dt))
            bufs[name] = t
            return t

        sb("identb", [128, 128], BF16)
        sb("ones_row", [1, GT], F32)
        sb("ones_colb", [128, 1], BF16)
        sb("bom_row", [1, D], F32)
        sb("b2_row", [1, D], F32)
        sb("const_bf", [1, D2], BF16)
        sb("colsum_bf", [1, D2], BF16)
        sb("const_f", [1, D2], F32)
        sb("gamma_c", [128, FC], F32)
        sb("beta_c", [128, FC], F32)
        sb("exp_ls", [128, DC], F32)
        sb("cmaskt", [128, NCORES], F32)
        for nm in ("s1c", "s2c", "mxc", "mnc", "mu_c", "r_c", "a_c", "bq_c",
                   "mu0_c", "r0_c", "b0_c", "r1_c", "t0_c", "t1_c", "t2_c"):
            sb(nm, [128, NT], F32)
        sb("colv", [128, 8], F32)
        sb("rowf", [16, 256], F32)
        sb("bcp", [128, 2], F32)
        sb("colk", [16, 8], F32)
        sb("bc16", [128, 16], F32)
        sb("sc16", [128, 16], F32)
        sb("junkb", [128, 1024], BF16)
        sb("carry", [128, DC], F32)
        sb("carry_r", [128, DC], F32)
        sb("carry_i", [128, DC], F32)
        sb("off_phi", [128, DC], F32)
        sb("off_r", [128, DC], F32)
        sb("off_i", [128, DC], F32)
        sb("gat", [128, NCORES, 16], F32)
        sb("xstats", [128, 2, D], F32)
        bufs["ps"] = [E(nc.psum_tensor(f"ps{i}", [128, GT], F32)) for i in range(6)]
        bufs["psS"] = bufs["ps"][5]
        bufs["psT2"] = [E(nc.psum_tensor("psT_a", [128, 1024], BF16)),
                        E(nc.psum_tensor("psT_b", [128, 1024], BF16))]

        emit_prologue(C, io)
        x_tiles_in = io["x_in"].rearrange("(n p) d -> n p d", p=128)
        out_tiles = io["out"].rearrange("(n p) d -> n p d", p=128)
        if STAGES == 0:
            pass
        elif STAGES == 1:
            emit_ffn(C, 0, x_tiles_in, out_tiles, io)
        elif STAGES == 2:
            emit_ffn(C, 0, x_tiles_in, io["xres"], io)
            emit_attn(C, io["xres"], out_tiles, io)
        else:
            emit_ffn(C, 0, x_tiles_in, io["xres"], io)
            emit_attn(C, io["xres"], io["xres"], io)
            emit_ffn(C, 1, io["xres"], out_tiles, io)

        snap = C.snap()

        def _(g):
            for s in SEMS:
                if snap[s]:
                    g.wait_ge(C.sem[s], snap[s])
        block.gpsimd(_)
    return nc



_CACHE = {}


def kernel(**inputs):
    x = np.ascontiguousarray(np.asarray(inputs["L0_attn_x"], np.float32)).reshape(B * S, D)
    womT = np.ascontiguousarray(np.asarray(inputs["L0_attn_W_omega"], np.float32).T)
    w1T = np.ascontiguousarray(np.asarray(inputs["L0_attn_W1"], np.float32).T)
    w2T = np.ascontiguousarray(np.asarray(inputs["L0_attn_W2"], np.float32).T)
    bom = np.ascontiguousarray(np.asarray(inputs["L0_attn_b_omega"], np.float32).reshape(1, D))
    b1 = np.ascontiguousarray(np.asarray(inputs["L0_attn_b1"], np.float32).reshape(1, D2))
    b2 = np.ascontiguousarray(np.asarray(inputs["L0_attn_b2"], np.float32).reshape(1, D))
    gamma = np.ascontiguousarray(np.asarray(inputs["L0_attn_ln_gamma"], np.float32).reshape(FC, 128).T)
    beta = np.ascontiguousarray(np.asarray(inputs["L0_attn_ln_beta"], np.float32).reshape(FC, 128).T)
    lsc = np.ascontiguousarray(np.asarray(inputs["L0_attn_log_scale"], np.float32).reshape(DC, 128).T)
    wup = np.ascontiguousarray(np.asarray(inputs["L0_ffn_w_up"], np.float32))
    wgate = np.ascontiguousarray(np.asarray(inputs["L0_ffn_w_gate"], np.float32))
    wdown = np.ascontiguousarray(np.asarray(inputs["L0_ffn_w_down"], np.float32))
    identb = np.eye(128, dtype=NPBF)
    pos = np.arange(1, S + 1, dtype=np.float64)
    invsq_full = (1.0 / np.sqrt(pos)).astype(np.float32)
    invp_full = (1.0 / pos).astype(np.float32)

    if "nc" not in _CACHE:
        _CACHE["nc"] = build_nc()
    nc = _CACHE["nc"]

    in_maps = []
    for c in range(NCORES):
        sl = slice(c * TPC, (c + 1) * TPC)
        so = (c % 4) * TPC
        cmask = np.zeros((1, NCORES), np.float32)
        for cp in range((c // 4) * 4, c):
            cmask[0, cp] = 1.0
        in_maps.append({
            "x": np.ascontiguousarray(x[sl]),
            "womT": womT, "w1T": w1T, "w2T": w2T,
            "bom": bom, "b1": b1, "b2": b2,
            "gammac": gamma, "betac": beta, "lsc": lsc,
            "wup": wup, "wgate": wgate, "wdown": wdown,
            "identb": identb,
            "invsq": np.ascontiguousarray(invsq_full[so:so + TPC].reshape(1, TPC)),
            "invp": np.ascontiguousarray(invp_full[so:so + TPC].reshape(1, TPC)),
            "cmask": cmask,
        })

    res = run_bass_kernel_spmd(nc, in_maps, core_ids=list(range(NCORES)))
    outp = np.concatenate([np.asarray(res.results[c]["out"], np.float32)
                           for c in range(NCORES)], axis=0)
    return outp.reshape(B, S, D)


if __name__ == "__main__":
    nc = build_nc()
    n = sum(1 for _ in nc.all_instructions())
    print("built, instructions:", n)


# version 08
# speedup vs baseline: 1.1092x; 1.0203x over previous
import contextlib
import math

import numpy as np
import ml_dtypes

import concourse.bass as bass
import concourse.mybir as mybir
from concourse.bass_utils import run_bass_kernel_spmd

F32 = mybir.dt.float32
BF16 = mybir.dt.bfloat16
I16 = mybir.dt.int16
U16 = mybir.dt.uint16
I32 = mybir.dt.int32
AF = mybir.ActivationFunctionType
ALU = mybir.AluOpType
AX = mybir.AxisListType.X
NPBF = ml_dtypes.bfloat16

NCORES = 8
B, S, D = 2, 16384, 1024
TPC = 4096
NT = 32
NG = 8
GT = 512
DC = 8
I3, IC = 3072, 24
D4, FC = 4096, 32
D2, EC2 = 2048, 16
LN_EPS = 1e-5
TWO_PI = 2.0 * math.pi
RC1 = float(np.float32(TWO_PI))
RC2 = float(np.float32(TWO_PI - RC1))
QMAX = 32767.0
PIH = float(np.pi / 2)
PIF = float(np.pi)

SEMS = ("le", "lo", "l2e", "l2o", "wae", "wao", "lx", "pe", "po", "px", "tc", "vc", "sc", "cc", "c1", "c2", "c3")


class Ctx:
    def __init__(self, nc, block, es):
        self.nc = nc
        self.block = block
        self.es = es
        self.n = {}
        self.sem = {}
        for name in SEMS:
            self.sem[name] = es.enter_context(nc.semaphore(name))
            self.n[name] = 0
        self.bufs = {}

    def inc(self, instr, name, k=1):
        instr.then_inc(self.sem[name], k)
        self.n[name] += k
        return self.n[name]

    def dinc(self, instr, name):
        return self.inc(instr, name, 16)

    def w(self, eng, name, val):
        if val > 0:
            eng.wait_ge(self.sem[name], val)

    def snap(self):
        return dict(self.n)

    def stream(self, kind):
        return DStream(self, kind)

    def fence(self, eng, name):
        if self.n[name] > 0:
            eng.wait_ge(self.sem[name], self.n[name])


class DStream:

    def __init__(self, C, kind):
        self.C = C
        self.names = {"l": ("le", "lo"), "l2": ("l2e", "l2o"), "w": ("wae", "wao"),
                      "p": ("pe", "po")}[kind]
        self.base = [C.n[self.names[0]], C.n[self.names[1]]]
        self.cnt = [0, 0]

    def issue(self, instr, j):
        nm = self.names[j % 2]
        self.C.inc(instr, nm, 16)
        self.cnt[j % 2] += 16

    def wait(self, eng, j, ndma=1):
        nm = self.names[j % 2]
        eng.wait_ge(self.C.sem[nm], self.base[j % 2] + 16 * ndma * (j // 2 + 1))

    def wait_all(self, eng):
        for i in (0, 1):
            if self.cnt[i]:
                eng.wait_ge(self.C.sem[self.names[i]], self.base[i] + self.cnt[i])


def full_barrier(C):
    snap = C.snap()

    def _(eng):
        for s in SEMS:
            if snap[s]:
                eng.wait_ge(C.sem[s], snap[s])
    for sec in (C.block.sync, C.block.gpsimd, C.block.tensor, C.block.vector, C.block.scalar):
        sec(_)



def emit_allreduce_max(C, cols, nvals):
    nc, b = C.nc, C.block
    bufs = C.bufs
    rowf, colk, bc16, bcp = bufs["rowf"], bufs["colk"], bufs["bc16"], bufs["bcp"]
    cc_in, cc_out, rowb_h = bufs["cc_in"], bufs["cc_out"], bufs["rowb_h"]
    v0 = C.n["vc"]

    def _(g):
        C.w(g, "vc", v0)
        C.fence(g, "px")
        C.dinc(g.dma_start(out=cc_in, in_=cols[:, 0:2]), "px")
        C.fence(g, "px")
        C.inc(g.collective_compute(
            "AllReduce", ALU.max, replica_groups=[list(range(NCORES))],
            ins=[cc_in.opt()], outs=[cc_out.opt()]), "cc")
        C.w(g, "cc", C.n["cc"])
        C.dinc(g.dma_start(out=bcp[:], in_=cc_out), "px")
        C.fence(g, "px")
        C.dinc(g.dma_start(out=rowf[0:1, 0:256], in_=bcp[:, 0:2]), "px")
    b.gpsimd(_)
    sp1 = C.n["px"]

    def _(v):
        C.w(v, "px", sp1)
        C.inc(v.tensor_reduce(out=colk[0:1, 0:2],
                              in_=rowf[0:1, 0:256].rearrange("r (p c) -> r c p", c=2),
                              axis=AX, op=ALU.max), "vc")
    b.vector(_)
    v1 = C.n["vc"]

    def _(g):
        C.w(g, "vc", v1)
        C.dinc(g.dma_start(out=rowb_h[0:1, 0:8], in_=colk[0:1, :]), "px")
        C.fence(g, "px")
        for j in range(nvals):
            C.dinc(g.dma_start(out=bc16[:, j:j + 1],
                               in_=rowb_h[0:1, j:j + 1].broadcast_to((128, 1))), "px")
    b.gpsimd(_)


def emit_carry_gather(C, carry_list, off_list):
    nc, b = C.nc, C.block
    bufs = C.bufs
    gat, cmaskt = bufs["gat"], bufs["cmaskt"]
    k = len(carry_list)
    v0 = C.n["vc"]
    w = k * DC
    ag_in = bufs["ag_in8"] if k == 1 else bufs["ag_in16"]
    ag_out = bufs["ag_out8"] if k == 1 else bufs["ag_out16"]

    def _(g):
        C.w(g, "vc", v0)
        C.fence(g, "px")
        for j, car in enumerate(carry_list):
            C.dinc(g.dma_start(out=ag_in[:, j * DC:(j + 1) * DC], in_=car[:]), "px")
        C.fence(g, "px")
        C.inc(g.collective_compute(
            "AllGather", ALU.bypass, replica_groups=[list(range(NCORES))],
            ins=[ag_in[:, 0:w].opt()], outs=[ag_out[:, 0:w].opt()]), "cc")
        C.w(g, "cc", C.n["cc"])
        C.dinc(g.dma_start(out=gat[:, :, 0:w],
                           in_=ag_out[:, 0:w].rearrange("(c p) n -> p c n", p=128)), "px")
    b.gpsimd(_)
    sp1 = C.n["px"]

    def _(v):
        C.w(v, "px", sp1)
        C.inc(v.tensor_tensor(out=gat[:, :, 0:w], in0=gat[:, :, 0:w],
                              in1=cmaskt[:, 0:NCORES].rearrange("p (c o) -> p c o", o=1)
                              .broadcast_to((128, NCORES, w)),
                              op=ALU.mult), "vc")
        C.w(v, "vc", C.n["vc"])
        for j, off in enumerate(off_list):
            C.inc(v.tensor_reduce(out=off[:],
                                  in_=gat[:, :, j * DC:(j + 1) * DC].rearrange("p c n -> p n c"),
                                  axis=AX, op=ALU.add), "vc")
    b.vector(_)
    full_barrier(C)


def emit_ln0_stats(C, xsrc, with_minmax, mu_dst, r_dst):
    nc, b = C.nc, C.block
    bufs = C.bufs
    s1c, s2c, mxc, mnc = bufs["s1c"], bufs["s2c"], bufs["mxc"], bufs["mnc"]
    junkb = bufs["junkb"]
    t0, t1, t2 = bufs["t0_c"], bufs["t1_c"], bufs["t2_c"]
    xa = bufs["xstats"]

    stx = C.stream("l")
    v0 = C.n["vc"]
    s0 = C.n["sc"]

    def _(sync):
        for i in range(NT):
            if i >= 2:
                C.w(sync, "vc", v0 + (i - 1))
                C.w(sync, "sc", s0 + (i - 1))
            stx.issue(sync.dma_start(out=xa[:, i % 2, :], in_=xsrc[i, :, :]), i)
    b.sync(_)

    def _(v):
        for i in range(NT):
            stx.wait(v, i)
            xt = xa[:, i % 2, :]
            v.tensor_scalar(out=junkb[:, 0:D], in0=xt, scalar1=1.0, scalar2=None,
                            op0=ALU.mult, op1=ALU.add, accum_out=s1c[:, i:i + 1])
            if with_minmax:
                v.tensor_scalar(out=junkb[:, 0:D], in0=xt, scalar1=1.0, scalar2=None,
                                op0=ALU.mult, op1=ALU.max, accum_out=mxc[:, i:i + 1])
                ins = v.tensor_scalar(out=junkb[:, 0:D], in0=xt, scalar1=-1.0,
                                      scalar2=None, op0=ALU.mult, op1=ALU.max,
                                      accum_out=mnc[:, i:i + 1])
            else:
                ins = v.tensor_copy(junkb[:, 0:1], xt[:, 0:1])
            C.inc(ins, "vc")
    b.vector(_)

    def _(s):
        for i in range(NT):
            stx.wait(s, i)
            C.inc(s.activation(out=junkb[:, 0:D], in_=xa[:, i % 2, :], func=AF.Square,
                               accum_out=s2c[:, i:i + 1]), "sc")
    b.scalar(_)

    v1, s1 = C.n["vc"], C.n["sc"]

    def _(v):
        C.w(v, "vc", v1)
        C.w(v, "sc", s1)
        v.tensor_scalar(out=mu_dst[:], in0=s1c[:], scalar1=1.0 / D, scalar2=None, op0=ALU.mult)
        C.inc(v.tensor_scalar(out=t0[:], in0=s2c[:], scalar1=1.0 / D, scalar2=None,
                              op0=ALU.mult), "vc")
        C.w(v, "vc", C.n["vc"])
        C.inc(v.tensor_mul(t1[:], mu_dst[:], mu_dst[:]), "vc")
        C.w(v, "vc", C.n["vc"])
        C.inc(v.tensor_sub(t0[:], t0[:], t1[:]), "vc")
        C.w(v, "vc", C.n["vc"])
        C.inc(v.tensor_scalar(out=t0[:], in0=t0[:], scalar1=LN_EPS, scalar2=None,
                              op0=ALU.add), "vc")
    b.vector(_)
    v2 = C.n["vc"]

    def _(s):
        C.w(s, "vc", v2)
        C.inc(s.sqrt(t1[:], t0[:]), "sc")
    b.scalar(_)
    s2 = C.n["sc"]

    def _(v):
        C.w(v, "sc", s2)
        C.inc(v.reciprocal(t2[:], t1[:]), "vc")
        C.w(v, "vc", C.n["vc"])
        C.inc(v.tensor_mul(t1[:], t2[:], t2[:]), "vc")
        C.w(v, "vc", C.n["vc"])
        C.inc(v.tensor_mul(t1[:], t1[:], t0[:]), "vc")
        C.w(v, "vc", C.n["vc"])
        C.inc(v.tensor_scalar(out=t1[:], in0=t1[:], scalar1=-0.5, scalar2=1.5,
                              op0=ALU.mult, op1=ALU.add), "vc")
        C.w(v, "vc", C.n["vc"])
        C.inc(v.tensor_mul(r_dst[:], t2[:], t1[:]), "vc")
    b.vector(_)



def emit_prologue(C, io):
    nc, b = C.nc, C.block
    bufs = C.bufs
    identb = bufs["identb"]
    ones_row, ones_colb = bufs["ones_row"], bufs["ones_colb"]
    bom_row, b2_row = bufs["bom_row"], bufs["b2_row"]
    const_bf, colsum_bf, const_f = bufs["const_bf"], bufs["colsum_bf"], bufs["const_f"]
    gamma_c, beta_c, exp_ls, cmaskt = bufs["gamma_c"], bufs["beta_c"], bufs["exp_ls"], bufs["cmaskt"]
    ps, psT2 = bufs["ps"], bufs["psT2"]

    with contextlib.ExitStack() as pes:
        PE = pes.enter_context
        wbuf = PE(nc.sbuf_tensor("p_wbuf", [128, 2, I3], F32))
        ls_raw = PE(nc.sbuf_tensor("p_lsraw", [128, DC], F32))
        b1_rowf = PE(nc.sbuf_tensor("p_b1row", [1, D2], F32))
        bmax = PE(nc.sbuf_tensor("p_bmax", [128, 2, 96], F32))
        f7 = PE(nc.sbuf_tensor("p_f7", [128, 2, 96], F32))
        smul = PE(nc.sbuf_tensor("p_smul", [128, 2, 96], F32))
        qi = PE(nc.sbuf_tensor("p_qi", [128, I3], I32))
        wq = PE(nc.sbuf_tensor("p_wq", [128, 2, I3], BF16))
        wtr = PE(nc.sbuf_tensor("p_wtr", [128, 2, 128], BF16))
        w1chunk = PE(nc.sbuf_tensor("p_w1c", [128, 2, D2], F32))
        w1bf = PE(nc.sbuf_tensor("p_w1bf", [128, 2, D2], BF16))

        def _(sync):
            for dst, src in ((identb[:], io["identb_in"]), (bom_row[:], io["bom_in"]),
                             (b1_rowf[:], io["b1_in"]), (b2_row[:], io["b2_in"]),
                             (gamma_c[:], io["gamma_in"]), (beta_c[:], io["beta_in"]),
                             (ls_raw[:], io["ls_in"])):
                C.dinc(sync.dma_start(out=dst, in_=src), "lx")
            C.dinc(sync.dma_start(out=cmaskt[:, 0:NCORES],
                                  in_=io["cmask_in"][0:1, :].broadcast_to((128, NCORES))), "lx")
            C.fence(sync, "lx")
        b.sync(_)
        ld_consts = C.n["lx"]

        def _(v):
            v.memset(ones_row[:], 1.0)
            v.memset(bufs["colk"][:], 0.0)
            v.memset(bufs["colv"][:], 0.0)
            C.inc(v.memset(ones_colb[:], 1.0), "vc")
        b.vector(_)



        def _(s):
            C.w(s, "lx", ld_consts)
            C.inc(s.activation(out=exp_ls[:], in_=ls_raw[:], func=AF.Exp), "sc")
        b.scalar(_)

        jobs = [(io["wup_raw"], io["wupT_h"], IC, D),
                (io["wgate_raw"], io["wgT_h"], IC, D),
                (io["wdown_raw"], io["wdT_h"], DC, I3)]
        for raw, dstT, nrc, ncols in jobs:
            nb = ncols // 32
            ncc = ncols // 128
            raw_t = raw.rearrange("(n p) c -> n p c", p=128)
            dst_t = dstT.rearrange("(n p) c -> n p c", p=128)
            stw = C.stream("l")
            stp = C.stream("p")
            v0 = C.n["vc"]
            t0n = C.n["tc"]
            s0 = C.n["sc"]

            def _(sync, raw_t=raw_t, nrc=nrc, ncols=ncols, v0=v0, stw=stw):
                for i in range(nrc):
                    if i >= 2:
                        C.w(sync, "vc", v0 + 5 * (i - 1))
                    stw.issue(sync.dma_start(out=wbuf[:, i % 2, 0:ncols], in_=raw_t[i, :, :]), i)
            b.sync(_)

            def _(v, nrc=nrc, ncols=ncols, nb=nb, t0n=t0n, ncc=ncc, stw=stw):
                for i in range(nrc):
                    sl = i % 2
                    stw.wait(v, i)
                    if i >= 2:
                        C.w(v, "tc", t0n + (i - 1) * ncc)
                    C.inc(v.tensor_reduce(out=bmax[:, sl, 0:nb],
                                          in_=wbuf[:, sl, 0:ncols]
                                          .rearrange("p (b k) -> p b k", k=32),
                                          axis=AX, op=ALU.max,
                                          apply_absolute_value=True), "vc")
                    C.w(v, "vc", C.n["vc"])
                    C.inc(v.tensor_scalar(out=bmax[:, sl, 0:nb], in0=bmax[:, sl, 0:nb],
                                           scalar1=7e-12, scalar2=None, op0=ALU.max), "vc")
                    C.w(v, "vc", C.n["vc"])
                    C.inc(v.reciprocal(f7[:, sl, 0:nb], bmax[:, sl, 0:nb]), "vc")
                    C.w(v, "vc", C.n["vc"])
                    v.tensor_scalar(out=f7[:, sl, 0:nb], in0=f7[:, sl, 0:nb],
                                    scalar1=7.0, scalar2=None, op0=ALU.mult)
                    C.inc(v.tensor_scalar(out=smul[:, sl, 0:nb], in0=bmax[:, sl, 0:nb],
                                          scalar1=1.0 / 7.0, scalar2=None,
                                          op0=ALU.mult), "vc")
                    C.w(v, "vc", C.n["vc"])
                    v.tensor_tensor(out=qi[:, 0:ncols].rearrange("p (b k) -> p b k", k=32),
                                    in0=wbuf[:, sl, 0:ncols].rearrange("p (b k) -> p b k", k=32),
                                    in1=f7[:, sl, 0:nb].rearrange("p (b o) -> p b o", o=1)
                                    .broadcast_to((128, nb, 32)),
                                    op=ALU.mult)
                    C.inc(v.tensor_tensor(
                        out=wq[:, sl, 0:ncols].rearrange("p (b k) -> p b k", k=32),
                        in0=qi[:, 0:ncols].rearrange("p (b k) -> p b k", k=32),
                        in1=smul[:, sl, 0:nb].rearrange("p (b o) -> p b o", o=1)
                        .broadcast_to((128, nb, 32)),
                        op=ALU.mult), "vc")
            b.vector(_)

            def _(t, nrc=nrc, ncc=ncc, v0=v0, s0=s0):
                for i in range(nrc):
                    C.w(t, "vc", v0 + 5 * (i + 1))
                    for cc_ in range(ncc):
                        k = i * ncc + cc_
                        if k >= 2:
                            C.w(t, "sc", s0 + k - 1)
                        C.inc(t.transpose(psT2[k % 2][:, 0:128],
                                          wq[:, i % 2, cc_ * 128:(cc_ + 1) * 128],
                                          identb[:]), "tc")
            b.tensor(_)

            def _(s, nrc=nrc, ncc=ncc, t0n=t0n, stp=stp):
                for k in range(nrc * ncc):
                    C.w(s, "tc", t0n + k + 1)
                    if k >= 2:
                        stp.wait(s, k - 2)
                    C.inc(s.activation(out=wtr[:, k % 2, :], in_=psT2[k % 2][:, 0:128],
                                       func=AF.Identity), "sc")
            b.scalar(_)

            def _(g, nrc=nrc, ncc=ncc, s0=s0, dst_t=dst_t, stp=stp):
                for k in range(nrc * ncc):
                    i, cc_ = divmod(k, ncc)
                    C.w(g, "sc", s0 + k + 1)
                    stp.issue(g.dma_start(out=dst_t[cc_, :, i * 128:(i + 1) * 128],
                                          in_=wtr[:, k % 2, :]), k)
            b.gpsimd(_)
            full_barrier(C)

        for src_in, dst_h, nch, ncols in ((io["womT_in"], io["womT_h"], DC, D),
                                          (io["w2T_in"], io["w2T_h"], EC2, D)):
            src_t = src_in.rearrange("(n p) c -> n p c", p=128)
            dst_t = dst_h.rearrange("(n p) c -> n p c", p=128)
            stw = C.stream("l")
            stp = C.stream("p")
            v0 = C.n["vc"]

            def _(sync, src_t=src_t, nch=nch, ncols=ncols, v0=v0, stw=stw):
                for i in range(nch):
                    if i >= 2:
                        C.w(sync, "vc", v0 + (i - 1))
                    stw.issue(sync.dma_start(out=wbuf[:, i % 2, 0:ncols], in_=src_t[i, :, :]), i)
            b.sync(_)

            t_start = C.n["tc"]

            def _(v, nch=nch, ncols=ncols, t_start=t_start, stw=stw, stp=stp):
                for i in range(nch):
                    stw.wait(v, i)
                    if i < 2:
                        C.w(v, "tc", t_start)
                    if i >= 2:
                        stp.wait(v, i - 2)
                    C.inc(v.tensor_copy(wq[:, i % 2, 0:ncols], wbuf[:, i % 2, 0:ncols]), "vc")
            b.vector(_)

            def _(g, nch=nch, ncols=ncols, v0=v0, dst_t=dst_t, stp=stp):
                for i in range(nch):
                    C.w(g, "vc", v0 + i + 1)
                    stp.issue(g.dma_start(out=dst_t[i, :, :], in_=wq[:, i % 2, 0:ncols]), i)
            b.gpsimd(_)
            full_barrier(C)

        w1T_t = io["w1T_in"].rearrange("(n p) c -> n p c", p=128)
        w1g_t = io["w1g_h"].rearrange("(n p) c -> n p c", p=128)
        stw1 = C.stream("l")
        stp1 = C.stream("p")
        v0 = C.n["vc"]
        t0n = C.n["tc"]

        def _(sync, v0=v0, t0n=t0n, stw1=stw1):
            for i in range(FC):
                if i >= 2:
                    C.w(sync, "vc", v0 + (i - 1))
                    C.w(sync, "tc", t0n + 4 * (i - 1))
                stw1.issue(sync.dma_start(out=w1chunk[:, i % 2, :], in_=w1T_t[i, :, :]), i)
        b.sync(_)

        def _(v, stw1=stw1, stp1=stp1):
            for i in range(FC):
                stw1.wait(v, i)
                if i >= 2:
                    stp1.wait(v, i - 2)
                C.inc(v.tensor_scalar(out=w1bf[:, i % 2, :], in0=w1chunk[:, i % 2, :],
                                      scalar1=gamma_c[:, i:i + 1], scalar2=None,
                                      op0=ALU.mult), "vc")
        b.vector(_)

        def _(g, v0=v0, stp1=stp1):
            for i in range(FC):
                C.w(g, "vc", v0 + i + 1)
                stp1.issue(g.dma_start(out=w1g_t[i, :, :], in_=w1bf[:, i % 2, :]), i)
        b.gpsimd(_)

        def _(t, stw1=stw1):
            for i in range(FC):
                stw1.wait(t, i)
                for j in range(4):
                    C.inc(t.matmul(ps[j][0:1, :], beta_c[:, i:i + 1],
                                   w1chunk[:, i % 2, j * 512:(j + 1) * 512],
                                   start=(i == 0), stop=(i == FC - 1)), "tc")
        b.tensor(_)
        t_beta = C.n["tc"]

        def _(v, t_beta=t_beta, stw1=stw1):
            C.w(v, "tc", t_beta)
            stw1.wait_all(v)
            for j in range(4):
                v.tensor_add(const_f[0:1, j * 512:(j + 1) * 512], ps[j][0:1, :],
                             b1_rowf[0:1, j * 512:(j + 1) * 512])
            C.inc(v.tensor_copy(const_bf[0:1, :], const_f[0:1, :]), "vc")
        b.vector(_)
        full_barrier(C)

        stc = C.stream("l")
        v0 = C.n["vc"]
        t0n = C.n["tc"]

        def _(sync, t0n=t0n, stc=stc, stp1=stp1):
            stp1.wait_all(sync)
            C.w(sync, "vc", C.n["vc"])
            for i in range(FC):
                if i >= 2:
                    C.w(sync, "tc", t0n + 4 * (i - 1))
                stc.issue(sync.dma_start(out=w1bf[:, i % 2, :], in_=w1g_t[i, :, :]), i)
        b.sync(_)

        def _(t, stc=stc):
            for i in range(FC):
                stc.wait(t, i)
                for j in range(4):
                    C.inc(t.matmul(ps[j][0:1, :], ones_colb[:, 0:1],
                                   w1bf[:, i % 2, j * 512:(j + 1) * 512],
                                   start=(i == 0), stop=(i == FC - 1)), "tc")
        b.tensor(_)

        def _(v):
            C.w(v, "tc", C.n["tc"])
            for j in range(4):
                C.inc(v.tensor_copy(colsum_bf[0:1, j * 512:(j + 1) * 512], ps[j][0:1, :]), "vc")
        b.vector(_)

        full_barrier(C)



def emit_ffn(C, call, xsrc, xdst, io):
    nc, b = C.nc, C.block
    bufs = C.bufs
    ps, psT2 = bufs["ps"], bufs["psT2"]
    identb = bufs["identb"]
    bc16, sc16 = bufs["bc16"], bufs["sc16"]
    mu_c, r_c, a_c, bq_c = bufs["mu_c"], bufs["r_c"], bufs["a_c"], bufs["bq_c"]
    mxc, mnc = bufs["mxc"], bufs["mnc"]
    t0, t1 = bufs["t0_c"], bufs["t1_c"]
    colv = bufs["colv"]
    sp_up, sp_gate, sp_gi, sp_sig, sp_p = (io["sp_up"], io["sp_gate"], io["sp_gi"],
                                           io["sp_sig"], io["sp_p"])

    emit_ln0_stats(C, xsrc, True, mu_c, r_c)

    def _(v):
        C.w(v, "vc", C.n["vc"])
        v.tensor_sub(t0[:], mxc[:], mu_c[:])
        C.inc(v.tensor_add(t1[:], mnc[:], mu_c[:]), "vc")
        C.w(v, "vc", C.n["vc"])
        C.inc(v.tensor_max(t0[:], t0[:], t1[:]), "vc")
        C.w(v, "vc", C.n["vc"])
        C.inc(v.tensor_mul(t0[:], t0[:], r_c[:]), "vc")
        C.w(v, "vc", C.n["vc"])
        C.inc(v.tensor_reduce(out=colv[:, 0:1], in_=t0[:], axis=AX, op=ALU.max), "vc")
    b.vector(_)

    emit_allreduce_max(C, colv, 1)
    sp_bc = C.n["px"]

    def _(v):
        C.w(v, "px", sp_bc)
        C.inc(v.tensor_scalar(out=sc16[:, 0:1], in0=bc16[:, 0:1], scalar1=1.0 / QMAX,
                              scalar2=1e-12, op0=ALU.mult, op1=ALU.max), "vc")
        C.w(v, "vc", C.n["vc"])
        C.inc(v.reciprocal(sc16[:, 1:2], sc16[:, 0:1]), "vc")
        C.w(v, "vc", C.n["vc"])
        C.inc(v.tensor_scalar(out=a_c[:], in0=r_c[:], scalar1=sc16[:, 1:2], scalar2=None,
                              op0=ALU.mult), "vc")
        C.w(v, "vc", C.n["vc"])
        C.inc(v.tensor_mul(bq_c[:], mu_c[:], a_c[:]), "vc")
        C.w(v, "vc", C.n["vc"])
        C.inc(v.tensor_scalar(out=bq_c[:], in0=bq_c[:], scalar1=-1.0, scalar2=None,
                              op0=ALU.mult), "vc")
    b.vector(_)
    full_barrier(C)

    with contextlib.ExitStack() as pes:
        PE = pes.enter_context
        xa = bufs["xstats"]
        qtile = PE(nc.sbuf_tensor(f"f{call}_q", [128, 2, D], I16))
        xqtm = PE(nc.sbuf_tensor(f"f{call}_xqtm", [128, 2, D], BF16))
        xqT = PE(nc.sbuf_tensor(f"f{call}_xqT", [128, DC, TPC], BF16))
        wst = PE(nc.sbuf_tensor(f"f{call}_wst", [128, 2, DC, 128], BF16))
        ev = PE(nc.sbuf_tensor(f"f{call}_ev", [128, 8, GT], BF16))
        amx = PE(nc.sbuf_tensor(f"f{call}_amx", [128, 2, 2 * IC * NG], F32))
        junk = PE(nc.sbuf_tensor(f"f{call}_junk", [128, GT], BF16))

        stx = C.stream("l")
        s0 = C.n["sc"]
        v0 = C.n["vc"]
        t0n = C.n["tc"]

        def _(sync, v0=v0, stx=stx):
            for i in range(NT):
                if i >= 2:
                    C.w(sync, "vc", v0 + (i - 1))
                stx.issue(sync.dma_start(out=xa[:, i % 2, :], in_=xsrc[i, :, :]), i)
        b.sync(_)

        def _(s, v0=v0, t0n=t0n, stx=stx):
            for i in range(NT):
                stx.wait(s, i)
                C.inc(s.activation(out=qtile[:, i % 2, :], in_=xa[:, i % 2, :],
                                   func=AF.Identity,
                                   scale=a_c[:, i:i + 1], bias=bq_c[:, i:i + 1]), "sc")
                for dc in range(DC):
                    k = i * DC + dc
                    C.w(s, "tc", t0n + k + 1)
                    C.inc(s.activation(out=xqT[:, dc, i * 128:(i + 1) * 128],
                                       in_=psT2[k % 2][:, 0:128], func=AF.Identity), "sc")
        b.scalar(_)

        def _(v, s0=s0):
            for i in range(NT):
                C.w(v, "sc", s0 + i * (DC + 1) + 1)
                C.inc(v.tensor_scalar(out=xqtm[:, i % 2, :], in0=qtile[:, i % 2, :],
                                      scalar1=sc16[:, 0:1], scalar2=None, op0=ALU.mult), "vc")
        b.vector(_)

        def _(t, v0=v0, s0=s0):
            for i in range(NT):
                C.w(t, "vc", v0 + i + 1)
                for dc in range(DC):
                    k = i * DC + dc
                    if k >= 2:
                        ii, dd = divmod(k - 2, DC)
                        C.w(t, "sc", s0 + ii * (DC + 1) + 1 + dd + 1)
                    C.inc(t.transpose(psT2[k % 2][:, 0:128],
                                      xqtm[:, i % 2, dc * 128:(dc + 1) * 128], identb[:]), "tc")
        b.tensor(_)

        full_barrier(C)

        if SUBSTAGE == 2:
            def _(g2):
                for s in ("tc", "vc", "sc"):
                    C.w(g2, s, C.n[s])
                for j, buf in enumerate((bufs["s1c"], bufs["s2c"], mxc, mnc, mu_c, r_c, a_c, bq_c)):
                    C.dinc(g2.dma_start(out=xdst[0, :, j * 32:(j + 1) * 32], in_=buf[:]), "px")
                C.dinc(g2.dma_start(out=xdst[0, :, 256:264], in_=colv[:, 0:8]), "px")
                C.dinc(g2.dma_start(out=xdst[0, :, 264:280], in_=bc16[:, :]), "px")
                C.dinc(g2.dma_start(out=xdst[0, :, 280:296], in_=sc16[:, :]), "px")
            b.gpsimd(_)
            full_barrier(C)
            return

        if SUBSTAGE in (3, 4, 5):
            wh_d = {3: io["wupT_h"], 4: io["wgT_h"], 5: io["wdT_h"]}[SUBSTAGE]
            nrows = wh_d.shape[0]
            wcols = wh_d.shape[1]
            nch = nrows // 128
            ntc = wcols // D
            wtmp_b = PE(nc.sbuf_tensor("dbg_wb", [128, wcols], BF16))
            wtmp_f = PE(nc.sbuf_tensor("dbg_wf", [128, wcols], F32))
            wh_t = wh_d.rearrange("(n p) c -> n p c", p=128)
            lxw = C.n["lx"]
            vw = C.n["vc"]
            pxw = C.n["px"]

            def _(sync, lxw=lxw, pxw=pxw):
                for i in range(nch):
                    if i >= 1:
                        C.w(sync, "px", pxw + 16 * ntc * i)
                    C.dinc(sync.dma_start(out=wtmp_b[:], in_=wh_t[i, :, :]), "lx")
            b.sync(_)

            def _(v, lxw=lxw):
                for i in range(nch):
                    C.w(v, "lx", lxw + 16 * (i + 1))
                    C.inc(v.tensor_copy(wtmp_f[:], wtmp_b[:]), "vc")
            b.vector(_)

            def _(g2, vw=vw):
                for i in range(nch):
                    C.w(g2, "vc", vw + i + 1)
                    for j in range(ntc):
                        C.dinc(g2.dma_start(out=xdst[i * ntc + j, :, :],
                                            in_=wtmp_f[:, j * D:(j + 1) * D]), "px")
            b.gpsimd(_)
            full_barrier(C)
            return

        if SUBSTAGE == 1:
            dbg = PE(nc.sbuf_tensor(f"f{call}_dbg", [128, 2, D], F32))
            st_d = C.stream("l")
            v0d = C.n["vc"]

            def _(sync, st_d=st_d, v0d=v0d):
                for i in range(NT):
                    if i >= 2:
                        C.w(sync, "vc", v0d + (i - 1))
                    st_d.issue(sync.dma_start(out=xa[:, i % 2, :], in_=xsrc[i, :, :]), i)
            b.sync(_)

            def _(s, st_d=st_d):
                for i in range(NT):
                    st_d.wait(s, i)
                    C.inc(s.activation(out=qtile[:, i % 2, :], in_=xa[:, i % 2, :],
                                       func=AF.Identity,
                                       scale=a_c[:, i:i + 1], bias=bq_c[:, i:i + 1]), "sc")
            b.scalar(_)
            s0d = C.n["sc"] - NT

            def _(v, s0d=s0d, v0d=v0d):
                for i in range(NT):
                    C.w(v, "sc", s0d + i + 1)
                    C.inc(v.tensor_scalar(out=dbg[:, i % 2, :], in0=qtile[:, i % 2, :],
                                          scalar1=sc16[:, 0:1], scalar2=None,
                                          op0=ALU.mult), "vc")
            b.vector(_)
            v1d = C.n["vc"] - NT

            def _(g2, v1d=v1d):
                for i in range(NT):
                    C.w(g2, "vc", v1d + i + 1)
                    C.dinc(g2.dma_start(out=xdst[i, :, :], in_=dbg[:, i % 2, :]), "px")
            b.gpsimd(_)
            full_barrier(C)
            return

        w_hbm = (io["wupT_h"], io["wgT_h"])
        w_sp = (sp_up, sp_gate)
        for quarter in range(NG // 2):
            groups = [2 * quarter, 2 * quarter + 1]
            for wi in range(2):
                wh = w_hbm[wi].rearrange("(c p) e -> p c e", p=128)
                stwm = C.stream("l2")
                stsp = C.stream("p")
                t0n = C.n["tc"]
                v0 = C.n["vc"]
                c10 = C.n["c1"]

                def _(sync, wh=wh, t0n=t0n, stwm=stwm):
                    for e in range(IC):
                        if e < 2:
                            C.w(sync, "tc", t0n)
                        else:
                            C.w(sync, "tc", t0n + (e - 1))
                        stwm.issue(sync.dma_start(out=wst[:, e % 2, :, :],
                                                  in_=wh[:, :, e * 128:(e + 1) * 128]), e)
                b.sync(_)

                def _(t, v0=v0, groups=groups, stwm=stwm):
                    for e in range(IC):
                        stwm.wait(t, e)
                        if e >= 2:
                            C.w(t, "vc", v0 + (e - 1) * 2)
                        for d in range(DC):
                            for gi in range(2):
                                ins = t.matmul(ps[2 * (e % 2) + gi][:],
                                               wst[:, e % 2, d, :],
                                               xqT[:, d, groups[gi] * GT:(groups[gi] + 1) * GT],
                                               start=(d == 0), stop=(d == DC - 1))
                        C.inc(ins, "tc")
                b.tensor(_)

                def _(v, t0n=t0n, wi=wi, groups=groups, stsp=stsp):
                    for e in range(IC):
                        C.w(v, "tc", t0n + e + 1)
                        if e >= 2:
                            stsp.wait(v, e - 2, ndma=2)
                        for gi in range(2):
                            g = groups[gi]
                            col = e * NG + g
                            ncol = IC * NG + col
                            v.tensor_copy(ev[:, 2 * (e % 2) + gi, :], ps[2 * (e % 2) + gi][:])
                            v.tensor_scalar(out=junk[:], in0=ev[:, 2 * (e % 2) + gi, :],
                                            scalar1=1.0, scalar2=None, op0=ALU.mult,
                                            op1=ALU.max, accum_out=amx[:, wi, col:col + 1])
                            C.inc(v.tensor_scalar(out=junk[:], in0=ev[:, 2 * (e % 2) + gi, :],
                                                  scalar1=-1.0, scalar2=None, op0=ALU.mult,
                                                  op1=ALU.max,
                                                  accum_out=amx[:, wi, ncol:ncol + 1]), "vc")
                b.vector(_)

                def _(g2, v0=v0, wi=wi, groups=groups, stsp=stsp):
                    for e in range(IC):
                        for gi in range(2):
                            C.w(g2, "vc", v0 + e * 2 + gi + 1)
                            stsp.issue(g2.dma_start(out=w_sp[wi][e, groups[gi], :, :],
                                                    in_=ev[:, 2 * (e % 2) + gi, :]), e)
                b.gpsimd(_)

        g_pe, g_po = C.n["pe"], C.n["po"]

        def _(v, g_pe=g_pe, g_po=g_po):
            C.w(v, "pe", g_pe)
            C.w(v, "po", g_po)
            C.w(v, "vc", C.n["vc"])
            C.inc(v.tensor_reduce(out=colv[:, 0:1], in_=amx[:, 0, :], axis=AX, op=ALU.max), "vc")
            C.inc(v.tensor_reduce(out=colv[:, 1:2], in_=amx[:, 1, :], axis=AX, op=ALU.max), "vc")
        b.vector(_)

        emit_allreduce_max(C, colv, 2)
        sp_bc = C.n["px"]

        def _(v, sp_bc=sp_bc):
            C.w(v, "px", sp_bc)
            C.inc(v.tensor_scalar(out=sc16[:, 2:3], in0=bc16[:, 0:1], scalar1=1.0 / QMAX,
                                  scalar2=1e-12, op0=ALU.mult, op1=ALU.max), "vc")
            C.w(v, "vc", C.n["vc"])
            v.reciprocal(sc16[:, 3:4], sc16[:, 2:3])
            C.inc(v.tensor_scalar(out=sc16[:, 4:5], in0=bc16[:, 1:2], scalar1=1.0 / QMAX,
                                  scalar2=1e-12, op0=ALU.mult, op1=ALU.max), "vc")
            C.w(v, "vc", C.n["vc"])
            C.inc(v.reciprocal(sc16[:, 5:6], sc16[:, 4:5]), "vc")
        b.vector(_)
        full_barrier(C)

        if SUBSTAGE == 10:
            dxf = PE(nc.sbuf_tensor("dbg_xf", [128, 2, 1024], F32))
            vd = C.n["vc"]
            pxd = C.n["px"]

            def _(v, pxd=pxd):
                for k in range(32):
                    d, q = divmod(k, 4)
                    if k >= 2:
                        C.w(v, "px", pxd + 16 * (k - 1))
                    C.inc(v.tensor_copy(dxf[:, k % 2, :],
                                        xqT[:, d, q * 1024:(q + 1) * 1024]), "vc")
            b.vector(_)

            def _(g2, vd=vd):
                for k in range(32):
                    C.w(g2, "vc", vd + k + 1)
                    C.dinc(g2.dma_start(out=xdst[k, :, :], in_=dxf[:, k % 2, :]), "px")
            b.gpsimd(_)
            full_barrier(C)
            return

        if SUBSTAGE in (7, 8, 9):
            e_base = {7: 0, 8: 8, 9: 16}[SUBSTAGE]
            dzb = PE(nc.sbuf_tensor("dbg_zb", [128, 2, GT], BF16))
            dzf = PE(nc.sbuf_tensor("dbg_zf", [128, 2, GT], F32))
            lxd = C.n["lx"]
            vd = C.n["vc"]
            pxd = C.n["px"]

            def _(sync, lxd=lxd, pxd=pxd, e_base=e_base):
                for k in range(64):
                    e, g = divmod(k, 8)
                    if k >= 2:
                        C.w(sync, "px", pxd + 16 * (k - 1))
                    C.dinc(sync.dma_start(out=dzb[:, k % 2, :],
                                          in_=sp_up[e_base + e, g, :, :]), "lx")
            b.sync(_)

            def _(v, lxd=lxd):
                for k in range(64):
                    C.w(v, "lx", lxd + 16 * (k + 1))
                    C.inc(v.tensor_copy(dzf[:, k % 2, :], dzb[:, k % 2, :]), "vc")
            b.vector(_)

            def _(g2, vd=vd):
                for k in range(64):
                    C.w(g2, "vc", vd + k + 1)
                    ti, half = divmod(k, 2)
                    C.dinc(g2.dma_start(out=xdst[ti, :, half * GT:(half + 1) * GT],
                                        in_=dzf[:, k % 2, :]), "px")
            b.gpsimd(_)
            full_barrier(C)
            return

        if SUBSTAGE == 6:
            def _(g2):
                for s in ("vc",):
                    C.w(g2, s, C.n[s])
                C.dinc(g2.dma_start(out=xdst[0, :, 0:384], in_=amx[:, 0, :]), "px")
                C.dinc(g2.dma_start(out=xdst[0, :, 384:768], in_=amx[:, 1, :]), "px")
                C.dinc(g2.dma_start(out=xdst[0, :, 768:776], in_=colv[:, 0:8]), "px")
                C.dinc(g2.dma_start(out=xdst[0, :, 776:792], in_=bc16[:, :]), "px")
                C.dinc(g2.dma_start(out=xdst[0, :, 792:808], in_=sc16[:, :]), "px")
            b.gpsimd(_)
            full_barrier(C)
            return

    ntile = IC * NG
    with contextlib.ExitStack() as pes:
        PE = pes.enter_context
        gtile = PE(nc.sbuf_tensor(f"f{call}_gt", [128, 2, GT], BF16))
        gi_t = PE(nc.sbuf_tensor(f"f{call}_gi", [128, 2, GT], I16))
        sig0 = PE(nc.sbuf_tensor(f"f{call}_sg0", [128, 2, GT], F32))
        sigu = PE(nc.sbuf_tensor(f"f{call}_sgu", [128, 2, GT], U16))
        tp = PE(nc.sbuf_tensor(f"f{call}_tp", [128, 2, GT], F32))
        tmaxc = PE(nc.sbuf_tensor(f"f{call}_tmx", [128, ntile], F32))
        uptile = PE(nc.sbuf_tensor(f"f{call}_up", [128, 2, GT], BF16))
        g2t = PE(nc.sbuf_tensor(f"f{call}_g2", [128, 2, GT], I16))
        uit = PE(nc.sbuf_tensor(f"f{call}_ui", [128, 2, GT], I16))
        ptile = PE(nc.sbuf_tensor(f"f{call}_p", [128, 2, GT], F32))

        stg = C.stream("l")
        stspc = C.stream("p")
        v0 = C.n["vc"]
        s0 = C.n["sc"]

        def _(sync, v0=v0, stg=stg):
            for k in range(ntile):
                e, g = divmod(k, NG)
                if k >= 2:
                    C.w(sync, "vc", v0 + (k - 1) * 2)
                stg.issue(sync.dma_start(out=gtile[:, k % 2, :], in_=sp_gate[e, g, :, :]), k)
        b.sync(_)

        def _(v, s0=s0, stg=stg, stspc=stspc):
            for k in range(ntile):
                sl = k % 2
                stg.wait(v, k)
                if k >= 2:
                    stspc.wait(v, k - 2, ndma=2)
                C.inc(v.tensor_scalar(out=gi_t[:, sl, :], in0=gtile[:, sl, :],
                                      scalar1=sc16[:, 5:6], scalar2=None, op0=ALU.mult), "vc")
                C.w(v, "sc", s0 + k + 1)
                v.tensor_scalar(out=sigu[:, sl, :], in0=sig0[:, sl, :], scalar1=65536.0,
                                scalar2=65535.0, op0=ALU.mult, op1=ALU.min)
                v.tensor_tensor(out=tp[:, sl, :], in0=gi_t[:, sl, :], in1=sigu[:, sl, :],
                                op=ALU.mult)
                C.inc(v.tensor_reduce(out=tmaxc[:, k:k + 1], in_=tp[:, sl, :], axis=AX,
                                      op=ALU.max, apply_absolute_value=True), "vc")
        b.vector(_)

        def _(s, v0=v0):
            for k in range(ntile):
                C.w(s, "vc", v0 + 2 * k + 1)
                C.inc(s.activation(out=sig0[:, k % 2, :], in_=gi_t[:, k % 2, :],
                                   func=AF.Sigmoid, scale=sc16[:, 4:5]), "sc")
        b.scalar(_)

        def _(g2, v0=v0, stspc=stspc):
            for k in range(ntile):
                e, g = divmod(k, NG)
                C.w(g2, "vc", v0 + 2 * (k + 1))
                stspc.issue(g2.dma_start(out=sp_gi[e, g, :, :], in_=gi_t[:, k % 2, :]), k)
                stspc.issue(g2.dma_start(out=sp_sig[e, g, :, :], in_=sigu[:, k % 2, :]), k)
        b.gpsimd(_)

        def _(v, stspc=stspc):
            stspc.wait_all(v)
            C.w(v, "vc", C.n["vc"])
            C.inc(v.tensor_reduce(out=colv[:, 0:1], in_=tmaxc[:], axis=AX, op=ALU.max), "vc")
        b.vector(_)
        emit_allreduce_max(C, colv, 1)
        sp_bc = C.n["px"]

        def _(v, sp_bc=sp_bc):
            C.w(v, "px", sp_bc)
            C.inc(v.tensor_scalar(out=sc16[:, 6:7], in0=sc16[:, 4:5],
                                  scalar1=float(2.0 ** -16),
                                  scalar2=None, op0=ALU.mult), "vc")
            C.w(v, "vc", C.n["vc"])
            C.inc(v.tensor_mul(sc16[:, 7:8], bc16[:, 0:1], sc16[:, 6:7]), "vc")
            C.w(v, "vc", C.n["vc"])
            C.inc(v.tensor_scalar(out=sc16[:, 7:8], in0=sc16[:, 7:8], scalar1=1.0 / QMAX,
                                  scalar2=1e-12, op0=ALU.mult, op1=ALU.max), "vc")
            C.w(v, "vc", C.n["vc"])
            C.inc(v.reciprocal(sc16[:, 8:9], sc16[:, 7:8]), "vc")
            C.w(v, "vc", C.n["vc"])
            C.inc(v.tensor_mul(sc16[:, 8:9], sc16[:, 8:9], sc16[:, 6:7]), "vc")
        b.vector(_)
        full_barrier(C)

        std_ = C.stream("l")
        stspd = C.stream("p")
        v0 = C.n["vc"]

        def _(sync, v0=v0, std_=std_):
            for k in range(ntile):
                e, g = divmod(k, NG)
                if k >= 2:
                    C.w(sync, "vc", v0 + (k - 1))
                std_.issue(sync.dma_start(out=gi_t[:, k % 2, :], in_=sp_gi[e, g, :, :]), k)
                std_.issue(sync.dma_start(out=sigu[:, k % 2, :], in_=sp_sig[e, g, :, :]), k)
                std_.issue(sync.dma_start(out=uptile[:, k % 2, :], in_=sp_up[e, g, :, :]), k)
        b.sync(_)

        def _(v, std_=std_, stspd=stspd):
            for k in range(ntile):
                sl = k % 2
                std_.wait(v, k, ndma=3)
                if k >= 2:
                    stspd.wait(v, k - 2)
                v.tensor_tensor(out=tp[:, sl, :], in0=gi_t[:, sl, :], in1=sigu[:, sl, :],
                                op=ALU.mult)
                v.tensor_scalar(out=g2t[:, sl, :], in0=tp[:, sl, :], scalar1=sc16[:, 8:9],
                                scalar2=None, op0=ALU.mult)
                v.tensor_scalar(out=uit[:, sl, :], in0=uptile[:, sl, :], scalar1=sc16[:, 3:4],
                                scalar2=None, op0=ALU.mult)
                v.tensor_tensor(out=ptile[:, sl, :], in0=g2t[:, sl, :], in1=uit[:, sl, :],
                                op=ALU.mult)
                C.inc(v.tensor_reduce(out=tmaxc[:, k:k + 1], in_=ptile[:, sl, :], axis=AX,
                                      op=ALU.max, apply_absolute_value=True), "vc")
        b.vector(_)

        def _(g2, v0=v0, stspd=stspd):
            for k in range(ntile):
                e, g = divmod(k, NG)
                C.w(g2, "vc", v0 + k + 1)
                stspd.issue(g2.dma_start(out=sp_p[e, g, :, :], in_=ptile[:, k % 2, :]), k)
        b.gpsimd(_)

        def _(v, stspd=stspd):
            stspd.wait_all(v)
            C.w(v, "vc", C.n["vc"])
            C.inc(v.tensor_reduce(out=colv[:, 0:1], in_=tmaxc[:], axis=AX, op=ALU.max), "vc")
        b.vector(_)
        emit_allreduce_max(C, colv, 1)
        sp_bc = C.n["px"]

        def _(v, sp_bc=sp_bc):
            C.w(v, "px", sp_bc)
            C.inc(v.tensor_mul(sc16[:, 9:10], sc16[:, 7:8], sc16[:, 2:3]), "vc")
            C.w(v, "vc", C.n["vc"])
            C.inc(v.tensor_mul(sc16[:, 10:11], sc16[:, 9:10], bc16[:, 0:1]), "vc")
            C.w(v, "vc", C.n["vc"])
            C.inc(v.tensor_scalar(out=sc16[:, 10:11], in0=sc16[:, 10:11], scalar1=1.0 / QMAX,
                                  scalar2=1e-12, op0=ALU.mult, op1=ALU.max), "vc")
            C.w(v, "vc", C.n["vc"])
            C.inc(v.reciprocal(sc16[:, 11:12], sc16[:, 10:11]), "vc")
            C.w(v, "vc", C.n["vc"])
            C.inc(v.tensor_mul(sc16[:, 11:12], sc16[:, 11:12], sc16[:, 9:10]), "vc")
        b.vector(_)
        full_barrier(C)

        with contextlib.ExitStack() as pes2:
            PE2 = pes2.enter_context
            eptile = PE2(nc.sbuf_tensor(f"f{call}_ep", [128, 4, GT], F32))
            ogrp = PE2(nc.sbuf_tensor(f"f{call}_og", [128, 2, IC, GT], BF16))
            wdsb = PE2(nc.sbuf_tensor(f"f{call}_wd", [128, IC, D], BF16))
            oi_t = PE2(nc.sbuf_tensor(f"f{call}_oi", [128, 2, GT], I16))
            xrt = PE2(nc.sbuf_tensor(f"f{call}_xr", [128, 4, D], F32))
            xnew = PE2(nc.sbuf_tensor(f"f{call}_xn", [128, 4, D], F32))
            wdh = io["wdT_h"].rearrange("(c p) e -> c p e", p=128)

            def _(sync):
                for e in range(IC):
                    C.dinc(sync.dma_start(out=wdsb[:, e, :], in_=wdh[e, :, :]), "lx")
                C.fence(sync, "lx")
            b.sync(_)
            ld_wd = C.n["lx"]

            stpa = C.stream("l")
            stpb = C.stream("w")
            v0 = C.n["vc"]
            c1_0 = C.n["c1"]
            px0 = C.n["px"]
            lx0 = C.n["lx"]

            vdone = {}
            S = {}
            cnt = v0
            for gi in range(NG + 1):
                if gi < NG:
                    for e in range(IC):
                        cnt += 2
                        vdone[gi * IC + e] = cnt
                if gi >= 1:
                    S[gi - 1] = cnt
                    cnt += 4

            def _(sync):
                for g in range(NG):
                    for e in range(IC):
                        k = g * IC + e
                        if k >= 4:
                            C.w(sync, "vc", vdone[k - 4])
                        (stpa, stpb)[k % 2].issue(
                            sync.dma_start(out=eptile[:, k % 4, :],
                                           in_=sp_p[e, g, :, :]), k // 2)
                    if g >= 1:
                        C.w(sync, "vc", S[g - 1] + 4)
                    for tt in range(4):
                        C.dinc(sync.dma_start(out=xrt[:, tt, :], in_=xsrc[g * 4 + tt, :, :]),
                               "lx")
            b.sync(_)

            def _(v):
                for gi in range(NG + 1):
                    if gi < NG:
                        for e in range(IC):
                            k = gi * IC + e
                            (stpa, stpb)[k % 2].wait(v, k // 2)
                            if e == 0 and gi >= 2:
                                C.w(v, "c1", c1_0 + 2 * IC * (gi - 1))
                            C.inc(v.tensor_scalar(out=oi_t[:, k % 2, :],
                                                  in0=eptile[:, k % 4, :],
                                                  scalar1=sc16[:, 11:12], scalar2=None,
                                                  op0=ALU.mult), "vc")
                            C.inc(v.tensor_scalar(out=ogrp[:, gi % 2, e, :],
                                                  in0=oi_t[:, k % 2, :],
                                                  scalar1=sc16[:, 10:11], scalar2=None,
                                                  op0=ALU.mult), "vc")
                    if gi >= 1:
                        g = gi - 1
                        if g >= 1:
                            C.w(v, "px", px0 + 64 * g)
                        C.w(v, "lx", lx0 + 64 * (g + 1))
                        for pair in range(2):
                            C.w(v, "c1", c1_0 + 2 * IC * g + IC * (pair + 1))
                            for t2 in range(2):
                                tt = pair * 2 + t2
                                v.scalar_tensor_tensor(out=xnew[:, tt, 0:GT],
                                                       in0=ps[2 * t2][:], scalar=0.5,
                                                       in1=xrt[:, tt, 0:GT],
                                                       op0=ALU.mult, op1=ALU.add)
                                C.inc(v.scalar_tensor_tensor(out=xnew[:, tt, GT:D],
                                                             in0=ps[2 * t2 + 1][:],
                                                             scalar=0.5,
                                                             in1=xrt[:, tt, GT:D],
                                                             op0=ALU.mult, op1=ALU.add),
                                      "vc")
            b.vector(_)

            def _(t):
                C.w(t, "lx", ld_wd)
                for g in range(NG):
                    C.w(t, "vc", vdone[g * IC + IC - 1])
                    if g >= 1:
                        C.w(t, "vc", S[g - 1] + 4)
                    for pair in range(2):
                        if pair == 1:
                            C.w(t, "vc", S[g] + 2)
                        for e in range(IC):
                            for t2 in range(2):
                                tt = pair * 2 + t2
                                t.matmul(ps[2 * t2][:],
                                         ogrp[:, g % 2, e, tt * 128:(tt + 1) * 128],
                                         wdsb[:, e, 0:GT],
                                         start=(e == 0), stop=(e == IC - 1))
                                ins = t.matmul(ps[2 * t2 + 1][:],
                                               ogrp[:, g % 2, e, tt * 128:(tt + 1) * 128],
                                               wdsb[:, e, GT:D],
                                               start=(e == 0), stop=(e == IC - 1))
                            C.inc(ins, "c1")
            b.tensor(_)

            def _(g2):
                for g in range(NG):
                    for tt in range(4):
                        C.w(g2, "vc", S[g] + tt + 1)
                        C.dinc(g2.dma_start(out=xdst[g * 4 + tt, :, :],
                                            in_=xnew[:, tt, :]), "px")
            b.gpsimd(_)
            full_barrier(C)

    full_barrier(C)



def emit_attn(C, xsrc, xdst, io):
    nc, b = C.nc, C.block
    bufs = C.bufs
    ps, psT2, psS = bufs["ps"], bufs["psT2"], bufs["psS"]
    identb = bufs["identb"]
    mu0_c, r0_c = bufs["mu0_c"], bufs["r0_c"]
    b0_c, r1_c = bufs["b0_c"], bufs["r1_c"]
    exp_ls = bufs["exp_ls"]
    bom_row, b2_row = bufs["bom_row"], bufs["b2_row"]
    ones_row, ones_colb = bufs["ones_row"], bufs["ones_colb"]
    const_bf, colsum_bf = bufs["const_bf"], bufs["colsum_bf"]
    carry, carry_r, carry_i = bufs["carry"], bufs["carry_r"], bufs["carry_i"]
    off_phi, off_r, off_i = bufs["off_phi"], bufs["off_r"], bufs["off_i"]
    colv = bufs["colv"]
    sp_phi, sp_xln = io["sp_phi"], io["sp_xln"]
    sp_cos, sp_sin = io["sp_cos"], io["sp_sin"]
    sp_cmr, sp_cmi = io["sp_cmr"], io["sp_cmi"]
    sp_Sr, sp_Si = io["sp_Sr"], io["sp_Si"]

    emit_ln0_stats(C, xsrc, False, mu0_c, r0_c)

    def _(v):
        C.w(v, "vc", C.n["vc"])
        C.inc(v.tensor_mul(b0_c[:], mu0_c[:], r0_c[:]), "vc")
        C.w(v, "vc", C.n["vc"])
        v.tensor_scalar(out=b0_c[:], in0=b0_c[:], scalar1=-1.0, scalar2=None, op0=ALU.mult)
        v.tensor_scalar(out=r1_c[:], in0=r0_c[:], scalar1=1.0, scalar2=None, op0=ALU.add)
        v.memset(carry[:], 0.0)
        v.memset(carry_r[:], 0.0)
        C.inc(v.memset(carry_i[:], 0.0), "vc")
    b.vector(_)
    full_barrier(C)

    with contextlib.ExitStack() as pes:
        PE = pes.enter_context
        xa = bufs["xstats"]
        xlntm = PE(nc.sbuf_tensor("a_xlntm", [128, 2, D], BF16))
        xlnT = PE(nc.sbuf_tensor("a_xlnT", [128, DC, GT], BF16))
        womsb = PE(nc.sbuf_tensor("a_wom", [128, DC, D], BF16))
        oma = PE(nc.sbuf_tensor("a_oma", [128, DC, GT], F32))
        invsqt = PE(nc.sbuf_tensor("a_invsq", [128, GT], F32))

        womt = io["womT_h"].rearrange("(c p) e -> c p e", p=128)

        def _(sync):
            for d in range(DC):
                C.dinc(sync.dma_start(out=womsb[:, d, :], in_=womt[d, :, :]), "lx")
            C.fence(sync, "lx")
        b.sync(_)
        ld_wom = C.n["lx"]

        for g in range(NG):
            stxa = C.stream("l")
            lx0 = C.n["lx"]
            s0 = C.n["sc"]
            v0 = C.n["vc"]
            t0n = C.n["tc"]

            def _(sync, g=g, v0=v0, stxa=stxa):
                C.dinc(sync.dma_start(out=invsqt[:],
                                      in_=io["invsq_in"][0:1, g * GT:(g + 1) * GT]
                                      .broadcast_to((128, GT))), "lx")
                for tt in range(4):
                    if tt >= 2:
                        C.w(sync, "vc", v0 + (tt - 1) * DC)
                    stxa.issue(sync.dma_start(out=xa[:, tt % 2, :], in_=xsrc[g * 4 + tt, :, :]), tt)
            b.sync(_)

            def _(s, g=g, stxa=stxa):
                for tt in range(4):
                    i = g * 4 + tt
                    stxa.wait(s, tt)
                    C.inc(s.activation(out=xlntm[:, tt % 2, :], in_=xa[:, tt % 2, :],
                                       func=AF.Identity,
                                       scale=r0_c[:, i:i + 1], bias=b0_c[:, i:i + 1]), "sc")
            b.scalar(_)

            def _(t, s0=s0, v0=v0):
                for tt in range(4):
                    C.w(t, "sc", s0 + tt + 1)
                    for dc in range(DC):
                        k = tt * DC + dc
                        if k >= 2:
                            C.w(t, "vc", v0 + k - 1)
                        C.inc(t.transpose(psT2[k % 2][:, 0:128],
                                          xlntm[:, tt % 2, dc * 128:(dc + 1) * 128],
                                          identb[:]), "tc")
            b.tensor(_)

            def _(v, t0n=t0n):
                for tt in range(4):
                    for dc in range(DC):
                        k = tt * DC + dc
                        C.w(v, "tc", t0n + k + 1)
                        C.inc(v.tensor_copy(xlnT[:, dc, tt * 128:(tt + 1) * 128],
                                            psT2[k % 2][:, 0:128]), "vc")
            b.vector(_)
            v_xlnT = C.n["vc"]
            t_tr = C.n["tc"]

            def _(g2, g=g, v_xlnT=v_xlnT):
                C.w(g2, "vc", v_xlnT)
                C.dinc(g2.dma_start(out=sp_xln[g, :, :, :], in_=xlnT[:]), "px")
            b.gpsimd(_)

            s_ev0 = C.n["sc"]

            def _(t, v_xlnT=v_xlnT, t_tr=t_tr, s_ev0=s_ev0):
                C.w(t, "vc", v_xlnT)
                C.w(t, "lx", ld_wom)
                for ec in range(DC):
                    bank = 4 + (ec % 2)
                    if ec >= 2:
                        C.w(t, "sc", s_ev0 + ec - 1)
                    for d in range(DC):
                        t.matmul(ps[bank][:], womsb[:, d, ec * 128:(ec + 1) * 128],
                                 xlnT[:, d, :], start=(d == 0), stop=False)
                    C.inc(t.matmul(ps[bank][:], bom_row[0:1, ec * 128:(ec + 1) * 128],
                                   ones_row[0:1, :], start=False, stop=True), "tc")
            b.tensor(_)
            t_om0 = t_tr

            def _(s, t_om0=t_om0):
                for ec in range(DC):
                    C.w(s, "tc", t_om0 + ec + 1)
                    C.inc(s.activation(out=oma[:, ec, :], in_=ps[4 + (ec % 2)][:],
                                       func=AF.Identity, scale=exp_ls[:, ec:ec + 1]), "sc")
            b.scalar(_)

            def _(v, g=g, lx0=lx0):
                C.w(v, "sc", C.n["sc"])
                C.w(v, "lx", lx0 + 16)
                v.tensor_tensor(out=oma[:], in0=oma[:],
                                in1=invsqt[:].rearrange("p (o t) -> p o t", o=1)
                                .broadcast_to((128, DC, GT)),
                                op=ALU.mult)
                for ec in range(DC):
                    ins_scan = v.tensor_tensor_scan(out=oma[:, ec, :], data0=oma[:, ec, :],
                                                    data1=oma[:, ec, :],
                                                    initial=carry[:, ec:ec + 1],
                                                    op0=ALU.add, op1=ALU.bypass)
                C.w(v, "vc", C.inc(ins_scan, "vc"))
                C.inc(v.tensor_copy(carry[:], oma[:, :, GT - 1]), "vc")
            b.vector(_)

            def _(g2, g=g):
                C.w(g2, "vc", C.n["vc"])
                C.dinc(g2.dma_start(out=sp_phi[g, :, :, :], in_=oma[:]), "px")
            b.gpsimd(_)
            full_barrier(C)

        emit_carry_gather(C, [carry], [off_phi])

        kint = PE(nc.sbuf_tensor("a_k", [128, DC, GT], I32))
        phi1 = PE(nc.sbuf_tensor("a_phi1", [128, DC, GT], F32))
        cosb = PE(nc.sbuf_tensor("a_cos", [128, DC, GT], BF16))
        sinb = PE(nc.sbuf_tensor("a_sin", [128, DC, GT], BF16))
        cmr = PE(nc.sbuf_tensor("a_cmr", [128, DC, GT], BF16))
        cmi = PE(nc.sbuf_tensor("a_cmi", [128, DC, GT], BF16))
        Srb = PE(nc.sbuf_tensor("a_Sr", [128, DC, GT], F32))
        Sib = PE(nc.sbuf_tensor("a_Si", [128, DC, GT], F32))

        for g in range(NG):
            lx0 = C.n["lx"]

            def _(sync, g=g):
                C.dinc(sync.dma_start(out=oma[:], in_=sp_phi[g, :, :, :]), "lx")
                C.dinc(sync.dma_start(out=xlnT[:], in_=sp_xln[g, :, :, :]), "lx")
            b.sync(_)

            def _(v, lx0=lx0):
                C.w(v, "lx", lx0 + 32)
                for ec in range(DC):
                    v.tensor_scalar(out=oma[:, ec, :], in0=oma[:, ec, :],
                                    scalar1=off_phi[:, ec:ec + 1], scalar2=None, op0=ALU.add)
                C.inc(v.tensor_copy(junk_one(C), oma[:, 0, 0:1]), "vc")
            b.vector(_)
            v_phi = C.n["vc"]

            def _(s, v_phi=v_phi):
                C.w(s, "vc", v_phi)
                C.inc(s.activation(out=kint[:], in_=oma[:], func=AF.Identity,
                                   scale=float(1.0 / TWO_PI)), "sc")
            b.scalar(_)
            s_k = C.n["sc"]

            def _(v, s_k=s_k):
                C.w(v, "sc", s_k)
                v.scalar_tensor_tensor(out=phi1[:], in0=kint[:], scalar=-RC1, in1=oma[:],
                                       op0=ALU.mult, op1=ALU.add)
                v.scalar_tensor_tensor(out=phi1[:], in0=kint[:], scalar=-RC2, in1=phi1[:],
                                       op0=ALU.mult, op1=ALU.add)
                v.tensor_scalar(out=Srb[:], in0=phi1[:], scalar1=PIF, scalar2=float(-TWO_PI),
                                op0=ALU.is_gt, op1=ALU.mult)
                v.tensor_add(phi1[:], phi1[:], Srb[:])
                v.tensor_scalar(out=Srb[:], in0=phi1[:], scalar1=-PIF, scalar2=float(TWO_PI),
                                op0=ALU.is_lt, op1=ALU.mult)
                v.tensor_add(phi1[:], phi1[:], Srb[:])
                C.inc(v.tensor_scalar(out=phi1[:], in0=phi1[:], scalar1=PIF,
                                      scalar2=-PIF, op0=ALU.min, op1=ALU.max), "vc")
            b.vector(_)
            v_ph1 = C.n["vc"]

            def _(s, v_ph1=v_ph1):
                C.w(s, "vc", v_ph1)
                C.inc(s.activation(out=sinb[:], in_=phi1[:], func=AF.Sin), "sc")
            b.scalar(_)
            s_sin = C.n["sc"]

            def _(v, s_sin=s_sin):
                C.w(v, "sc", s_sin)
                v.tensor_scalar(out=oma[:], in0=phi1[:], scalar1=PIH,
                                scalar2=float(-TWO_PI), op0=ALU.is_gt, op1=ALU.mult)
                C.inc(v.scalar_tensor_tensor(out=oma[:], in0=phi1[:], scalar=PIH,
                                             in1=oma[:], op0=ALU.add, op1=ALU.add), "vc")
            b.vector(_)
            v_psi = C.n["vc"]

            def _(s, v_psi=v_psi):
                C.w(s, "vc", v_psi)
                C.inc(s.activation(out=cosb[:], in_=oma[:], func=AF.Sin), "sc")
            b.scalar(_)
            s_cos = C.n["sc"]

            def _(v, s_cos=s_cos):
                C.w(v, "sc", s_cos)
                v.tensor_tensor(out=cmr[:], in0=xlnT[:], in1=cosb[:], op=ALU.mult)
                v.tensor_tensor(out=cmi[:], in0=xlnT[:], in1=sinb[:], op=ALU.mult)
                for ec in range(DC):
                    v.tensor_tensor_scan(out=Srb[:, ec, :], data0=cmr[:, ec, :],
                                         data1=cmr[:, ec, :], initial=carry_r[:, ec:ec + 1],
                                         op0=ALU.add, op1=ALU.bypass)
                    ins_scan2 = v.tensor_tensor_scan(out=Sib[:, ec, :], data0=cmi[:, ec, :],
                                                     data1=cmi[:, ec, :],
                                                     initial=carry_i[:, ec:ec + 1],
                                                     op0=ALU.add, op1=ALU.bypass)
                C.w(v, "vc", C.inc(ins_scan2, "vc"))
                v.tensor_copy(carry_r[:], Srb[:, :, GT - 1])
                C.inc(v.tensor_copy(carry_i[:], Sib[:, :, GT - 1]), "vc")
            b.vector(_)

            def _(g2, g=g):
                C.w(g2, "vc", C.n["vc"])
                C.dinc(g2.dma_start(out=sp_cos[g, :, :, :], in_=cosb[:]), "px")
                C.dinc(g2.dma_start(out=sp_sin[g, :, :, :], in_=sinb[:]), "px")
                C.dinc(g2.dma_start(out=sp_cmr[g, :, :, :], in_=cmr[:]), "px")
                C.dinc(g2.dma_start(out=sp_cmi[g, :, :, :], in_=cmi[:]), "px")
                C.dinc(g2.dma_start(out=sp_Sr[g, :, :, :], in_=Srb[:]), "px")
                C.dinc(g2.dma_start(out=sp_Si[g, :, :, :], in_=Sib[:]), "px")
            b.gpsimd(_)
            full_barrier(C)

        emit_carry_gather(C, [carry_r, carry_i], [off_r, off_i])

    with contextlib.ExitStack() as pes:
        PE = pes.enter_context
        ctxb = PE(nc.sbuf_tensor("a3_ctx", [128, FC, GT], BF16))
        Srb = PE(nc.sbuf_tensor("a3_Sr", [128, DC, GT], F32))
        Sib = PE(nc.sbuf_tensor("a3_Si", [128, DC, GT], F32))
        cosb = PE(nc.sbuf_tensor("a3_cos", [128, DC, GT], BF16))
        sinb = PE(nc.sbuf_tensor("a3_sin", [128, DC, GT], BF16))
        tmp1 = PE(nc.sbuf_tensor("a3_t1", [128, GT], BF16))
        tmp2 = PE(nc.sbuf_tensor("a3_t2", [128, GT], BF16))
        invpt = PE(nc.sbuf_tensor("a3_invp", [128, GT], F32))
        w1st = PE(nc.sbuf_tensor("a3_w1", [128, 4, GT], BF16))
        w2sb = PE(nc.sbuf_tensor("a3_w2", [128, EC2, D], BF16))
        sqc = PE(nc.sbuf_tensor("a3_sq", [128, 2, GT], BF16))
        hT = PE(nc.sbuf_tensor("a3_hT", [128, 4, EC2, 128], BF16))
        stt4 = PE(nc.sbuf_tensor("a3_st", [128, 16], F32))
        rows_f = PE(nc.sbuf_tensor("a3_rowf", [1, GT], F32))
        negmu_rb = PE(nc.sbuf_tensor("a3_nmu", [1, GT], BF16))
        invr_rb = PE(nc.sbuf_tensor("a3_ivr", [1, GT], BF16))
        rows_f2 = PE(nc.sbuf_tensor("a3_rowf2", [1, GT], F32))
        xrt = PE(nc.sbuf_tensor("a3_xr", [128, 2, D], F32))
        xnew = PE(nc.sbuf_tensor("a3_xn", [128, 2, D], F32))

        w1gt = io["w1g_h"].rearrange("(c p) e -> c p e", p=128)
        w2t = io["w2T_h"].rearrange("(c p) e -> c p e", p=128)

        def _(sync):
            for e2c in range(EC2):
                C.dinc(sync.dma_start(out=w2sb[:, e2c, :], in_=w2t[e2c, :, :]), "lx")
            C.fence(sync, "lx")
        b.sync(_)
        ld_w2 = C.n["lx"]

        for g in range(NG):
            def _(sync, g=g):
                C.dinc(sync.dma_start(out=Srb[:], in_=sp_Sr[g, :, :, :]), "lx")
                C.dinc(sync.dma_start(out=Sib[:], in_=sp_Si[g, :, :, :]), "lx")
                C.dinc(sync.dma_start(out=cosb[:], in_=sp_cos[g, :, :, :]), "lx")
                C.dinc(sync.dma_start(out=sinb[:], in_=sp_sin[g, :, :, :]), "lx")
                C.dinc(sync.dma_start(out=ctxb[:, 0:DC, :], in_=sp_cmr[g, :, :, :]), "lx")
                C.dinc(sync.dma_start(out=ctxb[:, DC:2 * DC, :], in_=sp_cmi[g, :, :, :]), "lx")
                C.dinc(sync.dma_start(out=invpt[:],
                                      in_=io["invp_in"][0:1, g * GT:(g + 1) * GT]
                                      .broadcast_to((128, GT))), "lx")
                C.fence(sync, "lx")
            b.sync(_)
            ld_sp = C.n["lx"]

            def _(v, ld_sp=ld_sp):
                C.w(v, "lx", ld_sp)
                for ec in range(DC):
                    v.tensor_scalar(out=Srb[:, ec, :], in0=Srb[:, ec, :],
                                    scalar1=off_r[:, ec:ec + 1], scalar2=None, op0=ALU.add)
                    v.tensor_scalar(out=Sib[:, ec, :], in0=Sib[:, ec, :],
                                    scalar1=off_i[:, ec:ec + 1], scalar2=None, op0=ALU.add)
                for ec in range(DC):
                    v.tensor_tensor(out=tmp1[:], in0=Srb[:, ec, :], in1=cosb[:, ec, :], op=ALU.mult)
                    v.tensor_tensor(out=tmp2[:], in0=Sib[:, ec, :], in1=sinb[:, ec, :], op=ALU.mult)
                    v.tensor_add(tmp1[:], tmp1[:], tmp2[:])
                    v.tensor_tensor(out=ctxb[:, 2 * DC + ec, :], in0=tmp1[:], in1=invpt[:], op=ALU.mult)
                    v.tensor_tensor(out=tmp1[:], in0=Sib[:, ec, :], in1=cosb[:, ec, :], op=ALU.mult)
                    v.tensor_tensor(out=tmp2[:], in0=Srb[:, ec, :], in1=sinb[:, ec, :], op=ALU.mult)
                    v.tensor_sub(tmp1[:], tmp1[:], tmp2[:])
                    v.tensor_tensor(out=ctxb[:, 3 * DC + ec, :], in0=tmp1[:], in1=invpt[:], op=ALU.mult)
                C.inc(v.tensor_copy(junk_one(C), ctxb[:, FC - 1, 0:1]), "vc")
            b.vector(_)
            v_ctx = C.n["vc"]

            s_sq0 = C.n["sc"]
            c20 = C.n["c2"]

            def _(s, v_ctx=v_ctx, c20=c20):
                C.w(s, "vc", v_ctx)
                for fc in range(FC):
                    if fc >= 2:
                        C.w(s, "c2", c20 + (fc - 1))
                    C.inc(s.activation(out=sqc[:, fc % 2, :], in_=ctxb[:, fc, :],
                                       func=AF.Square), "sc")
            b.scalar(_)

            stw1a = C.stream("l")
            stw1b = C.stream("w")
            c10 = C.n["c1"]

            def _(sync, g=g, v_ctx=v_ctx, c10=c10, stw1a=stw1a, stw1b=stw1b):
                C.w(sync, "vc", v_ctx)
                for e2g in range(4):
                    for fc in range(FC):
                        k = e2g * FC + fc
                        if k >= 4:
                            C.w(sync, "c1", c10 + (k - 3))
                        st = (stw1a, stw1b)[k % 2]
                        st.issue(sync.dma_start(out=w1st[:, k % 4, :],
                                                in_=w1gt[fc, :, e2g * GT:(e2g + 1) * GT]), k // 2)
            b.sync(_)

            t_g0 = C.n["tc"]
            c10_t = c10
            c30 = C.n["c3"]
            v_rows = v_ctx + 12
            s_gelu0 = s_sq0 + FC + 1

            def _(t, v_ctx=v_ctx, stw1a=stw1a, stw1b=stw1b, s_sq0=s_sq0, v_rows=v_rows,
                  s_gelu0=s_gelu0, c30=c30):
                C.w(t, "vc", v_ctx)
                for e2g in range(4):
                    if e2g > 0:
                        C.w(t, "sc", s_gelu0 + 4 * e2g)
                    for fc in range(FC):
                        k = e2g * FC + fc
                        (stw1a, stw1b)[k % 2].wait(t, k // 2)
                        if e2g == 0:
                            C.w(t, "sc", s_sq0 + fc + 1)
                        for tt in range(4):
                            mm_w1 = t.matmul(ps[tt][:], ctxb[:, fc, tt * 128:(tt + 1) * 128],
                                             w1st[:, k % 4, :], start=(fc == 0), stop=False)
                            if e2g == 0:
                                t.matmul(psS[:, 2 * tt:2 * tt + 1],
                                         ctxb[:, fc, tt * 128:(tt + 1) * 128],
                                         ones_colb[:, 0:1],
                                         start=(fc == 0 and tt == 0), stop=False)
                                mm_sq = t.matmul(psS[:, 2 * tt + 1:2 * tt + 2],
                                                 sqc[:, fc % 2, tt * 128:(tt + 1) * 128],
                                                 ones_colb[:, 0:1], start=False,
                                                 stop=(fc == FC - 1 and tt == 3))
                        C.inc(mm_w1, "c1")
                        if e2g == 0:
                            C.inc(mm_sq, "c2")
                    if e2g == 0:
                        C.w(t, "vc", v_rows)
                    for tt in range(4):
                        t.matmul(ps[tt][:], negmu_rb[0:1, tt * 128:(tt + 1) * 128],
                                 colsum_bf[0:1, e2g * GT:(e2g + 1) * GT],
                                 start=False, stop=False)
                        last_r1 = t.matmul(ps[tt][:], invr_rb[0:1, tt * 128:(tt + 1) * 128],
                                           const_bf[0:1, e2g * GT:(e2g + 1) * GT],
                                           start=False, stop=True)
                    C.inc(last_r1, "c3")
            b.tensor(_)

            def _(v, c20=c20):
                C.w(v, "c2", c20 + FC)
                v.tensor_scalar(out=stt4[:, 0:4], in0=psS[:, 0:8:2], scalar1=1.0 / D4,
                                scalar2=None, op0=ALU.mult)
                C.inc(v.tensor_scalar(out=stt4[:, 4:8], in0=psS[:, 1:8:2], scalar1=1.0 / D4,
                                      scalar2=None, op0=ALU.mult), "vc")
                C.w(v, "vc", C.n["vc"])
                C.inc(v.tensor_mul(stt4[:, 8:12], stt4[:, 0:4], stt4[:, 0:4]), "vc")
                C.w(v, "vc", C.n["vc"])
                C.inc(v.tensor_sub(stt4[:, 4:8], stt4[:, 4:8], stt4[:, 8:12]), "vc")
                C.w(v, "vc", C.n["vc"])
                C.inc(v.tensor_scalar(out=stt4[:, 4:8], in0=stt4[:, 4:8], scalar1=LN_EPS,
                                      scalar2=None, op0=ALU.add), "vc")
            b.vector(_)
            v_ve = C.n["vc"]

            def _(s, v_ve=v_ve):
                C.w(s, "vc", v_ve)
                C.inc(s.sqrt(stt4[:, 8:12], stt4[:, 4:8]), "sc")
            b.scalar(_)
            s_sv = C.n["sc"]

            def _(v, s_sv=s_sv):
                C.w(v, "sc", s_sv)
                C.inc(v.reciprocal(stt4[:, 12:16], stt4[:, 8:12]), "vc")
                C.w(v, "vc", C.n["vc"])
                C.inc(v.tensor_mul(stt4[:, 0:4], stt4[:, 12:16], stt4[:, 12:16]), "vc")
                C.w(v, "vc", C.n["vc"])
                C.inc(v.tensor_mul(stt4[:, 0:4], stt4[:, 0:4], stt4[:, 4:8]), "vc")
                C.w(v, "vc", C.n["vc"])
                C.inc(v.tensor_scalar(out=stt4[:, 0:4], in0=stt4[:, 0:4], scalar1=-0.5,
                                      scalar2=1.5, op0=ALU.mult, op1=ALU.add), "vc")
                C.w(v, "vc", C.n["vc"])
                C.inc(v.tensor_mul(stt4[:, 12:16], stt4[:, 12:16], stt4[:, 0:4]), "vc")
                C.w(v, "vc", C.n["vc"])
                C.inc(v.tensor_scalar(out=stt4[:, 0:4], in0=psS[:, 0:8:2],
                                      scalar1=-1.0 / D4, scalar2=None, op0=ALU.mult), "vc")
            b.vector(_)
            v_nm = C.n["vc"]

            def _(g2, v_nm=v_nm):
                C.w(g2, "vc", v_nm)
                C.fence(g2, "px")
                for tt in range(4):
                    C.dinc(g2.dma_start(out=rows_f[0:1, tt * 128:(tt + 1) * 128],
                                        in_=stt4[:, tt:tt + 1]), "px")
                    C.dinc(g2.dma_start(out=rows_f2[0:1, tt * 128:(tt + 1) * 128],
                                        in_=stt4[:, 8 + tt:9 + tt]), "px")
            b.gpsimd(_)
            g_rows = C.n["px"]

            def _(v, g_rows=g_rows):
                C.w(v, "px", g_rows)
                C.inc(v.tensor_copy(negmu_rb[0:1, :], rows_f[0:1, :]), "vc")
                C.inc(v.tensor_copy(invr_rb[0:1, :], rows_f2[0:1, :]), "vc")
            b.vector(_)
            assert C.n["vc"] == v_rows, (C.n["vc"], v_rows)

            def hslot(k):
                return cosb[:, k, :] if k < DC else sinb[:, k - DC, :]

            def _(s, c30=c30, s_gelu0=s_gelu0):
                assert C.n["sc"] == s_gelu0, (C.n["sc"], s_gelu0)
                for e2g in range(4):
                    C.w(s, "c3", c30 + e2g + 1)
                    for tt in range(4):
                        C.inc(s.activation(out=hslot(e2g * 4 + tt), in_=ps[tt][:],
                                           func=AF.Gelu,
                                           scale=stt4[:, 12 + tt:13 + tt]), "sc")
            b.scalar(_)

            t_tr0 = C.n["tc"]
            v_tr0 = v_rows

            def _(t, s_gelu0=s_gelu0, v_tr0=v_tr0):
                C.w(t, "sc", s_gelu0 + 16)
                n = 0
                for e2g in range(4):
                    for tt in range(4):
                        for q in range(4):
                            if n >= 2:
                                C.w(t, "vc", v_tr0 + n - 1)
                            C.inc(t.transpose(
                                psT2[n % 2][:, 0:128],
                                hslot(e2g * 4 + tt)[:, q * 128:(q + 1) * 128],
                                identb[:]), "tc")
                            n += 1
            b.tensor(_)

            def _(v, t_tr0=t_tr0):
                n = 0
                for e2g in range(4):
                    for tt in range(4):
                        for q in range(4):
                            n += 1
                            C.w(v, "tc", t_tr0 + n)
                            C.inc(v.tensor_copy(hT[:, tt, e2g * 4 + q, :],
                                                psT2[(n - 1) % 2][:, 0:128]), "vc")
            b.vector(_)
            v_resid0 = C.n["vc"]
            assert v_resid0 == v_tr0 + 64

            t_w20 = C.n["tc"]
            stx3 = C.stream("l2")
            stst3 = C.stream("p")
            v_resid_base = v_resid0

            def _(t, v_resid0=v_resid0, t_w20=t_w20):
                C.w(t, "vc", v_resid0)
                C.w(t, "lx", ld_w2)
                for tt in range(4):
                    bank0 = (tt % 2) * 2
                    if tt >= 2:
                        C.w(t, "vc", v_resid0 + (tt - 1))
                    for e2c in range(EC2):
                        t.matmul(ps[bank0][:], hT[:, tt, e2c, :], w2sb[:, e2c, 0:GT],
                                 start=(e2c == 0), stop=False)
                        t.matmul(ps[bank0 + 1][:], hT[:, tt, e2c, :], w2sb[:, e2c, GT:D],
                                 start=(e2c == 0), stop=False)
                    t.matmul(ps[bank0][:], ones_row[0:1, 0:128], b2_row[0:1, 0:GT],
                             start=False, stop=True)
                    C.inc(t.matmul(ps[bank0 + 1][:], ones_row[0:1, 0:128],
                                   b2_row[0:1, GT:D], start=False, stop=True), "tc")
            b.tensor(_)

            def _(v, g=g, t_w20=t_w20, stx3=stx3):
                for tt in range(4):
                    i = g * 4 + tt
                    bank0 = (tt % 2) * 2
                    C.w(v, "tc", t_w20 + tt + 1)
                    stx3.wait(v, tt)
                    v.tensor_scalar(out=xnew[:, tt % 2, :], in0=xrt[:, tt % 2, :],
                                    scalar1=r1_c[:, i:i + 1], scalar2=b0_c[:, i:i + 1],
                                    op0=ALU.mult, op1=ALU.add)
                    v.tensor_add(xnew[:, tt % 2, 0:GT], xnew[:, tt % 2, 0:GT], ps[bank0][:])
                    C.inc(v.tensor_add(xnew[:, tt % 2, GT:D], xnew[:, tt % 2, GT:D],
                                       ps[bank0 + 1][:]), "vc")
            b.vector(_)
            v_res_end = C.n["vc"]

            def _(sync, g=g, v_resid_base=v_resid_base, stx3=stx3, stst3=stst3):
                for tt in range(4):
                    if tt >= 2:
                        C.w(sync, "vc", v_resid_base + (tt - 1))
                        stst3.wait(sync, tt - 2)
                    stx3.issue(sync.dma_start(out=xrt[:, tt % 2, :], in_=xsrc[g * 4 + tt, :, :]), tt)
            b.sync(_)

            def _(g2, g=g, v_res_end=v_res_end, stst3=stst3):
                for tt in range(4):
                    C.w(g2, "vc", v_res_end - (3 - tt))
                    stst3.issue(g2.dma_start(out=xdst[g * 4 + tt, :, :], in_=xnew[:, tt % 2, :]), tt)
            b.gpsimd(_)
            full_barrier(C)

    full_barrier(C)


def junk_one(C):
    return C.bufs["junkb"][:, 0:1]



DEBUG_BUILD = False
STAGES = 3
SUBSTAGE = 0


def build_nc():
    nc = bass.Bass("TRN2", target_bir_lowering=False, debug=DEBUG_BUILD,
                   detect_race_conditions=False)
    io = {}
    io["x_in"] = nc.dram_tensor("x", [TPC, D], F32, kind="ExternalInput").ap()
    io["out"] = nc.dram_tensor("out", [TPC, D], F32, kind="ExternalOutput").ap()
    io["wup_raw"] = nc.dram_tensor("wup", [I3, D], F32, kind="ExternalInput").ap()
    io["wgate_raw"] = nc.dram_tensor("wgate", [I3, D], F32, kind="ExternalInput").ap()
    io["wdown_raw"] = nc.dram_tensor("wdown", [D, I3], F32, kind="ExternalInput").ap()
    io["womT_in"] = nc.dram_tensor("womT", [D, D], F32, kind="ExternalInput").ap()
    io["w1T_in"] = nc.dram_tensor("w1T", [D4, D2], F32, kind="ExternalInput").ap()
    io["w2T_in"] = nc.dram_tensor("w2T", [D2, D], F32, kind="ExternalInput").ap()
    io["bom_in"] = nc.dram_tensor("bom", [1, D], F32, kind="ExternalInput").ap()
    io["b1_in"] = nc.dram_tensor("b1", [1, D2], F32, kind="ExternalInput").ap()
    io["b2_in"] = nc.dram_tensor("b2", [1, D], F32, kind="ExternalInput").ap()
    io["gamma_in"] = nc.dram_tensor("gammac", [128, FC], F32, kind="ExternalInput").ap()
    io["beta_in"] = nc.dram_tensor("betac", [128, FC], F32, kind="ExternalInput").ap()
    io["ls_in"] = nc.dram_tensor("lsc", [128, DC], F32, kind="ExternalInput").ap()
    io["identb_in"] = nc.dram_tensor("identb", [128, 128], BF16, kind="ExternalInput").ap()
    io["invsq_in"] = nc.dram_tensor("invsq", [1, TPC], F32, kind="ExternalInput").ap()
    io["invp_in"] = nc.dram_tensor("invp", [1, TPC], F32, kind="ExternalInput").ap()
    io["cmask_in"] = nc.dram_tensor("cmask", [1, NCORES], F32, kind="ExternalInput").ap()

    io["xres"] = nc.dram_tensor("xres", [NT, 128, D], F32).ap()
    io["wupT_h"] = nc.dram_tensor("wupT_h", [D, I3], BF16).ap()
    io["wgT_h"] = nc.dram_tensor("wgT_h", [D, I3], BF16).ap()
    io["wdT_h"] = nc.dram_tensor("wdT_h", [I3, D], BF16).ap()
    io["womT_h"] = nc.dram_tensor("womT_h", [D, D], BF16).ap()
    io["w1g_h"] = nc.dram_tensor("w1g_h", [D4, D2], BF16).ap()
    io["w2T_h"] = nc.dram_tensor("w2T_h", [D2, D], BF16).ap()
    io["sp_up"] = nc.dram_tensor("sp_up", [IC, NG, 128, GT], BF16).ap()
    io["sp_gate"] = nc.dram_tensor("sp_gate", [IC, NG, 128, GT], BF16).ap()
    io["sp_gi"] = nc.dram_tensor("sp_gi", [IC, NG, 128, GT], I16).ap()
    io["sp_sig"] = nc.dram_tensor("sp_sig", [IC, NG, 128, GT], U16).ap()
    io["sp_p"] = nc.dram_tensor("sp_p", [IC, NG, 128, GT], F32).ap()
    io["sp_phi"] = nc.dram_tensor("sp_phi", [NG, 128, DC, GT], F32).ap()
    io["sp_xln"] = nc.dram_tensor("sp_xln", [NG, 128, DC, GT], BF16).ap()
    io["sp_cos"] = nc.dram_tensor("sp_cos", [NG, 128, DC, GT], BF16).ap()
    io["sp_sin"] = nc.dram_tensor("sp_sin", [NG, 128, DC, GT], BF16).ap()
    io["sp_cmr"] = nc.dram_tensor("sp_cmr", [NG, 128, DC, GT], BF16).ap()
    io["sp_cmi"] = nc.dram_tensor("sp_cmi", [NG, 128, DC, GT], BF16).ap()
    io["sp_Sr"] = nc.dram_tensor("sp_Sr", [NG, 128, DC, GT], F32).ap()
    io["sp_Si"] = nc.dram_tensor("sp_Si", [NG, 128, DC, GT], F32).ap()

    es = contextlib.ExitStack()
    with es:
        E = es.enter_context
        block = E(nc.Block())
        C = Ctx(nc, block, es)
        bufs = C.bufs
        bufs["cc_in"] = nc.dram_tensor("cc_in", [128, 2], F32).ap()
        bufs["cc_out"] = nc.dram_tensor("cc_out", [128, 2], F32).ap()
        bufs["rowb_h"] = nc.dram_tensor("rowb_h", [16, 8], F32).ap()
        bufs["ag_in8"] = nc.dram_tensor("ag_in8", [128, 8], F32).ap()
        bufs["ag_out8"] = nc.dram_tensor("ag_out8", [NCORES * 128, 8], F32).ap()
        bufs["ag_in16"] = nc.dram_tensor("ag_in16", [128, 16], F32).ap()
        bufs["ag_out16"] = nc.dram_tensor("ag_out16", [NCORES * 128, 16], F32).ap()

        def sb(name, shape, dt):
            t = E(nc.sbuf_tensor("sb_" + name, shape, dt))
            bufs[name] = t
            return t

        sb("identb", [128, 128], BF16)
        sb("ones_row", [1, GT], F32)
        sb("ones_colb", [128, 1], BF16)
        sb("bom_row", [1, D], F32)
        sb("b2_row", [1, D], F32)
        sb("const_bf", [1, D2], BF16)
        sb("colsum_bf", [1, D2], BF16)
        sb("const_f", [1, D2], F32)
        sb("gamma_c", [128, FC], F32)
        sb("beta_c", [128, FC], F32)
        sb("exp_ls", [128, DC], F32)
        sb("cmaskt", [128, NCORES], F32)
        for nm in ("s1c", "s2c", "mxc", "mnc", "mu_c", "r_c", "a_c", "bq_c",
                   "mu0_c", "r0_c", "b0_c", "r1_c", "t0_c", "t1_c", "t2_c"):
            sb(nm, [128, NT], F32)
        sb("colv", [128, 8], F32)
        sb("rowf", [16, 256], F32)
        sb("bcp", [128, 2], F32)
        sb("colk", [16, 8], F32)
        sb("bc16", [128, 16], F32)
        sb("sc16", [128, 16], F32)
        sb("junkb", [128, 1024], BF16)
        sb("carry", [128, DC], F32)
        sb("carry_r", [128, DC], F32)
        sb("carry_i", [128, DC], F32)
        sb("off_phi", [128, DC], F32)
        sb("off_r", [128, DC], F32)
        sb("off_i", [128, DC], F32)
        sb("gat", [128, NCORES, 16], F32)
        sb("xstats", [128, 2, D], F32)
        bufs["ps"] = [E(nc.psum_tensor(f"ps{i}", [128, GT], F32)) for i in range(6)]
        bufs["psS"] = bufs["ps"][5]
        bufs["psT2"] = [E(nc.psum_tensor("psT_a", [128, 1024], BF16)),
                        E(nc.psum_tensor("psT_b", [128, 1024], BF16))]

        emit_prologue(C, io)
        x_tiles_in = io["x_in"].rearrange("(n p) d -> n p d", p=128)
        out_tiles = io["out"].rearrange("(n p) d -> n p d", p=128)
        if STAGES == 0:
            pass
        elif STAGES == 1:
            emit_ffn(C, 0, x_tiles_in, out_tiles, io)
        elif STAGES == 2:
            emit_ffn(C, 0, x_tiles_in, io["xres"], io)
            emit_attn(C, io["xres"], out_tiles, io)
        else:
            emit_ffn(C, 0, x_tiles_in, io["xres"], io)
            emit_attn(C, io["xres"], io["xres"], io)
            emit_ffn(C, 1, io["xres"], out_tiles, io)

        snap = C.snap()

        def _(g):
            for s in SEMS:
                if snap[s]:
                    g.wait_ge(C.sem[s], snap[s])
        block.gpsimd(_)
    return nc



_CACHE = {}


def kernel(**inputs):
    x = np.ascontiguousarray(np.asarray(inputs["L0_attn_x"], np.float32)).reshape(B * S, D)
    womT = np.ascontiguousarray(np.asarray(inputs["L0_attn_W_omega"], np.float32).T)
    w1T = np.ascontiguousarray(np.asarray(inputs["L0_attn_W1"], np.float32).T)
    w2T = np.ascontiguousarray(np.asarray(inputs["L0_attn_W2"], np.float32).T)
    bom = np.ascontiguousarray(np.asarray(inputs["L0_attn_b_omega"], np.float32).reshape(1, D))
    b1 = np.ascontiguousarray(np.asarray(inputs["L0_attn_b1"], np.float32).reshape(1, D2))
    b2 = np.ascontiguousarray(np.asarray(inputs["L0_attn_b2"], np.float32).reshape(1, D))
    gamma = np.ascontiguousarray(np.asarray(inputs["L0_attn_ln_gamma"], np.float32).reshape(FC, 128).T)
    beta = np.ascontiguousarray(np.asarray(inputs["L0_attn_ln_beta"], np.float32).reshape(FC, 128).T)
    lsc = np.ascontiguousarray(np.asarray(inputs["L0_attn_log_scale"], np.float32).reshape(DC, 128).T)
    wup = np.ascontiguousarray(np.asarray(inputs["L0_ffn_w_up"], np.float32))
    wgate = np.ascontiguousarray(np.asarray(inputs["L0_ffn_w_gate"], np.float32))
    wdown = np.ascontiguousarray(np.asarray(inputs["L0_ffn_w_down"], np.float32))
    identb = np.eye(128, dtype=NPBF)
    pos = np.arange(1, S + 1, dtype=np.float64)
    invsq_full = (1.0 / np.sqrt(pos)).astype(np.float32)
    invp_full = (1.0 / pos).astype(np.float32)

    if "nc" not in _CACHE:
        _CACHE["nc"] = build_nc()
    nc = _CACHE["nc"]

    in_maps = []
    for c in range(NCORES):
        sl = slice(c * TPC, (c + 1) * TPC)
        so = (c % 4) * TPC
        cmask = np.zeros((1, NCORES), np.float32)
        for cp in range((c // 4) * 4, c):
            cmask[0, cp] = 1.0
        in_maps.append({
            "x": np.ascontiguousarray(x[sl]),
            "womT": womT, "w1T": w1T, "w2T": w2T,
            "bom": bom, "b1": b1, "b2": b2,
            "gammac": gamma, "betac": beta, "lsc": lsc,
            "wup": wup, "wgate": wgate, "wdown": wdown,
            "identb": identb,
            "invsq": np.ascontiguousarray(invsq_full[so:so + TPC].reshape(1, TPC)),
            "invp": np.ascontiguousarray(invp_full[so:so + TPC].reshape(1, TPC)),
            "cmask": cmask,
        })

    res = run_bass_kernel_spmd(nc, in_maps, core_ids=list(range(NCORES)))
    outp = np.concatenate([np.asarray(res.results[c]["out"], np.float32)
                           for c in range(NCORES)], axis=0)
    return outp.reshape(B, S, D)


if __name__ == "__main__":
    nc = build_nc()
    n = sum(1 for _ in nc.all_instructions())
    print("built, instructions:", n)


# version 09
# speedup vs baseline: 1.1229x; 1.0123x over previous
import contextlib
import math

import numpy as np
import ml_dtypes

import concourse.bass as bass
import concourse.mybir as mybir
from concourse.bass_utils import run_bass_kernel_spmd

F32 = mybir.dt.float32
BF16 = mybir.dt.bfloat16
I16 = mybir.dt.int16
U16 = mybir.dt.uint16
I32 = mybir.dt.int32
AF = mybir.ActivationFunctionType
ALU = mybir.AluOpType
AX = mybir.AxisListType.X
NPBF = ml_dtypes.bfloat16

NCORES = 8
B, S, D = 2, 16384, 1024
TPC = 4096
NT = 32
NG = 8
GT = 512
DC = 8
I3, IC = 3072, 24
D4, FC = 4096, 32
D2, EC2 = 2048, 16
LN_EPS = 1e-5
TWO_PI = 2.0 * math.pi
RC1 = float(np.float32(TWO_PI))
RC2 = float(np.float32(TWO_PI - RC1))
QMAX = 32767.0
PIH = float(np.pi / 2)
PIF = float(np.pi)

SEMS = ("le", "lo", "l2e", "l2o", "wae", "wao", "lx", "pe", "po", "px", "tc", "vc", "sc", "cc", "c1", "c2", "c3")


class Ctx:
    def __init__(self, nc, block, es):
        self.nc = nc
        self.block = block
        self.es = es
        self.n = {}
        self.sem = {}
        for name in SEMS:
            self.sem[name] = es.enter_context(nc.semaphore(name))
            self.n[name] = 0
        self.bufs = {}

    def inc(self, instr, name, k=1):
        instr.then_inc(self.sem[name], k)
        self.n[name] += k
        return self.n[name]

    def dinc(self, instr, name):
        return self.inc(instr, name, 16)

    def w(self, eng, name, val):
        if val > 0:
            eng.wait_ge(self.sem[name], val)

    def snap(self):
        return dict(self.n)

    def stream(self, kind):
        return DStream(self, kind)

    def fence(self, eng, name):
        if self.n[name] > 0:
            eng.wait_ge(self.sem[name], self.n[name])


class DStream:

    def __init__(self, C, kind):
        self.C = C
        self.names = {"l": ("le", "lo"), "l2": ("l2e", "l2o"), "w": ("wae", "wao"),
                      "p": ("pe", "po")}[kind]
        self.base = [C.n[self.names[0]], C.n[self.names[1]]]
        self.cnt = [0, 0]

    def issue(self, instr, j):
        nm = self.names[j % 2]
        self.C.inc(instr, nm, 16)
        self.cnt[j % 2] += 16

    def wait(self, eng, j, ndma=1):
        nm = self.names[j % 2]
        eng.wait_ge(self.C.sem[nm], self.base[j % 2] + 16 * ndma * (j // 2 + 1))

    def wait_all(self, eng):
        for i in (0, 1):
            if self.cnt[i]:
                eng.wait_ge(self.C.sem[self.names[i]], self.base[i] + self.cnt[i])


def full_barrier(C):
    snap = C.snap()

    def _(eng):
        for s in SEMS:
            if snap[s]:
                eng.wait_ge(C.sem[s], snap[s])
    for sec in (C.block.sync, C.block.gpsimd, C.block.tensor, C.block.vector, C.block.scalar):
        sec(_)



def emit_allreduce_max(C, cols, nvals):
    nc, b = C.nc, C.block
    bufs = C.bufs
    rowf, colk, bc16, bcp = bufs["rowf"], bufs["colk"], bufs["bc16"], bufs["bcp"]
    cc_in, cc_out, rowb_h = bufs["cc_in"], bufs["cc_out"], bufs["rowb_h"]
    v0 = C.n["vc"]

    def _(g):
        C.w(g, "vc", v0)
        C.fence(g, "px")
        C.dinc(g.dma_start(out=cc_in, in_=cols[:, 0:2]), "px")
        C.fence(g, "px")
        C.inc(g.collective_compute(
            "AllReduce", ALU.max, replica_groups=[list(range(NCORES))],
            ins=[cc_in.opt()], outs=[cc_out.opt()]), "cc")
        C.w(g, "cc", C.n["cc"])
        C.dinc(g.dma_start(out=bcp[:], in_=cc_out), "px")
        C.fence(g, "px")
        C.dinc(g.dma_start(out=rowf[0:1, 0:256], in_=bcp[:, 0:2]), "px")
    b.gpsimd(_)
    sp1 = C.n["px"]

    def _(v):
        C.w(v, "px", sp1)
        C.inc(v.tensor_reduce(out=colk[0:1, 0:2],
                              in_=rowf[0:1, 0:256].rearrange("r (p c) -> r c p", c=2),
                              axis=AX, op=ALU.max), "vc")
    b.vector(_)
    v1 = C.n["vc"]

    def _(g):
        C.w(g, "vc", v1)
        C.dinc(g.dma_start(out=rowb_h[0:1, 0:8], in_=colk[0:1, :]), "px")
        C.fence(g, "px")
        for j in range(nvals):
            C.dinc(g.dma_start(out=bc16[:, j:j + 1],
                               in_=rowb_h[0:1, j:j + 1].broadcast_to((128, 1))), "px")
    b.gpsimd(_)


def emit_carry_gather(C, carry_list, off_list):
    nc, b = C.nc, C.block
    bufs = C.bufs
    gat, cmaskt = bufs["gat"], bufs["cmaskt"]
    k = len(carry_list)
    v0 = C.n["vc"]
    w = k * DC
    ag_in = bufs["ag_in8"] if k == 1 else bufs["ag_in16"]
    ag_out = bufs["ag_out8"] if k == 1 else bufs["ag_out16"]

    def _(g):
        C.w(g, "vc", v0)
        C.fence(g, "px")
        for j, car in enumerate(carry_list):
            C.dinc(g.dma_start(out=ag_in[:, j * DC:(j + 1) * DC], in_=car[:]), "px")
        C.fence(g, "px")
        C.inc(g.collective_compute(
            "AllGather", ALU.bypass, replica_groups=[list(range(NCORES))],
            ins=[ag_in[:, 0:w].opt()], outs=[ag_out[:, 0:w].opt()]), "cc")
        C.w(g, "cc", C.n["cc"])
        C.dinc(g.dma_start(out=gat[:, :, 0:w],
                           in_=ag_out[:, 0:w].rearrange("(c p) n -> p c n", p=128)), "px")
    b.gpsimd(_)
    sp1 = C.n["px"]

    def _(v):
        C.w(v, "px", sp1)
        C.inc(v.tensor_tensor(out=gat[:, :, 0:w], in0=gat[:, :, 0:w],
                              in1=cmaskt[:, 0:NCORES].rearrange("p (c o) -> p c o", o=1)
                              .broadcast_to((128, NCORES, w)),
                              op=ALU.mult), "vc")
        C.w(v, "vc", C.n["vc"])
        for j, off in enumerate(off_list):
            C.inc(v.tensor_reduce(out=off[:],
                                  in_=gat[:, :, j * DC:(j + 1) * DC].rearrange("p c n -> p n c"),
                                  axis=AX, op=ALU.add), "vc")
    b.vector(_)
    full_barrier(C)


def emit_ln0_stats(C, xsrc, with_minmax, mu_dst, r_dst):
    nc, b = C.nc, C.block
    bufs = C.bufs
    s1c, s2c, mxc, mnc = bufs["s1c"], bufs["s2c"], bufs["mxc"], bufs["mnc"]
    junkb = bufs["junkb"]
    t0, t1, t2 = bufs["t0_c"], bufs["t1_c"], bufs["t2_c"]
    xa = bufs["xstats"]

    stx = C.stream("l")
    v0 = C.n["vc"]
    s0 = C.n["sc"]

    def _(sync):
        for i in range(NT):
            if i >= 2:
                C.w(sync, "vc", v0 + (i - 1))
                C.w(sync, "sc", s0 + (i - 1))
            stx.issue(sync.dma_start(out=xa[:, i % 2, :], in_=xsrc[i, :, :]), i)
    b.sync(_)

    def _(v):
        for i in range(NT):
            stx.wait(v, i)
            xt = xa[:, i % 2, :]
            v.tensor_scalar(out=junkb[:, 0:D], in0=xt, scalar1=1.0, scalar2=None,
                            op0=ALU.mult, op1=ALU.add, accum_out=s1c[:, i:i + 1])
            if with_minmax:
                v.tensor_scalar(out=junkb[:, 0:D], in0=xt, scalar1=1.0, scalar2=None,
                                op0=ALU.mult, op1=ALU.max, accum_out=mxc[:, i:i + 1])
                ins = v.tensor_scalar(out=junkb[:, 0:D], in0=xt, scalar1=-1.0,
                                      scalar2=None, op0=ALU.mult, op1=ALU.max,
                                      accum_out=mnc[:, i:i + 1])
            else:
                ins = v.tensor_copy(junkb[:, 0:1], xt[:, 0:1])
            C.inc(ins, "vc")
    b.vector(_)

    def _(s):
        for i in range(NT):
            stx.wait(s, i)
            C.inc(s.activation(out=junkb[:, 0:D], in_=xa[:, i % 2, :], func=AF.Square,
                               accum_out=s2c[:, i:i + 1]), "sc")
    b.scalar(_)

    v1, s1 = C.n["vc"], C.n["sc"]

    def _(v):
        C.w(v, "vc", v1)
        C.w(v, "sc", s1)
        v.tensor_scalar(out=mu_dst[:], in0=s1c[:], scalar1=1.0 / D, scalar2=None, op0=ALU.mult)
        C.inc(v.tensor_scalar(out=t0[:], in0=s2c[:], scalar1=1.0 / D, scalar2=None,
                              op0=ALU.mult), "vc")
        C.w(v, "vc", C.n["vc"])
        C.inc(v.tensor_mul(t1[:], mu_dst[:], mu_dst[:]), "vc")
        C.w(v, "vc", C.n["vc"])
        C.inc(v.tensor_sub(t0[:], t0[:], t1[:]), "vc")
        C.w(v, "vc", C.n["vc"])
        C.inc(v.tensor_scalar(out=t0[:], in0=t0[:], scalar1=LN_EPS, scalar2=None,
                              op0=ALU.add), "vc")
    b.vector(_)
    v2 = C.n["vc"]

    def _(s):
        C.w(s, "vc", v2)
        C.inc(s.sqrt(t1[:], t0[:]), "sc")
    b.scalar(_)
    s2 = C.n["sc"]

    def _(v):
        C.w(v, "sc", s2)
        C.inc(v.reciprocal(t2[:], t1[:]), "vc")
        C.w(v, "vc", C.n["vc"])
        C.inc(v.tensor_mul(t1[:], t2[:], t2[:]), "vc")
        C.w(v, "vc", C.n["vc"])
        C.inc(v.tensor_mul(t1[:], t1[:], t0[:]), "vc")
        C.w(v, "vc", C.n["vc"])
        C.inc(v.tensor_scalar(out=t1[:], in0=t1[:], scalar1=-0.5, scalar2=1.5,
                              op0=ALU.mult, op1=ALU.add), "vc")
        C.w(v, "vc", C.n["vc"])
        C.inc(v.tensor_mul(r_dst[:], t2[:], t1[:]), "vc")
    b.vector(_)



def emit_prologue(C, io):
    nc, b = C.nc, C.block
    bufs = C.bufs
    identb = bufs["identb"]
    ones_row, ones_colb = bufs["ones_row"], bufs["ones_colb"]
    bom_row, b2_row = bufs["bom_row"], bufs["b2_row"]
    const_bf, colsum_bf, const_f = bufs["const_bf"], bufs["colsum_bf"], bufs["const_f"]
    gamma_c, beta_c, exp_ls, cmaskt = bufs["gamma_c"], bufs["beta_c"], bufs["exp_ls"], bufs["cmaskt"]
    ps, psT2 = bufs["ps"], bufs["psT2"]

    with contextlib.ExitStack() as pes:
        PE = pes.enter_context
        wbuf = PE(nc.sbuf_tensor("p_wbuf", [128, 2, I3], F32))
        ls_raw = PE(nc.sbuf_tensor("p_lsraw", [128, DC], F32))
        b1_rowf = PE(nc.sbuf_tensor("p_b1row", [1, D2], F32))
        bmax = PE(nc.sbuf_tensor("p_bmax", [128, 2, 96], F32))
        f7 = PE(nc.sbuf_tensor("p_f7", [128, 2, 96], F32))
        smul = PE(nc.sbuf_tensor("p_smul", [128, 2, 96], F32))
        qi = PE(nc.sbuf_tensor("p_qi", [128, I3], I32))
        wq = PE(nc.sbuf_tensor("p_wq", [128, 2, I3], BF16))
        wtr = PE(nc.sbuf_tensor("p_wtr", [128, 2, 128], BF16))
        w1chunk = PE(nc.sbuf_tensor("p_w1c", [128, 2, D2], F32))
        w1bf = PE(nc.sbuf_tensor("p_w1bf", [128, 2, D2], BF16))

        def _(sync):
            for dst, src in ((identb[:], io["identb_in"]), (bom_row[:], io["bom_in"]),
                             (b1_rowf[:], io["b1_in"]), (b2_row[:], io["b2_in"]),
                             (gamma_c[:], io["gamma_in"]), (beta_c[:], io["beta_in"]),
                             (ls_raw[:], io["ls_in"])):
                C.dinc(sync.dma_start(out=dst, in_=src), "lx")
            C.dinc(sync.dma_start(out=cmaskt[:, 0:NCORES],
                                  in_=io["cmask_in"][0:1, :].broadcast_to((128, NCORES))), "lx")
            C.fence(sync, "lx")
        b.sync(_)
        ld_consts = C.n["lx"]

        def _(v):
            v.memset(ones_row[:], 1.0)
            v.memset(bufs["colk"][:], 0.0)
            v.memset(bufs["colv"][:], 0.0)
            C.inc(v.memset(ones_colb[:], 1.0), "vc")
        b.vector(_)



        def _(s):
            C.w(s, "lx", ld_consts)
            C.inc(s.activation(out=exp_ls[:], in_=ls_raw[:], func=AF.Exp), "sc")
        b.scalar(_)

        jobs = [(io["wup_raw"], io["wupT_h"], IC, D),
                (io["wgate_raw"], io["wgT_h"], IC, D),
                (io["wdown_raw"], io["wdT_h"], DC, I3)]
        for raw, dstT, nrc, ncols in jobs:
            nb = ncols // 32
            ncc = ncols // 128
            raw_t = raw.rearrange("(n p) c -> n p c", p=128)
            dst_t = dstT.rearrange("(n p) c -> n p c", p=128)
            stw = C.stream("l")
            stp = C.stream("p")
            v0 = C.n["vc"]
            t0n = C.n["tc"]
            s0 = C.n["sc"]

            def _(sync, raw_t=raw_t, nrc=nrc, ncols=ncols, v0=v0, stw=stw):
                for i in range(nrc):
                    if i >= 2:
                        C.w(sync, "vc", v0 + 5 * (i - 1))
                    stw.issue(sync.dma_start(out=wbuf[:, i % 2, 0:ncols], in_=raw_t[i, :, :]), i)
            b.sync(_)

            def _(v, nrc=nrc, ncols=ncols, nb=nb, t0n=t0n, ncc=ncc, stw=stw):
                for i in range(nrc):
                    sl = i % 2
                    stw.wait(v, i)
                    if i >= 2:
                        C.w(v, "tc", t0n + (i - 1) * ncc)
                    C.inc(v.tensor_reduce(out=bmax[:, sl, 0:nb],
                                          in_=wbuf[:, sl, 0:ncols]
                                          .rearrange("p (b k) -> p b k", k=32),
                                          axis=AX, op=ALU.max,
                                          apply_absolute_value=True), "vc")
                    C.w(v, "vc", C.n["vc"])
                    C.inc(v.tensor_scalar(out=bmax[:, sl, 0:nb], in0=bmax[:, sl, 0:nb],
                                           scalar1=7e-12, scalar2=None, op0=ALU.max), "vc")
                    C.w(v, "vc", C.n["vc"])
                    C.inc(v.reciprocal(f7[:, sl, 0:nb], bmax[:, sl, 0:nb]), "vc")
                    C.w(v, "vc", C.n["vc"])
                    v.tensor_scalar(out=f7[:, sl, 0:nb], in0=f7[:, sl, 0:nb],
                                    scalar1=7.0, scalar2=None, op0=ALU.mult)
                    C.inc(v.tensor_scalar(out=smul[:, sl, 0:nb], in0=bmax[:, sl, 0:nb],
                                          scalar1=1.0 / 7.0, scalar2=None,
                                          op0=ALU.mult), "vc")
                    C.w(v, "vc", C.n["vc"])
                    v.tensor_tensor(out=qi[:, 0:ncols].rearrange("p (b k) -> p b k", k=32),
                                    in0=wbuf[:, sl, 0:ncols].rearrange("p (b k) -> p b k", k=32),
                                    in1=f7[:, sl, 0:nb].rearrange("p (b o) -> p b o", o=1)
                                    .broadcast_to((128, nb, 32)),
                                    op=ALU.mult)
                    C.inc(v.tensor_tensor(
                        out=wq[:, sl, 0:ncols].rearrange("p (b k) -> p b k", k=32),
                        in0=qi[:, 0:ncols].rearrange("p (b k) -> p b k", k=32),
                        in1=smul[:, sl, 0:nb].rearrange("p (b o) -> p b o", o=1)
                        .broadcast_to((128, nb, 32)),
                        op=ALU.mult), "vc")
            b.vector(_)

            def _(t, nrc=nrc, ncc=ncc, v0=v0, s0=s0):
                for i in range(nrc):
                    C.w(t, "vc", v0 + 5 * (i + 1))
                    for cc_ in range(ncc):
                        k = i * ncc + cc_
                        if k >= 2:
                            C.w(t, "sc", s0 + k - 1)
                        C.inc(t.transpose(psT2[k % 2][:, 0:128],
                                          wq[:, i % 2, cc_ * 128:(cc_ + 1) * 128],
                                          identb[:]), "tc")
            b.tensor(_)

            def _(s, nrc=nrc, ncc=ncc, t0n=t0n, stp=stp):
                for k in range(nrc * ncc):
                    C.w(s, "tc", t0n + k + 1)
                    if k >= 2:
                        stp.wait(s, k - 2)
                    C.inc(s.activation(out=wtr[:, k % 2, :], in_=psT2[k % 2][:, 0:128],
                                       func=AF.Identity), "sc")
            b.scalar(_)

            def _(g, nrc=nrc, ncc=ncc, s0=s0, dst_t=dst_t, stp=stp):
                for k in range(nrc * ncc):
                    i, cc_ = divmod(k, ncc)
                    C.w(g, "sc", s0 + k + 1)
                    stp.issue(g.dma_start(out=dst_t[cc_, :, i * 128:(i + 1) * 128],
                                          in_=wtr[:, k % 2, :]), k)
            b.gpsimd(_)
            full_barrier(C)

        for src_in, dst_h, nch, ncols in ((io["womT_in"], io["womT_h"], DC, D),
                                          (io["w2T_in"], io["w2T_h"], EC2, D)):
            src_t = src_in.rearrange("(n p) c -> n p c", p=128)
            dst_t = dst_h.rearrange("(n p) c -> n p c", p=128)
            stw = C.stream("l")
            stp = C.stream("p")
            v0 = C.n["vc"]

            def _(sync, src_t=src_t, nch=nch, ncols=ncols, v0=v0, stw=stw):
                for i in range(nch):
                    if i >= 2:
                        C.w(sync, "vc", v0 + (i - 1))
                    stw.issue(sync.dma_start(out=wbuf[:, i % 2, 0:ncols], in_=src_t[i, :, :]), i)
            b.sync(_)

            t_start = C.n["tc"]

            def _(v, nch=nch, ncols=ncols, t_start=t_start, stw=stw, stp=stp):
                for i in range(nch):
                    stw.wait(v, i)
                    if i < 2:
                        C.w(v, "tc", t_start)
                    if i >= 2:
                        stp.wait(v, i - 2)
                    C.inc(v.tensor_copy(wq[:, i % 2, 0:ncols], wbuf[:, i % 2, 0:ncols]), "vc")
            b.vector(_)

            def _(g, nch=nch, ncols=ncols, v0=v0, dst_t=dst_t, stp=stp):
                for i in range(nch):
                    C.w(g, "vc", v0 + i + 1)
                    stp.issue(g.dma_start(out=dst_t[i, :, :], in_=wq[:, i % 2, 0:ncols]), i)
            b.gpsimd(_)
            full_barrier(C)

        w1T_t = io["w1T_in"].rearrange("(n p) c -> n p c", p=128)
        w1g_t = io["w1g_h"].rearrange("(n p) c -> n p c", p=128)
        stw1 = C.stream("l")
        stp1 = C.stream("p")
        v0 = C.n["vc"]
        t0n = C.n["tc"]

        def _(sync, v0=v0, t0n=t0n, stw1=stw1):
            for i in range(FC):
                if i >= 2:
                    C.w(sync, "vc", v0 + (i - 1))
                    C.w(sync, "tc", t0n + 4 * (i - 1))
                stw1.issue(sync.dma_start(out=w1chunk[:, i % 2, :], in_=w1T_t[i, :, :]), i)
        b.sync(_)

        def _(v, stw1=stw1, stp1=stp1):
            for i in range(FC):
                stw1.wait(v, i)
                if i >= 2:
                    stp1.wait(v, i - 2)
                C.inc(v.tensor_scalar(out=w1bf[:, i % 2, :], in0=w1chunk[:, i % 2, :],
                                      scalar1=gamma_c[:, i:i + 1], scalar2=None,
                                      op0=ALU.mult), "vc")
        b.vector(_)

        def _(g, v0=v0, stp1=stp1):
            for i in range(FC):
                C.w(g, "vc", v0 + i + 1)
                stp1.issue(g.dma_start(out=w1g_t[i, :, :], in_=w1bf[:, i % 2, :]), i)
        b.gpsimd(_)

        def _(t, stw1=stw1):
            for i in range(FC):
                stw1.wait(t, i)
                for j in range(4):
                    C.inc(t.matmul(ps[j][0:1, :], beta_c[:, i:i + 1],
                                   w1chunk[:, i % 2, j * 512:(j + 1) * 512],
                                   start=(i == 0), stop=(i == FC - 1)), "tc")
        b.tensor(_)
        t_beta = C.n["tc"]

        def _(v, t_beta=t_beta, stw1=stw1):
            C.w(v, "tc", t_beta)
            stw1.wait_all(v)
            for j in range(4):
                v.tensor_add(const_f[0:1, j * 512:(j + 1) * 512], ps[j][0:1, :],
                             b1_rowf[0:1, j * 512:(j + 1) * 512])
            C.inc(v.tensor_copy(const_bf[0:1, :], const_f[0:1, :]), "vc")
        b.vector(_)
        full_barrier(C)

        stc = C.stream("l")
        v0 = C.n["vc"]
        t0n = C.n["tc"]

        def _(sync, t0n=t0n, stc=stc, stp1=stp1):
            stp1.wait_all(sync)
            C.w(sync, "vc", C.n["vc"])
            for i in range(FC):
                if i >= 2:
                    C.w(sync, "tc", t0n + 4 * (i - 1))
                stc.issue(sync.dma_start(out=w1bf[:, i % 2, :], in_=w1g_t[i, :, :]), i)
        b.sync(_)

        def _(t, stc=stc):
            for i in range(FC):
                stc.wait(t, i)
                for j in range(4):
                    C.inc(t.matmul(ps[j][0:1, :], ones_colb[:, 0:1],
                                   w1bf[:, i % 2, j * 512:(j + 1) * 512],
                                   start=(i == 0), stop=(i == FC - 1)), "tc")
        b.tensor(_)

        def _(v):
            C.w(v, "tc", C.n["tc"])
            for j in range(4):
                C.inc(v.tensor_copy(colsum_bf[0:1, j * 512:(j + 1) * 512], ps[j][0:1, :]), "vc")
        b.vector(_)

        full_barrier(C)



def emit_ffn(C, call, xsrc, xdst, io):
    nc, b = C.nc, C.block
    bufs = C.bufs
    ps, psT2 = bufs["ps"], bufs["psT2"]
    identb = bufs["identb"]
    bc16, sc16 = bufs["bc16"], bufs["sc16"]
    mu_c, r_c, a_c, bq_c = bufs["mu_c"], bufs["r_c"], bufs["a_c"], bufs["bq_c"]
    mxc, mnc = bufs["mxc"], bufs["mnc"]
    t0, t1 = bufs["t0_c"], bufs["t1_c"]
    colv = bufs["colv"]
    sp_up, sp_gate, sp_gi, sp_sig, sp_p = (io["sp_up"], io["sp_gate"], io["sp_gi"],
                                           io["sp_sig"], io["sp_p"])

    emit_ln0_stats(C, xsrc, True, mu_c, r_c)

    def _(v):
        C.w(v, "vc", C.n["vc"])
        v.tensor_sub(t0[:], mxc[:], mu_c[:])
        C.inc(v.tensor_add(t1[:], mnc[:], mu_c[:]), "vc")
        C.w(v, "vc", C.n["vc"])
        C.inc(v.tensor_max(t0[:], t0[:], t1[:]), "vc")
        C.w(v, "vc", C.n["vc"])
        C.inc(v.tensor_mul(t0[:], t0[:], r_c[:]), "vc")
        C.w(v, "vc", C.n["vc"])
        C.inc(v.tensor_reduce(out=colv[:, 0:1], in_=t0[:], axis=AX, op=ALU.max), "vc")
    b.vector(_)

    emit_allreduce_max(C, colv, 1)
    sp_bc = C.n["px"]

    def _(v):
        C.w(v, "px", sp_bc)
        C.inc(v.tensor_scalar(out=sc16[:, 0:1], in0=bc16[:, 0:1], scalar1=1.0 / QMAX,
                              scalar2=1e-12, op0=ALU.mult, op1=ALU.max), "vc")
        C.w(v, "vc", C.n["vc"])
        C.inc(v.reciprocal(sc16[:, 1:2], sc16[:, 0:1]), "vc")
        C.w(v, "vc", C.n["vc"])
        C.inc(v.tensor_scalar(out=a_c[:], in0=r_c[:], scalar1=sc16[:, 1:2], scalar2=None,
                              op0=ALU.mult), "vc")
        C.w(v, "vc", C.n["vc"])
        C.inc(v.tensor_mul(bq_c[:], mu_c[:], a_c[:]), "vc")
        C.w(v, "vc", C.n["vc"])
        C.inc(v.tensor_scalar(out=bq_c[:], in0=bq_c[:], scalar1=-1.0, scalar2=None,
                              op0=ALU.mult), "vc")
    b.vector(_)
    full_barrier(C)

    with contextlib.ExitStack() as pes:
        PE = pes.enter_context
        xa = bufs["xstats"]
        qtile = PE(nc.sbuf_tensor(f"f{call}_q", [128, 2, D], I16))
        xqtm = PE(nc.sbuf_tensor(f"f{call}_xqtm", [128, 2, D], BF16))
        xqT = PE(nc.sbuf_tensor(f"f{call}_xqT", [128, DC, TPC], BF16))
        wst = PE(nc.sbuf_tensor(f"f{call}_wst", [128, 2, DC, 128], BF16))
        ev = PE(nc.sbuf_tensor(f"f{call}_ev", [128, 8, GT], BF16))
        amx = PE(nc.sbuf_tensor(f"f{call}_amx", [128, 2, 2 * IC * NG], F32))
        junk = PE(nc.sbuf_tensor(f"f{call}_junk", [128, GT], BF16))

        stx = C.stream("l")
        s0 = C.n["sc"]
        v0 = C.n["vc"]
        t0n = C.n["tc"]

        def _(sync, v0=v0, stx=stx):
            for i in range(NT):
                if i >= 2:
                    C.w(sync, "vc", v0 + (i - 1))
                stx.issue(sync.dma_start(out=xa[:, i % 2, :], in_=xsrc[i, :, :]), i)
        b.sync(_)

        def _(s, v0=v0, t0n=t0n, stx=stx):
            for i in range(NT):
                stx.wait(s, i)
                C.inc(s.activation(out=qtile[:, i % 2, :], in_=xa[:, i % 2, :],
                                   func=AF.Identity,
                                   scale=a_c[:, i:i + 1], bias=bq_c[:, i:i + 1]), "sc")
                for dc in range(DC):
                    k = i * DC + dc
                    C.w(s, "tc", t0n + k + 1)
                    C.inc(s.activation(out=xqT[:, dc, i * 128:(i + 1) * 128],
                                       in_=psT2[k % 2][:, 0:128], func=AF.Identity), "sc")
        b.scalar(_)

        def _(v, s0=s0):
            for i in range(NT):
                C.w(v, "sc", s0 + i * (DC + 1) + 1)
                C.inc(v.tensor_scalar(out=xqtm[:, i % 2, :], in0=qtile[:, i % 2, :],
                                      scalar1=sc16[:, 0:1], scalar2=None, op0=ALU.mult), "vc")
        b.vector(_)

        def _(t, v0=v0, s0=s0):
            for i in range(NT):
                C.w(t, "vc", v0 + i + 1)
                for dc in range(DC):
                    k = i * DC + dc
                    if k >= 2:
                        ii, dd = divmod(k - 2, DC)
                        C.w(t, "sc", s0 + ii * (DC + 1) + 1 + dd + 1)
                    C.inc(t.transpose(psT2[k % 2][:, 0:128],
                                      xqtm[:, i % 2, dc * 128:(dc + 1) * 128], identb[:]), "tc")
        b.tensor(_)

        full_barrier(C)

        if SUBSTAGE == 2:
            def _(g2):
                for s in ("tc", "vc", "sc"):
                    C.w(g2, s, C.n[s])
                for j, buf in enumerate((bufs["s1c"], bufs["s2c"], mxc, mnc, mu_c, r_c, a_c, bq_c)):
                    C.dinc(g2.dma_start(out=xdst[0, :, j * 32:(j + 1) * 32], in_=buf[:]), "px")
                C.dinc(g2.dma_start(out=xdst[0, :, 256:264], in_=colv[:, 0:8]), "px")
                C.dinc(g2.dma_start(out=xdst[0, :, 264:280], in_=bc16[:, :]), "px")
                C.dinc(g2.dma_start(out=xdst[0, :, 280:296], in_=sc16[:, :]), "px")
            b.gpsimd(_)
            full_barrier(C)
            return

        if SUBSTAGE in (3, 4, 5):
            wh_d = {3: io["wupT_h"], 4: io["wgT_h"], 5: io["wdT_h"]}[SUBSTAGE]
            nrows = wh_d.shape[0]
            wcols = wh_d.shape[1]
            nch = nrows // 128
            ntc = wcols // D
            wtmp_b = PE(nc.sbuf_tensor("dbg_wb", [128, wcols], BF16))
            wtmp_f = PE(nc.sbuf_tensor("dbg_wf", [128, wcols], F32))
            wh_t = wh_d.rearrange("(n p) c -> n p c", p=128)
            lxw = C.n["lx"]
            vw = C.n["vc"]
            pxw = C.n["px"]

            def _(sync, lxw=lxw, pxw=pxw):
                for i in range(nch):
                    if i >= 1:
                        C.w(sync, "px", pxw + 16 * ntc * i)
                    C.dinc(sync.dma_start(out=wtmp_b[:], in_=wh_t[i, :, :]), "lx")
            b.sync(_)

            def _(v, lxw=lxw):
                for i in range(nch):
                    C.w(v, "lx", lxw + 16 * (i + 1))
                    C.inc(v.tensor_copy(wtmp_f[:], wtmp_b[:]), "vc")
            b.vector(_)

            def _(g2, vw=vw):
                for i in range(nch):
                    C.w(g2, "vc", vw + i + 1)
                    for j in range(ntc):
                        C.dinc(g2.dma_start(out=xdst[i * ntc + j, :, :],
                                            in_=wtmp_f[:, j * D:(j + 1) * D]), "px")
            b.gpsimd(_)
            full_barrier(C)
            return

        if SUBSTAGE == 1:
            dbg = PE(nc.sbuf_tensor(f"f{call}_dbg", [128, 2, D], F32))
            st_d = C.stream("l")
            v0d = C.n["vc"]

            def _(sync, st_d=st_d, v0d=v0d):
                for i in range(NT):
                    if i >= 2:
                        C.w(sync, "vc", v0d + (i - 1))
                    st_d.issue(sync.dma_start(out=xa[:, i % 2, :], in_=xsrc[i, :, :]), i)
            b.sync(_)

            def _(s, st_d=st_d):
                for i in range(NT):
                    st_d.wait(s, i)
                    C.inc(s.activation(out=qtile[:, i % 2, :], in_=xa[:, i % 2, :],
                                       func=AF.Identity,
                                       scale=a_c[:, i:i + 1], bias=bq_c[:, i:i + 1]), "sc")
            b.scalar(_)
            s0d = C.n["sc"] - NT

            def _(v, s0d=s0d, v0d=v0d):
                for i in range(NT):
                    C.w(v, "sc", s0d + i + 1)
                    C.inc(v.tensor_scalar(out=dbg[:, i % 2, :], in0=qtile[:, i % 2, :],
                                          scalar1=sc16[:, 0:1], scalar2=None,
                                          op0=ALU.mult), "vc")
            b.vector(_)
            v1d = C.n["vc"] - NT

            def _(g2, v1d=v1d):
                for i in range(NT):
                    C.w(g2, "vc", v1d + i + 1)
                    C.dinc(g2.dma_start(out=xdst[i, :, :], in_=dbg[:, i % 2, :]), "px")
            b.gpsimd(_)
            full_barrier(C)
            return

        w_hbm = (io["wupT_h"], io["wgT_h"])
        w_sp = (sp_up, sp_gate)
        for quarter in range(NG // 2):
            groups = [2 * quarter, 2 * quarter + 1]
            for wi in range(2):
                wh = w_hbm[wi].rearrange("(c p) e -> p c e", p=128)
                stwm = C.stream("l2")
                stsp = C.stream("p")
                t0n = C.n["tc"]
                v0 = C.n["vc"]
                c10 = C.n["c1"]

                def _(sync, wh=wh, t0n=t0n, stwm=stwm):
                    for e in range(IC):
                        if e < 2:
                            C.w(sync, "tc", t0n)
                        else:
                            C.w(sync, "tc", t0n + (e - 1))
                        stwm.issue(sync.dma_start(out=wst[:, e % 2, :, :],
                                                  in_=wh[:, :, e * 128:(e + 1) * 128]), e)
                b.sync(_)

                def _(t, v0=v0, groups=groups, stwm=stwm):
                    for e in range(IC):
                        stwm.wait(t, e)
                        if e >= 2:
                            C.w(t, "vc", v0 + (e - 1) * 2)
                        for d in range(DC):
                            for gi in range(2):
                                ins = t.matmul(ps[2 * (e % 2) + gi][:],
                                               wst[:, e % 2, d, :],
                                               xqT[:, d, groups[gi] * GT:(groups[gi] + 1) * GT],
                                               start=(d == 0), stop=(d == DC - 1))
                        C.inc(ins, "tc")
                b.tensor(_)

                def _(v, t0n=t0n, wi=wi, groups=groups, stsp=stsp):
                    for e in range(IC):
                        C.w(v, "tc", t0n + e + 1)
                        if e >= 2:
                            stsp.wait(v, e - 2, ndma=2)
                        for gi in range(2):
                            g = groups[gi]
                            col = e * NG + g
                            ncol = IC * NG + col
                            v.tensor_copy(ev[:, 2 * (e % 2) + gi, :], ps[2 * (e % 2) + gi][:])
                            v.tensor_scalar(out=junk[:], in0=ev[:, 2 * (e % 2) + gi, :],
                                            scalar1=1.0, scalar2=None, op0=ALU.mult,
                                            op1=ALU.max, accum_out=amx[:, wi, col:col + 1])
                            C.inc(v.tensor_scalar(out=junk[:], in0=ev[:, 2 * (e % 2) + gi, :],
                                                  scalar1=-1.0, scalar2=None, op0=ALU.mult,
                                                  op1=ALU.max,
                                                  accum_out=amx[:, wi, ncol:ncol + 1]), "vc")
                b.vector(_)

                def _(g2, v0=v0, wi=wi, groups=groups, stsp=stsp):
                    for e in range(IC):
                        for gi in range(2):
                            C.w(g2, "vc", v0 + e * 2 + gi + 1)
                            stsp.issue(g2.dma_start(out=w_sp[wi][e, groups[gi], :, :],
                                                    in_=ev[:, 2 * (e % 2) + gi, :]), e)
                b.gpsimd(_)

        g_pe, g_po = C.n["pe"], C.n["po"]

        def _(v, g_pe=g_pe, g_po=g_po):
            C.w(v, "pe", g_pe)
            C.w(v, "po", g_po)
            C.w(v, "vc", C.n["vc"])
            C.inc(v.tensor_reduce(out=colv[:, 0:1], in_=amx[:, 0, :], axis=AX, op=ALU.max), "vc")
            C.inc(v.tensor_reduce(out=colv[:, 1:2], in_=amx[:, 1, :], axis=AX, op=ALU.max), "vc")
        b.vector(_)

        emit_allreduce_max(C, colv, 2)
        sp_bc = C.n["px"]

        def _(v, sp_bc=sp_bc):
            C.w(v, "px", sp_bc)
            C.inc(v.tensor_scalar(out=sc16[:, 2:3], in0=bc16[:, 0:1], scalar1=1.0 / QMAX,
                                  scalar2=1e-12, op0=ALU.mult, op1=ALU.max), "vc")
            C.w(v, "vc", C.n["vc"])
            v.reciprocal(sc16[:, 3:4], sc16[:, 2:3])
            C.inc(v.tensor_scalar(out=sc16[:, 4:5], in0=bc16[:, 1:2], scalar1=1.0 / QMAX,
                                  scalar2=1e-12, op0=ALU.mult, op1=ALU.max), "vc")
            C.w(v, "vc", C.n["vc"])
            C.inc(v.reciprocal(sc16[:, 5:6], sc16[:, 4:5]), "vc")
        b.vector(_)
        full_barrier(C)

        if SUBSTAGE == 10:
            dxf = PE(nc.sbuf_tensor("dbg_xf", [128, 2, 1024], F32))
            vd = C.n["vc"]
            pxd = C.n["px"]

            def _(v, pxd=pxd):
                for k in range(32):
                    d, q = divmod(k, 4)
                    if k >= 2:
                        C.w(v, "px", pxd + 16 * (k - 1))
                    C.inc(v.tensor_copy(dxf[:, k % 2, :],
                                        xqT[:, d, q * 1024:(q + 1) * 1024]), "vc")
            b.vector(_)

            def _(g2, vd=vd):
                for k in range(32):
                    C.w(g2, "vc", vd + k + 1)
                    C.dinc(g2.dma_start(out=xdst[k, :, :], in_=dxf[:, k % 2, :]), "px")
            b.gpsimd(_)
            full_barrier(C)
            return

        if SUBSTAGE in (7, 8, 9):
            e_base = {7: 0, 8: 8, 9: 16}[SUBSTAGE]
            dzb = PE(nc.sbuf_tensor("dbg_zb", [128, 2, GT], BF16))
            dzf = PE(nc.sbuf_tensor("dbg_zf", [128, 2, GT], F32))
            lxd = C.n["lx"]
            vd = C.n["vc"]
            pxd = C.n["px"]

            def _(sync, lxd=lxd, pxd=pxd, e_base=e_base):
                for k in range(64):
                    e, g = divmod(k, 8)
                    if k >= 2:
                        C.w(sync, "px", pxd + 16 * (k - 1))
                    C.dinc(sync.dma_start(out=dzb[:, k % 2, :],
                                          in_=sp_up[e_base + e, g, :, :]), "lx")
            b.sync(_)

            def _(v, lxd=lxd):
                for k in range(64):
                    C.w(v, "lx", lxd + 16 * (k + 1))
                    C.inc(v.tensor_copy(dzf[:, k % 2, :], dzb[:, k % 2, :]), "vc")
            b.vector(_)

            def _(g2, vd=vd):
                for k in range(64):
                    C.w(g2, "vc", vd + k + 1)
                    ti, half = divmod(k, 2)
                    C.dinc(g2.dma_start(out=xdst[ti, :, half * GT:(half + 1) * GT],
                                        in_=dzf[:, k % 2, :]), "px")
            b.gpsimd(_)
            full_barrier(C)
            return

        if SUBSTAGE == 6:
            def _(g2):
                for s in ("vc",):
                    C.w(g2, s, C.n[s])
                C.dinc(g2.dma_start(out=xdst[0, :, 0:384], in_=amx[:, 0, :]), "px")
                C.dinc(g2.dma_start(out=xdst[0, :, 384:768], in_=amx[:, 1, :]), "px")
                C.dinc(g2.dma_start(out=xdst[0, :, 768:776], in_=colv[:, 0:8]), "px")
                C.dinc(g2.dma_start(out=xdst[0, :, 776:792], in_=bc16[:, :]), "px")
                C.dinc(g2.dma_start(out=xdst[0, :, 792:808], in_=sc16[:, :]), "px")
            b.gpsimd(_)
            full_barrier(C)
            return

    ntile = IC * NG
    with contextlib.ExitStack() as pes:
        PE = pes.enter_context
        gtile = PE(nc.sbuf_tensor(f"f{call}_gt", [128, 4, GT], BF16))
        gi_t = PE(nc.sbuf_tensor(f"f{call}_gi", [128, 2, GT], I16))
        sig0 = PE(nc.sbuf_tensor(f"f{call}_sg0", [128, 2, GT], F32))
        sigu = PE(nc.sbuf_tensor(f"f{call}_sgu", [128, 2, GT], U16))
        tp = PE(nc.sbuf_tensor(f"f{call}_tp", [128, 2, GT], F32))
        tmaxc = PE(nc.sbuf_tensor(f"f{call}_tmx", [128, ntile], F32))
        uptile = PE(nc.sbuf_tensor(f"f{call}_up", [128, 2, GT], BF16))
        g2t = PE(nc.sbuf_tensor(f"f{call}_g2", [128, 2, GT], I16))
        uit = PE(nc.sbuf_tensor(f"f{call}_ui", [128, 2, GT], I16))
        ptile = PE(nc.sbuf_tensor(f"f{call}_p", [128, 2, GT], F32))

        stga = C.stream("l")
        stgb = C.stream("l2")
        stspc = C.stream("p")
        v0 = C.n["vc"]
        s0 = C.n["sc"]

        def _(sync, v0=v0, stga=stga, stgb=stgb):
            for k in range(ntile):
                e, g = divmod(k, NG)
                if k >= 4:
                    C.w(sync, "vc", v0 + (k - 3) * 2)
                (stga, stgb)[k % 2].issue(
                    sync.dma_start(out=gtile[:, k % 4, :], in_=sp_gate[e, g, :, :]), k // 2)
        b.sync(_)

        def _(v, s0=s0, stga=stga, stgb=stgb, stspc=stspc):
            for k in range(ntile):
                sl = k % 2
                (stga, stgb)[k % 2].wait(v, k // 2)
                if k >= 2:
                    stspc.wait(v, k - 2, ndma=2)
                C.inc(v.tensor_scalar(out=gi_t[:, sl, :], in0=gtile[:, k % 4, :],
                                      scalar1=sc16[:, 5:6], scalar2=None, op0=ALU.mult), "vc")
                C.w(v, "sc", s0 + k + 1)
                v.tensor_scalar(out=sigu[:, sl, :], in0=sig0[:, sl, :], scalar1=65536.0,
                                scalar2=65535.0, op0=ALU.mult, op1=ALU.min)
                v.tensor_tensor(out=tp[:, sl, :], in0=gi_t[:, sl, :], in1=sigu[:, sl, :],
                                op=ALU.mult)
                C.inc(v.tensor_reduce(out=tmaxc[:, k:k + 1], in_=tp[:, sl, :], axis=AX,
                                      op=ALU.max, apply_absolute_value=True), "vc")
        b.vector(_)

        def _(s, v0=v0):
            for k in range(ntile):
                C.w(s, "vc", v0 + 2 * k + 1)
                C.inc(s.activation(out=sig0[:, k % 2, :], in_=gi_t[:, k % 2, :],
                                   func=AF.Sigmoid, scale=sc16[:, 4:5]), "sc")
        b.scalar(_)

        def _(g2, v0=v0, stspc=stspc):
            for k in range(ntile):
                e, g = divmod(k, NG)
                C.w(g2, "vc", v0 + 2 * (k + 1))
                stspc.issue(g2.dma_start(out=sp_gi[e, g, :, :], in_=gi_t[:, k % 2, :]), k)
                stspc.issue(g2.dma_start(out=sp_sig[e, g, :, :], in_=sigu[:, k % 2, :]), k)
        b.gpsimd(_)

        def _(v, stspc=stspc):
            stspc.wait_all(v)
            C.w(v, "vc", C.n["vc"])
            C.inc(v.tensor_reduce(out=colv[:, 0:1], in_=tmaxc[:], axis=AX, op=ALU.max), "vc")
        b.vector(_)
        emit_allreduce_max(C, colv, 1)
        sp_bc = C.n["px"]

        def _(v, sp_bc=sp_bc):
            C.w(v, "px", sp_bc)
            C.inc(v.tensor_scalar(out=sc16[:, 6:7], in0=sc16[:, 4:5],
                                  scalar1=float(2.0 ** -16),
                                  scalar2=None, op0=ALU.mult), "vc")
            C.w(v, "vc", C.n["vc"])
            C.inc(v.tensor_mul(sc16[:, 7:8], bc16[:, 0:1], sc16[:, 6:7]), "vc")
            C.w(v, "vc", C.n["vc"])
            C.inc(v.tensor_scalar(out=sc16[:, 7:8], in0=sc16[:, 7:8], scalar1=1.0 / QMAX,
                                  scalar2=1e-12, op0=ALU.mult, op1=ALU.max), "vc")
            C.w(v, "vc", C.n["vc"])
            C.inc(v.reciprocal(sc16[:, 8:9], sc16[:, 7:8]), "vc")
            C.w(v, "vc", C.n["vc"])
            C.inc(v.tensor_mul(sc16[:, 8:9], sc16[:, 8:9], sc16[:, 6:7]), "vc")
        b.vector(_)
        full_barrier(C)

        stda = C.stream("l")
        stdb = C.stream("l2")
        stdc = C.stream("w")
        stspd = C.stream("p")
        v0 = C.n["vc"]

        def _(sync, v0=v0, stda=stda, stdb=stdb, stdc=stdc):
            for k in range(ntile):
                e, g = divmod(k, NG)
                if k >= 2:
                    C.w(sync, "vc", v0 + (k - 1))
                stda.issue(sync.dma_start(out=gi_t[:, k % 2, :], in_=sp_gi[e, g, :, :]), k)
                stdb.issue(sync.dma_start(out=sigu[:, k % 2, :], in_=sp_sig[e, g, :, :]), k)
                stdc.issue(sync.dma_start(out=uptile[:, k % 2, :], in_=sp_up[e, g, :, :]), k)
        b.sync(_)

        def _(v, stda=stda, stdb=stdb, stdc=stdc, stspd=stspd):
            for k in range(ntile):
                sl = k % 2
                stda.wait(v, k)
                stdb.wait(v, k)
                stdc.wait(v, k)
                if k >= 2:
                    stspd.wait(v, k - 2)
                v.tensor_tensor(out=tp[:, sl, :], in0=gi_t[:, sl, :], in1=sigu[:, sl, :],
                                op=ALU.mult)
                v.tensor_scalar(out=g2t[:, sl, :], in0=tp[:, sl, :], scalar1=sc16[:, 8:9],
                                scalar2=None, op0=ALU.mult)
                v.tensor_scalar(out=uit[:, sl, :], in0=uptile[:, sl, :], scalar1=sc16[:, 3:4],
                                scalar2=None, op0=ALU.mult)
                v.tensor_tensor(out=ptile[:, sl, :], in0=g2t[:, sl, :], in1=uit[:, sl, :],
                                op=ALU.mult)
                C.inc(v.tensor_reduce(out=tmaxc[:, k:k + 1], in_=ptile[:, sl, :], axis=AX,
                                      op=ALU.max, apply_absolute_value=True), "vc")
        b.vector(_)

        def _(g2, v0=v0, stspd=stspd):
            for k in range(ntile):
                e, g = divmod(k, NG)
                C.w(g2, "vc", v0 + k + 1)
                stspd.issue(g2.dma_start(out=sp_p[e, g, :, :], in_=ptile[:, k % 2, :]), k)
        b.gpsimd(_)

        def _(v, stspd=stspd):
            stspd.wait_all(v)
            C.w(v, "vc", C.n["vc"])
            C.inc(v.tensor_reduce(out=colv[:, 0:1], in_=tmaxc[:], axis=AX, op=ALU.max), "vc")
        b.vector(_)
        emit_allreduce_max(C, colv, 1)
        sp_bc = C.n["px"]

        def _(v, sp_bc=sp_bc):
            C.w(v, "px", sp_bc)
            C.inc(v.tensor_mul(sc16[:, 9:10], sc16[:, 7:8], sc16[:, 2:3]), "vc")
            C.w(v, "vc", C.n["vc"])
            C.inc(v.tensor_mul(sc16[:, 10:11], sc16[:, 9:10], bc16[:, 0:1]), "vc")
            C.w(v, "vc", C.n["vc"])
            C.inc(v.tensor_scalar(out=sc16[:, 10:11], in0=sc16[:, 10:11], scalar1=1.0 / QMAX,
                                  scalar2=1e-12, op0=ALU.mult, op1=ALU.max), "vc")
            C.w(v, "vc", C.n["vc"])
            C.inc(v.reciprocal(sc16[:, 11:12], sc16[:, 10:11]), "vc")
            C.w(v, "vc", C.n["vc"])
            C.inc(v.tensor_mul(sc16[:, 11:12], sc16[:, 11:12], sc16[:, 9:10]), "vc")
        b.vector(_)
        full_barrier(C)

        with contextlib.ExitStack() as pes2:
            PE2 = pes2.enter_context
            eptile = PE2(nc.sbuf_tensor(f"f{call}_ep", [128, 4, GT], F32))
            ogrp = PE2(nc.sbuf_tensor(f"f{call}_og", [128, 2, IC, GT], BF16))
            wdsb = PE2(nc.sbuf_tensor(f"f{call}_wd", [128, IC, D], BF16))
            oi_t = PE2(nc.sbuf_tensor(f"f{call}_oi", [128, 2, GT], I16))
            xrt = PE2(nc.sbuf_tensor(f"f{call}_xr", [128, 4, D], F32))
            xnew = PE2(nc.sbuf_tensor(f"f{call}_xn", [128, 4, D], F32))
            wdh = io["wdT_h"].rearrange("(c p) e -> c p e", p=128)

            def _(sync):
                for e in range(IC):
                    C.dinc(sync.dma_start(out=wdsb[:, e, :], in_=wdh[e, :, :]), "lx")
                C.fence(sync, "lx")
            b.sync(_)
            ld_wd = C.n["lx"]

            stpa = C.stream("l")
            stpb = C.stream("w")
            v0 = C.n["vc"]
            c1_0 = C.n["c1"]
            px0 = C.n["px"]
            lx0 = C.n["lx"]

            vdone = {}
            S = {}
            cnt = v0
            for gi in range(NG + 1):
                if gi < NG:
                    for e in range(IC):
                        cnt += 2
                        vdone[gi * IC + e] = cnt
                if gi >= 1:
                    S[gi - 1] = cnt
                    cnt += 4

            def _(sync):
                for g in range(NG):
                    for e in range(IC):
                        k = g * IC + e
                        if k >= 4:
                            C.w(sync, "vc", vdone[k - 4])
                        (stpa, stpb)[k % 2].issue(
                            sync.dma_start(out=eptile[:, k % 4, :],
                                           in_=sp_p[e, g, :, :]), k // 2)
                    if g >= 1:
                        C.w(sync, "vc", S[g - 1] + 4)
                    for tt in range(4):
                        C.dinc(sync.dma_start(out=xrt[:, tt, :], in_=xsrc[g * 4 + tt, :, :]),
                               "lx")
            b.sync(_)

            def _(v):
                for gi in range(NG + 1):
                    if gi < NG:
                        for e in range(IC):
                            k = gi * IC + e
                            (stpa, stpb)[k % 2].wait(v, k // 2)
                            if e == 0 and gi >= 2:
                                C.w(v, "c1", c1_0 + 2 * IC * (gi - 1))
                            C.inc(v.tensor_scalar(out=oi_t[:, k % 2, :],
                                                  in0=eptile[:, k % 4, :],
                                                  scalar1=sc16[:, 11:12], scalar2=None,
                                                  op0=ALU.mult), "vc")
                            C.inc(v.tensor_scalar(out=ogrp[:, gi % 2, e, :],
                                                  in0=oi_t[:, k % 2, :],
                                                  scalar1=sc16[:, 10:11], scalar2=None,
                                                  op0=ALU.mult), "vc")
                    if gi >= 1:
                        g = gi - 1
                        if g >= 1:
                            C.w(v, "px", px0 + 64 * g)
                        C.w(v, "lx", lx0 + 64 * (g + 1))
                        for pair in range(2):
                            C.w(v, "c1", c1_0 + 2 * IC * g + IC * (pair + 1))
                            for t2 in range(2):
                                tt = pair * 2 + t2
                                v.scalar_tensor_tensor(out=xnew[:, tt, 0:GT],
                                                       in0=ps[2 * t2][:], scalar=0.5,
                                                       in1=xrt[:, tt, 0:GT],
                                                       op0=ALU.mult, op1=ALU.add)
                                C.inc(v.scalar_tensor_tensor(out=xnew[:, tt, GT:D],
                                                             in0=ps[2 * t2 + 1][:],
                                                             scalar=0.5,
                                                             in1=xrt[:, tt, GT:D],
                                                             op0=ALU.mult, op1=ALU.add),
                                      "vc")
            b.vector(_)

            def _(t):
                C.w(t, "lx", ld_wd)
                for g in range(NG):
                    C.w(t, "vc", vdone[g * IC + IC - 1])
                    if g >= 1:
                        C.w(t, "vc", S[g - 1] + 4)
                    for pair in range(2):
                        if pair == 1:
                            C.w(t, "vc", S[g] + 2)
                        for e in range(IC):
                            for t2 in range(2):
                                tt = pair * 2 + t2
                                t.matmul(ps[2 * t2][:],
                                         ogrp[:, g % 2, e, tt * 128:(tt + 1) * 128],
                                         wdsb[:, e, 0:GT],
                                         start=(e == 0), stop=(e == IC - 1))
                                ins = t.matmul(ps[2 * t2 + 1][:],
                                               ogrp[:, g % 2, e, tt * 128:(tt + 1) * 128],
                                               wdsb[:, e, GT:D],
                                               start=(e == 0), stop=(e == IC - 1))
                            C.inc(ins, "c1")
            b.tensor(_)

            def _(g2):
                for g in range(NG):
                    for tt in range(4):
                        C.w(g2, "vc", S[g] + tt + 1)
                        C.dinc(g2.dma_start(out=xdst[g * 4 + tt, :, :],
                                            in_=xnew[:, tt, :]), "px")
            b.gpsimd(_)
            full_barrier(C)

    full_barrier(C)



def emit_attn(C, xsrc, xdst, io):
    nc, b = C.nc, C.block
    bufs = C.bufs
    ps, psT2, psS = bufs["ps"], bufs["psT2"], bufs["psS"]
    identb = bufs["identb"]
    mu0_c, r0_c = bufs["mu0_c"], bufs["r0_c"]
    b0_c, r1_c = bufs["b0_c"], bufs["r1_c"]
    exp_ls = bufs["exp_ls"]
    bom_row, b2_row = bufs["bom_row"], bufs["b2_row"]
    ones_row, ones_colb = bufs["ones_row"], bufs["ones_colb"]
    const_bf, colsum_bf = bufs["const_bf"], bufs["colsum_bf"]
    carry, carry_r, carry_i = bufs["carry"], bufs["carry_r"], bufs["carry_i"]
    off_phi, off_r, off_i = bufs["off_phi"], bufs["off_r"], bufs["off_i"]
    colv = bufs["colv"]
    sp_phi, sp_xln = io["sp_phi"], io["sp_xln"]
    sp_cos, sp_sin = io["sp_cos"], io["sp_sin"]
    sp_cmr, sp_cmi = io["sp_cmr"], io["sp_cmi"]
    sp_Sr, sp_Si = io["sp_Sr"], io["sp_Si"]

    emit_ln0_stats(C, xsrc, False, mu0_c, r0_c)

    def _(v):
        C.w(v, "vc", C.n["vc"])
        C.inc(v.tensor_mul(b0_c[:], mu0_c[:], r0_c[:]), "vc")
        C.w(v, "vc", C.n["vc"])
        v.tensor_scalar(out=b0_c[:], in0=b0_c[:], scalar1=-1.0, scalar2=None, op0=ALU.mult)
        v.tensor_scalar(out=r1_c[:], in0=r0_c[:], scalar1=1.0, scalar2=None, op0=ALU.add)
        v.memset(carry[:], 0.0)
        v.memset(carry_r[:], 0.0)
        C.inc(v.memset(carry_i[:], 0.0), "vc")
    b.vector(_)
    full_barrier(C)

    with contextlib.ExitStack() as pes:
        PE = pes.enter_context
        xa = bufs["xstats"]
        xlntm = PE(nc.sbuf_tensor("a_xlntm", [128, 2, D], BF16))
        xlnT = PE(nc.sbuf_tensor("a_xlnT", [128, DC, GT], BF16))
        womsb = PE(nc.sbuf_tensor("a_wom", [128, DC, D], BF16))
        oma = PE(nc.sbuf_tensor("a_oma", [128, DC, GT], F32))
        invsqt = PE(nc.sbuf_tensor("a_invsq", [128, GT], F32))

        womt = io["womT_h"].rearrange("(c p) e -> c p e", p=128)

        def _(sync):
            for d in range(DC):
                C.dinc(sync.dma_start(out=womsb[:, d, :], in_=womt[d, :, :]), "lx")
            C.fence(sync, "lx")
        b.sync(_)
        ld_wom = C.n["lx"]

        for g in range(NG):
            stxa = C.stream("l")
            lx0 = C.n["lx"]
            s0 = C.n["sc"]
            v0 = C.n["vc"]
            t0n = C.n["tc"]

            def _(sync, g=g, v0=v0, stxa=stxa):
                C.dinc(sync.dma_start(out=invsqt[:],
                                      in_=io["invsq_in"][0:1, g * GT:(g + 1) * GT]
                                      .broadcast_to((128, GT))), "lx")
                for tt in range(4):
                    if tt >= 2:
                        C.w(sync, "vc", v0 + (tt - 1) * DC)
                    stxa.issue(sync.dma_start(out=xa[:, tt % 2, :], in_=xsrc[g * 4 + tt, :, :]), tt)
            b.sync(_)

            def _(s, g=g, stxa=stxa):
                for tt in range(4):
                    i = g * 4 + tt
                    stxa.wait(s, tt)
                    C.inc(s.activation(out=xlntm[:, tt % 2, :], in_=xa[:, tt % 2, :],
                                       func=AF.Identity,
                                       scale=r0_c[:, i:i + 1], bias=b0_c[:, i:i + 1]), "sc")
            b.scalar(_)

            def _(t, s0=s0, v0=v0):
                for tt in range(4):
                    C.w(t, "sc", s0 + tt + 1)
                    for dc in range(DC):
                        k = tt * DC + dc
                        if k >= 2:
                            C.w(t, "vc", v0 + k - 1)
                        C.inc(t.transpose(psT2[k % 2][:, 0:128],
                                          xlntm[:, tt % 2, dc * 128:(dc + 1) * 128],
                                          identb[:]), "tc")
            b.tensor(_)

            def _(v, t0n=t0n):
                for tt in range(4):
                    for dc in range(DC):
                        k = tt * DC + dc
                        C.w(v, "tc", t0n + k + 1)
                        C.inc(v.tensor_copy(xlnT[:, dc, tt * 128:(tt + 1) * 128],
                                            psT2[k % 2][:, 0:128]), "vc")
            b.vector(_)
            v_xlnT = C.n["vc"]
            t_tr = C.n["tc"]

            def _(g2, g=g, v_xlnT=v_xlnT):
                C.w(g2, "vc", v_xlnT)
                C.dinc(g2.dma_start(out=sp_xln[g, :, :, :], in_=xlnT[:]), "px")
            b.gpsimd(_)

            s_ev0 = C.n["sc"]

            def _(t, v_xlnT=v_xlnT, t_tr=t_tr, s_ev0=s_ev0):
                C.w(t, "vc", v_xlnT)
                C.w(t, "lx", ld_wom)
                for ec in range(DC):
                    bank = 4 + (ec % 2)
                    if ec >= 2:
                        C.w(t, "sc", s_ev0 + ec - 1)
                    for d in range(DC):
                        t.matmul(ps[bank][:], womsb[:, d, ec * 128:(ec + 1) * 128],
                                 xlnT[:, d, :], start=(d == 0), stop=False)
                    C.inc(t.matmul(ps[bank][:], bom_row[0:1, ec * 128:(ec + 1) * 128],
                                   ones_row[0:1, :], start=False, stop=True), "tc")
            b.tensor(_)
            t_om0 = t_tr

            def _(s, t_om0=t_om0):
                for ec in range(DC):
                    C.w(s, "tc", t_om0 + ec + 1)
                    C.inc(s.activation(out=oma[:, ec, :], in_=ps[4 + (ec % 2)][:],
                                       func=AF.Identity, scale=exp_ls[:, ec:ec + 1]), "sc")
            b.scalar(_)

            def _(v, g=g, lx0=lx0):
                C.w(v, "sc", C.n["sc"])
                C.w(v, "lx", lx0 + 16)
                v.tensor_tensor(out=oma[:], in0=oma[:],
                                in1=invsqt[:].rearrange("p (o t) -> p o t", o=1)
                                .broadcast_to((128, DC, GT)),
                                op=ALU.mult)
                for ec in range(DC):
                    ins_scan = v.tensor_tensor_scan(out=oma[:, ec, :], data0=oma[:, ec, :],
                                                    data1=oma[:, ec, :],
                                                    initial=carry[:, ec:ec + 1],
                                                    op0=ALU.add, op1=ALU.bypass)
                C.w(v, "vc", C.inc(ins_scan, "vc"))
                C.inc(v.tensor_copy(carry[:], oma[:, :, GT - 1]), "vc")
            b.vector(_)

            def _(g2, g=g):
                C.w(g2, "vc", C.n["vc"])
                C.dinc(g2.dma_start(out=sp_phi[g, :, :, :], in_=oma[:]), "px")
            b.gpsimd(_)
            full_barrier(C)

        emit_carry_gather(C, [carry], [off_phi])

        kint = PE(nc.sbuf_tensor("a_k", [128, DC, GT], I32))
        phi1 = PE(nc.sbuf_tensor("a_phi1", [128, DC, GT], F32))
        cosb = PE(nc.sbuf_tensor("a_cos", [128, DC, GT], BF16))
        sinb = PE(nc.sbuf_tensor("a_sin", [128, DC, GT], BF16))
        cmr = PE(nc.sbuf_tensor("a_cmr", [128, DC, GT], BF16))
        cmi = PE(nc.sbuf_tensor("a_cmi", [128, DC, GT], BF16))
        Srb = PE(nc.sbuf_tensor("a_Sr", [128, DC, GT], F32))
        Sib = PE(nc.sbuf_tensor("a_Si", [128, DC, GT], F32))

        for g in range(NG):
            lx0 = C.n["lx"]

            def _(sync, g=g):
                C.dinc(sync.dma_start(out=oma[:], in_=sp_phi[g, :, :, :]), "lx")
                C.dinc(sync.dma_start(out=xlnT[:], in_=sp_xln[g, :, :, :]), "lx")
            b.sync(_)

            def _(v, lx0=lx0):
                C.w(v, "lx", lx0 + 32)
                for ec in range(DC):
                    v.tensor_scalar(out=oma[:, ec, :], in0=oma[:, ec, :],
                                    scalar1=off_phi[:, ec:ec + 1], scalar2=None, op0=ALU.add)
                C.inc(v.tensor_copy(junk_one(C), oma[:, 0, 0:1]), "vc")
            b.vector(_)
            v_phi = C.n["vc"]

            def _(s, v_phi=v_phi):
                C.w(s, "vc", v_phi)
                C.inc(s.activation(out=kint[:], in_=oma[:], func=AF.Identity,
                                   scale=float(1.0 / TWO_PI)), "sc")
            b.scalar(_)
            s_k = C.n["sc"]

            def _(v, s_k=s_k):
                C.w(v, "sc", s_k)
                v.scalar_tensor_tensor(out=phi1[:], in0=kint[:], scalar=-RC1, in1=oma[:],
                                       op0=ALU.mult, op1=ALU.add)
                v.scalar_tensor_tensor(out=phi1[:], in0=kint[:], scalar=-RC2, in1=phi1[:],
                                       op0=ALU.mult, op1=ALU.add)
                v.tensor_scalar(out=Srb[:], in0=phi1[:], scalar1=PIF, scalar2=float(-TWO_PI),
                                op0=ALU.is_gt, op1=ALU.mult)
                v.tensor_add(phi1[:], phi1[:], Srb[:])
                v.tensor_scalar(out=Srb[:], in0=phi1[:], scalar1=-PIF, scalar2=float(TWO_PI),
                                op0=ALU.is_lt, op1=ALU.mult)
                v.tensor_add(phi1[:], phi1[:], Srb[:])
                C.inc(v.tensor_scalar(out=phi1[:], in0=phi1[:], scalar1=PIF,
                                      scalar2=-PIF, op0=ALU.min, op1=ALU.max), "vc")
            b.vector(_)
            v_ph1 = C.n["vc"]

            def _(s, v_ph1=v_ph1):
                C.w(s, "vc", v_ph1)
                C.inc(s.activation(out=sinb[:], in_=phi1[:], func=AF.Sin), "sc")
            b.scalar(_)
            s_sin = C.n["sc"]

            def _(v, s_sin=s_sin):
                C.w(v, "sc", s_sin)
                v.tensor_scalar(out=oma[:], in0=phi1[:], scalar1=PIH,
                                scalar2=float(-TWO_PI), op0=ALU.is_gt, op1=ALU.mult)
                C.inc(v.scalar_tensor_tensor(out=oma[:], in0=phi1[:], scalar=PIH,
                                             in1=oma[:], op0=ALU.add, op1=ALU.add), "vc")
            b.vector(_)
            v_psi = C.n["vc"]

            def _(s, v_psi=v_psi):
                C.w(s, "vc", v_psi)
                C.inc(s.activation(out=cosb[:], in_=oma[:], func=AF.Sin), "sc")
            b.scalar(_)
            s_cos = C.n["sc"]

            def _(v, s_cos=s_cos):
                C.w(v, "sc", s_cos)
                v.tensor_tensor(out=cmr[:], in0=xlnT[:], in1=cosb[:], op=ALU.mult)
                v.tensor_tensor(out=cmi[:], in0=xlnT[:], in1=sinb[:], op=ALU.mult)
                for ec in range(DC):
                    v.tensor_tensor_scan(out=Srb[:, ec, :], data0=cmr[:, ec, :],
                                         data1=cmr[:, ec, :], initial=carry_r[:, ec:ec + 1],
                                         op0=ALU.add, op1=ALU.bypass)
                    ins_scan2 = v.tensor_tensor_scan(out=Sib[:, ec, :], data0=cmi[:, ec, :],
                                                     data1=cmi[:, ec, :],
                                                     initial=carry_i[:, ec:ec + 1],
                                                     op0=ALU.add, op1=ALU.bypass)
                C.w(v, "vc", C.inc(ins_scan2, "vc"))
                v.tensor_copy(carry_r[:], Srb[:, :, GT - 1])
                C.inc(v.tensor_copy(carry_i[:], Sib[:, :, GT - 1]), "vc")
            b.vector(_)

            def _(g2, g=g):
                C.w(g2, "vc", C.n["vc"])
                C.dinc(g2.dma_start(out=sp_cos[g, :, :, :], in_=cosb[:]), "px")
                C.dinc(g2.dma_start(out=sp_sin[g, :, :, :], in_=sinb[:]), "px")
                C.dinc(g2.dma_start(out=sp_cmr[g, :, :, :], in_=cmr[:]), "px")
                C.dinc(g2.dma_start(out=sp_cmi[g, :, :, :], in_=cmi[:]), "px")
                C.dinc(g2.dma_start(out=sp_Sr[g, :, :, :], in_=Srb[:]), "px")
                C.dinc(g2.dma_start(out=sp_Si[g, :, :, :], in_=Sib[:]), "px")
            b.gpsimd(_)
            full_barrier(C)

        emit_carry_gather(C, [carry_r, carry_i], [off_r, off_i])

    with contextlib.ExitStack() as pes:
        PE = pes.enter_context
        ctxb = PE(nc.sbuf_tensor("a3_ctx", [128, FC, GT], BF16))
        Srb = PE(nc.sbuf_tensor("a3_Sr", [128, DC, GT], F32))
        Sib = PE(nc.sbuf_tensor("a3_Si", [128, DC, GT], F32))
        cosb = PE(nc.sbuf_tensor("a3_cos", [128, DC, GT], BF16))
        sinb = PE(nc.sbuf_tensor("a3_sin", [128, DC, GT], BF16))
        tmp1 = PE(nc.sbuf_tensor("a3_t1", [128, GT], BF16))
        tmp2 = PE(nc.sbuf_tensor("a3_t2", [128, GT], BF16))
        invpt = PE(nc.sbuf_tensor("a3_invp", [128, GT], F32))
        w1st = PE(nc.sbuf_tensor("a3_w1", [128, 4, GT], BF16))
        w2sb = PE(nc.sbuf_tensor("a3_w2", [128, EC2, D], BF16))
        sqc = PE(nc.sbuf_tensor("a3_sq", [128, 2, GT], BF16))
        hT = PE(nc.sbuf_tensor("a3_hT", [128, 4, EC2, 128], BF16))
        stt4 = PE(nc.sbuf_tensor("a3_st", [128, 16], F32))
        rows_f = PE(nc.sbuf_tensor("a3_rowf", [1, GT], F32))
        negmu_rb = PE(nc.sbuf_tensor("a3_nmu", [1, GT], BF16))
        invr_rb = PE(nc.sbuf_tensor("a3_ivr", [1, GT], BF16))
        rows_f2 = PE(nc.sbuf_tensor("a3_rowf2", [1, GT], F32))
        xrt = PE(nc.sbuf_tensor("a3_xr", [128, 2, D], F32))
        xnew = PE(nc.sbuf_tensor("a3_xn", [128, 2, D], F32))

        w1gt = io["w1g_h"].rearrange("(c p) e -> c p e", p=128)
        w2t = io["w2T_h"].rearrange("(c p) e -> c p e", p=128)

        def _(sync):
            for e2c in range(EC2):
                C.dinc(sync.dma_start(out=w2sb[:, e2c, :], in_=w2t[e2c, :, :]), "lx")
            C.fence(sync, "lx")
        b.sync(_)
        ld_w2 = C.n["lx"]

        for g in range(NG):
            def _(sync, g=g):
                C.dinc(sync.dma_start(out=Srb[:], in_=sp_Sr[g, :, :, :]), "lx")
                C.dinc(sync.dma_start(out=Sib[:], in_=sp_Si[g, :, :, :]), "lx")
                C.dinc(sync.dma_start(out=cosb[:], in_=sp_cos[g, :, :, :]), "lx")
                C.dinc(sync.dma_start(out=sinb[:], in_=sp_sin[g, :, :, :]), "lx")
                C.dinc(sync.dma_start(out=ctxb[:, 0:DC, :], in_=sp_cmr[g, :, :, :]), "lx")
                C.dinc(sync.dma_start(out=ctxb[:, DC:2 * DC, :], in_=sp_cmi[g, :, :, :]), "lx")
                C.dinc(sync.dma_start(out=invpt[:],
                                      in_=io["invp_in"][0:1, g * GT:(g + 1) * GT]
                                      .broadcast_to((128, GT))), "lx")
                C.fence(sync, "lx")
            b.sync(_)
            ld_sp = C.n["lx"]

            def _(v, ld_sp=ld_sp):
                C.w(v, "lx", ld_sp)
                for ec in range(DC):
                    v.tensor_scalar(out=Srb[:, ec, :], in0=Srb[:, ec, :],
                                    scalar1=off_r[:, ec:ec + 1], scalar2=None, op0=ALU.add)
                    v.tensor_scalar(out=Sib[:, ec, :], in0=Sib[:, ec, :],
                                    scalar1=off_i[:, ec:ec + 1], scalar2=None, op0=ALU.add)
                for ec in range(DC):
                    v.tensor_tensor(out=tmp1[:], in0=Srb[:, ec, :], in1=cosb[:, ec, :], op=ALU.mult)
                    v.tensor_tensor(out=tmp2[:], in0=Sib[:, ec, :], in1=sinb[:, ec, :], op=ALU.mult)
                    v.tensor_add(tmp1[:], tmp1[:], tmp2[:])
                    v.tensor_tensor(out=ctxb[:, 2 * DC + ec, :], in0=tmp1[:], in1=invpt[:], op=ALU.mult)
                    v.tensor_tensor(out=tmp1[:], in0=Sib[:, ec, :], in1=cosb[:, ec, :], op=ALU.mult)
                    v.tensor_tensor(out=tmp2[:], in0=Srb[:, ec, :], in1=sinb[:, ec, :], op=ALU.mult)
                    v.tensor_sub(tmp1[:], tmp1[:], tmp2[:])
                    v.tensor_tensor(out=ctxb[:, 3 * DC + ec, :], in0=tmp1[:], in1=invpt[:], op=ALU.mult)
                C.inc(v.tensor_copy(junk_one(C), ctxb[:, FC - 1, 0:1]), "vc")
            b.vector(_)
            v_ctx = C.n["vc"]

            s_sq0 = C.n["sc"]
            c20 = C.n["c2"]

            def _(s, v_ctx=v_ctx, c20=c20):
                C.w(s, "vc", v_ctx)
                for fc in range(FC):
                    if fc >= 2:
                        C.w(s, "c2", c20 + (fc - 1))
                    C.inc(s.activation(out=sqc[:, fc % 2, :], in_=ctxb[:, fc, :],
                                       func=AF.Square), "sc")
            b.scalar(_)

            stw1a = C.stream("l")
            stw1b = C.stream("w")
            c10 = C.n["c1"]

            def _(sync, g=g, v_ctx=v_ctx, c10=c10, stw1a=stw1a, stw1b=stw1b):
                C.w(sync, "vc", v_ctx)
                for e2g in range(4):
                    for fc in range(FC):
                        k = e2g * FC + fc
                        if k >= 4:
                            C.w(sync, "c1", c10 + (k - 3))
                        st = (stw1a, stw1b)[k % 2]
                        st.issue(sync.dma_start(out=w1st[:, k % 4, :],
                                                in_=w1gt[fc, :, e2g * GT:(e2g + 1) * GT]), k // 2)
            b.sync(_)

            t_g0 = C.n["tc"]
            c10_t = c10
            c30 = C.n["c3"]
            v_rows = v_ctx + 12
            s_gelu0 = s_sq0 + FC + 1

            def _(t, v_ctx=v_ctx, stw1a=stw1a, stw1b=stw1b, s_sq0=s_sq0, v_rows=v_rows,
                  s_gelu0=s_gelu0, c30=c30):
                C.w(t, "vc", v_ctx)
                for e2g in range(4):
                    if e2g > 0:
                        C.w(t, "sc", s_gelu0 + 4 * e2g)
                    for fc in range(FC):
                        k = e2g * FC + fc
                        (stw1a, stw1b)[k % 2].wait(t, k // 2)
                        if e2g == 0:
                            C.w(t, "sc", s_sq0 + fc + 1)
                        for tt in range(4):
                            mm_w1 = t.matmul(ps[tt][:], ctxb[:, fc, tt * 128:(tt + 1) * 128],
                                             w1st[:, k % 4, :], start=(fc == 0), stop=False)
                            if e2g == 0:
                                t.matmul(psS[:, 2 * tt:2 * tt + 1],
                                         ctxb[:, fc, tt * 128:(tt + 1) * 128],
                                         ones_colb[:, 0:1],
                                         start=(fc == 0 and tt == 0), stop=False)
                                mm_sq = t.matmul(psS[:, 2 * tt + 1:2 * tt + 2],
                                                 sqc[:, fc % 2, tt * 128:(tt + 1) * 128],
                                                 ones_colb[:, 0:1], start=False,
                                                 stop=(fc == FC - 1 and tt == 3))
                        C.inc(mm_w1, "c1")
                        if e2g == 0:
                            C.inc(mm_sq, "c2")
                    if e2g == 0:
                        C.w(t, "vc", v_rows)
                    for tt in range(4):
                        t.matmul(ps[tt][:], negmu_rb[0:1, tt * 128:(tt + 1) * 128],
                                 colsum_bf[0:1, e2g * GT:(e2g + 1) * GT],
                                 start=False, stop=False)
                        last_r1 = t.matmul(ps[tt][:], invr_rb[0:1, tt * 128:(tt + 1) * 128],
                                           const_bf[0:1, e2g * GT:(e2g + 1) * GT],
                                           start=False, stop=True)
                    C.inc(last_r1, "c3")
            b.tensor(_)

            def _(v, c20=c20):
                C.w(v, "c2", c20 + FC)
                v.tensor_scalar(out=stt4[:, 0:4], in0=psS[:, 0:8:2], scalar1=1.0 / D4,
                                scalar2=None, op0=ALU.mult)
                C.inc(v.tensor_scalar(out=stt4[:, 4:8], in0=psS[:, 1:8:2], scalar1=1.0 / D4,
                                      scalar2=None, op0=ALU.mult), "vc")
                C.w(v, "vc", C.n["vc"])
                C.inc(v.tensor_mul(stt4[:, 8:12], stt4[:, 0:4], stt4[:, 0:4]), "vc")
                C.w(v, "vc", C.n["vc"])
                C.inc(v.tensor_sub(stt4[:, 4:8], stt4[:, 4:8], stt4[:, 8:12]), "vc")
                C.w(v, "vc", C.n["vc"])
                C.inc(v.tensor_scalar(out=stt4[:, 4:8], in0=stt4[:, 4:8], scalar1=LN_EPS,
                                      scalar2=None, op0=ALU.add), "vc")
            b.vector(_)
            v_ve = C.n["vc"]

            def _(s, v_ve=v_ve):
                C.w(s, "vc", v_ve)
                C.inc(s.sqrt(stt4[:, 8:12], stt4[:, 4:8]), "sc")
            b.scalar(_)
            s_sv = C.n["sc"]

            def _(v, s_sv=s_sv):
                C.w(v, "sc", s_sv)
                C.inc(v.reciprocal(stt4[:, 12:16], stt4[:, 8:12]), "vc")
                C.w(v, "vc", C.n["vc"])
                C.inc(v.tensor_mul(stt4[:, 0:4], stt4[:, 12:16], stt4[:, 12:16]), "vc")
                C.w(v, "vc", C.n["vc"])
                C.inc(v.tensor_mul(stt4[:, 0:4], stt4[:, 0:4], stt4[:, 4:8]), "vc")
                C.w(v, "vc", C.n["vc"])
                C.inc(v.tensor_scalar(out=stt4[:, 0:4], in0=stt4[:, 0:4], scalar1=-0.5,
                                      scalar2=1.5, op0=ALU.mult, op1=ALU.add), "vc")
                C.w(v, "vc", C.n["vc"])
                C.inc(v.tensor_mul(stt4[:, 12:16], stt4[:, 12:16], stt4[:, 0:4]), "vc")
                C.w(v, "vc", C.n["vc"])
                C.inc(v.tensor_scalar(out=stt4[:, 0:4], in0=psS[:, 0:8:2],
                                      scalar1=-1.0 / D4, scalar2=None, op0=ALU.mult), "vc")
            b.vector(_)
            v_nm = C.n["vc"]

            def _(g2, v_nm=v_nm):
                C.w(g2, "vc", v_nm)
                C.fence(g2, "px")
                for tt in range(4):
                    C.dinc(g2.dma_start(out=rows_f[0:1, tt * 128:(tt + 1) * 128],
                                        in_=stt4[:, tt:tt + 1]), "px")
                    C.dinc(g2.dma_start(out=rows_f2[0:1, tt * 128:(tt + 1) * 128],
                                        in_=stt4[:, 8 + tt:9 + tt]), "px")
            b.gpsimd(_)
            g_rows = C.n["px"]

            def _(v, g_rows=g_rows):
                C.w(v, "px", g_rows)
                C.inc(v.tensor_copy(negmu_rb[0:1, :], rows_f[0:1, :]), "vc")
                C.inc(v.tensor_copy(invr_rb[0:1, :], rows_f2[0:1, :]), "vc")
            b.vector(_)
            assert C.n["vc"] == v_rows, (C.n["vc"], v_rows)

            def hslot(k):
                return cosb[:, k, :] if k < DC else sinb[:, k - DC, :]

            def _(s, c30=c30, s_gelu0=s_gelu0):
                assert C.n["sc"] == s_gelu0, (C.n["sc"], s_gelu0)
                for e2g in range(4):
                    C.w(s, "c3", c30 + e2g + 1)
                    for tt in range(4):
                        C.inc(s.activation(out=hslot(e2g * 4 + tt), in_=ps[tt][:],
                                           func=AF.Gelu,
                                           scale=stt4[:, 12 + tt:13 + tt]), "sc")
            b.scalar(_)

            t_tr0 = C.n["tc"]
            v_tr0 = v_rows

            def _(t, s_gelu0=s_gelu0, v_tr0=v_tr0):
                C.w(t, "sc", s_gelu0 + 16)
                n = 0
                for e2g in range(4):
                    for tt in range(4):
                        for q in range(4):
                            if n >= 2:
                                C.w(t, "vc", v_tr0 + n - 1)
                            C.inc(t.transpose(
                                psT2[n % 2][:, 0:128],
                                hslot(e2g * 4 + tt)[:, q * 128:(q + 1) * 128],
                                identb[:]), "tc")
                            n += 1
            b.tensor(_)

            def _(v, t_tr0=t_tr0):
                n = 0
                for e2g in range(4):
                    for tt in range(4):
                        for q in range(4):
                            n += 1
                            C.w(v, "tc", t_tr0 + n)
                            C.inc(v.tensor_copy(hT[:, tt, e2g * 4 + q, :],
                                                psT2[(n - 1) % 2][:, 0:128]), "vc")
            b.vector(_)
            v_resid0 = C.n["vc"]
            assert v_resid0 == v_tr0 + 64

            t_w20 = C.n["tc"]
            stx3 = C.stream("l2")
            stst3 = C.stream("p")
            v_resid_base = v_resid0

            def _(t, v_resid0=v_resid0, t_w20=t_w20):
                C.w(t, "vc", v_resid0)
                C.w(t, "lx", ld_w2)
                for tt in range(4):
                    bank0 = (tt % 2) * 2
                    if tt >= 2:
                        C.w(t, "vc", v_resid0 + (tt - 1))
                    for e2c in range(EC2):
                        t.matmul(ps[bank0][:], hT[:, tt, e2c, :], w2sb[:, e2c, 0:GT],
                                 start=(e2c == 0), stop=False)
                        t.matmul(ps[bank0 + 1][:], hT[:, tt, e2c, :], w2sb[:, e2c, GT:D],
                                 start=(e2c == 0), stop=False)
                    t.matmul(ps[bank0][:], ones_row[0:1, 0:128], b2_row[0:1, 0:GT],
                             start=False, stop=True)
                    C.inc(t.matmul(ps[bank0 + 1][:], ones_row[0:1, 0:128],
                                   b2_row[0:1, GT:D], start=False, stop=True), "tc")
            b.tensor(_)

            def _(v, g=g, t_w20=t_w20, stx3=stx3):
                for tt in range(4):
                    i = g * 4 + tt
                    bank0 = (tt % 2) * 2
                    C.w(v, "tc", t_w20 + tt + 1)
                    stx3.wait(v, tt)
                    v.tensor_scalar(out=xnew[:, tt % 2, :], in0=xrt[:, tt % 2, :],
                                    scalar1=r1_c[:, i:i + 1], scalar2=b0_c[:, i:i + 1],
                                    op0=ALU.mult, op1=ALU.add)
                    v.tensor_add(xnew[:, tt % 2, 0:GT], xnew[:, tt % 2, 0:GT], ps[bank0][:])
                    C.inc(v.tensor_add(xnew[:, tt % 2, GT:D], xnew[:, tt % 2, GT:D],
                                       ps[bank0 + 1][:]), "vc")
            b.vector(_)
            v_res_end = C.n["vc"]

            def _(sync, g=g, v_resid_base=v_resid_base, stx3=stx3, stst3=stst3):
                for tt in range(4):
                    if tt >= 2:
                        C.w(sync, "vc", v_resid_base + (tt - 1))
                        stst3.wait(sync, tt - 2)
                    stx3.issue(sync.dma_start(out=xrt[:, tt % 2, :], in_=xsrc[g * 4 + tt, :, :]), tt)
            b.sync(_)

            def _(g2, g=g, v_res_end=v_res_end, stst3=stst3):
                for tt in range(4):
                    C.w(g2, "vc", v_res_end - (3 - tt))
                    stst3.issue(g2.dma_start(out=xdst[g * 4 + tt, :, :], in_=xnew[:, tt % 2, :]), tt)
            b.gpsimd(_)
            full_barrier(C)

    full_barrier(C)


def junk_one(C):
    return C.bufs["junkb"][:, 0:1]



DEBUG_BUILD = False
STAGES = 3
SUBSTAGE = 0


def build_nc():
    nc = bass.Bass("TRN2", target_bir_lowering=False, debug=DEBUG_BUILD,
                   detect_race_conditions=False)
    io = {}
    io["x_in"] = nc.dram_tensor("x", [TPC, D], F32, kind="ExternalInput").ap()
    io["out"] = nc.dram_tensor("out", [TPC, D], F32, kind="ExternalOutput").ap()
    io["wup_raw"] = nc.dram_tensor("wup", [I3, D], F32, kind="ExternalInput").ap()
    io["wgate_raw"] = nc.dram_tensor("wgate", [I3, D], F32, kind="ExternalInput").ap()
    io["wdown_raw"] = nc.dram_tensor("wdown", [D, I3], F32, kind="ExternalInput").ap()
    io["womT_in"] = nc.dram_tensor("womT", [D, D], F32, kind="ExternalInput").ap()
    io["w1T_in"] = nc.dram_tensor("w1T", [D4, D2], F32, kind="ExternalInput").ap()
    io["w2T_in"] = nc.dram_tensor("w2T", [D2, D], F32, kind="ExternalInput").ap()
    io["bom_in"] = nc.dram_tensor("bom", [1, D], F32, kind="ExternalInput").ap()
    io["b1_in"] = nc.dram_tensor("b1", [1, D2], F32, kind="ExternalInput").ap()
    io["b2_in"] = nc.dram_tensor("b2", [1, D], F32, kind="ExternalInput").ap()
    io["gamma_in"] = nc.dram_tensor("gammac", [128, FC], F32, kind="ExternalInput").ap()
    io["beta_in"] = nc.dram_tensor("betac", [128, FC], F32, kind="ExternalInput").ap()
    io["ls_in"] = nc.dram_tensor("lsc", [128, DC], F32, kind="ExternalInput").ap()
    io["identb_in"] = nc.dram_tensor("identb", [128, 128], BF16, kind="ExternalInput").ap()
    io["invsq_in"] = nc.dram_tensor("invsq", [1, TPC], F32, kind="ExternalInput").ap()
    io["invp_in"] = nc.dram_tensor("invp", [1, TPC], F32, kind="ExternalInput").ap()
    io["cmask_in"] = nc.dram_tensor("cmask", [1, NCORES], F32, kind="ExternalInput").ap()

    io["xres"] = nc.dram_tensor("xres", [NT, 128, D], F32).ap()
    io["wupT_h"] = nc.dram_tensor("wupT_h", [D, I3], BF16).ap()
    io["wgT_h"] = nc.dram_tensor("wgT_h", [D, I3], BF16).ap()
    io["wdT_h"] = nc.dram_tensor("wdT_h", [I3, D], BF16).ap()
    io["womT_h"] = nc.dram_tensor("womT_h", [D, D], BF16).ap()
    io["w1g_h"] = nc.dram_tensor("w1g_h", [D4, D2], BF16).ap()
    io["w2T_h"] = nc.dram_tensor("w2T_h", [D2, D], BF16).ap()
    io["sp_up"] = nc.dram_tensor("sp_up", [IC, NG, 128, GT], BF16).ap()
    io["sp_gate"] = nc.dram_tensor("sp_gate", [IC, NG, 128, GT], BF16).ap()
    io["sp_gi"] = nc.dram_tensor("sp_gi", [IC, NG, 128, GT], I16).ap()
    io["sp_sig"] = nc.dram_tensor("sp_sig", [IC, NG, 128, GT], U16).ap()
    io["sp_p"] = nc.dram_tensor("sp_p", [IC, NG, 128, GT], F32).ap()
    io["sp_phi"] = nc.dram_tensor("sp_phi", [NG, 128, DC, GT], F32).ap()
    io["sp_xln"] = nc.dram_tensor("sp_xln", [NG, 128, DC, GT], BF16).ap()
    io["sp_cos"] = nc.dram_tensor("sp_cos", [NG, 128, DC, GT], BF16).ap()
    io["sp_sin"] = nc.dram_tensor("sp_sin", [NG, 128, DC, GT], BF16).ap()
    io["sp_cmr"] = nc.dram_tensor("sp_cmr", [NG, 128, DC, GT], BF16).ap()
    io["sp_cmi"] = nc.dram_tensor("sp_cmi", [NG, 128, DC, GT], BF16).ap()
    io["sp_Sr"] = nc.dram_tensor("sp_Sr", [NG, 128, DC, GT], F32).ap()
    io["sp_Si"] = nc.dram_tensor("sp_Si", [NG, 128, DC, GT], F32).ap()

    es = contextlib.ExitStack()
    with es:
        E = es.enter_context
        block = E(nc.Block())
        C = Ctx(nc, block, es)
        bufs = C.bufs
        bufs["cc_in"] = nc.dram_tensor("cc_in", [128, 2], F32).ap()
        bufs["cc_out"] = nc.dram_tensor("cc_out", [128, 2], F32).ap()
        bufs["rowb_h"] = nc.dram_tensor("rowb_h", [16, 8], F32).ap()
        bufs["ag_in8"] = nc.dram_tensor("ag_in8", [128, 8], F32).ap()
        bufs["ag_out8"] = nc.dram_tensor("ag_out8", [NCORES * 128, 8], F32).ap()
        bufs["ag_in16"] = nc.dram_tensor("ag_in16", [128, 16], F32).ap()
        bufs["ag_out16"] = nc.dram_tensor("ag_out16", [NCORES * 128, 16], F32).ap()

        def sb(name, shape, dt):
            t = E(nc.sbuf_tensor("sb_" + name, shape, dt))
            bufs[name] = t
            return t

        sb("identb", [128, 128], BF16)
        sb("ones_row", [1, GT], F32)
        sb("ones_colb", [128, 1], BF16)
        sb("bom_row", [1, D], F32)
        sb("b2_row", [1, D], F32)
        sb("const_bf", [1, D2], BF16)
        sb("colsum_bf", [1, D2], BF16)
        sb("const_f", [1, D2], F32)
        sb("gamma_c", [128, FC], F32)
        sb("beta_c", [128, FC], F32)
        sb("exp_ls", [128, DC], F32)
        sb("cmaskt", [128, NCORES], F32)
        for nm in ("s1c", "s2c", "mxc", "mnc", "mu_c", "r_c", "a_c", "bq_c",
                   "mu0_c", "r0_c", "b0_c", "r1_c", "t0_c", "t1_c", "t2_c"):
            sb(nm, [128, NT], F32)
        sb("colv", [128, 8], F32)
        sb("rowf", [16, 256], F32)
        sb("bcp", [128, 2], F32)
        sb("colk", [16, 8], F32)
        sb("bc16", [128, 16], F32)
        sb("sc16", [128, 16], F32)
        sb("junkb", [128, 1024], BF16)
        sb("carry", [128, DC], F32)
        sb("carry_r", [128, DC], F32)
        sb("carry_i", [128, DC], F32)
        sb("off_phi", [128, DC], F32)
        sb("off_r", [128, DC], F32)
        sb("off_i", [128, DC], F32)
        sb("gat", [128, NCORES, 16], F32)
        sb("xstats", [128, 2, D], F32)
        bufs["ps"] = [E(nc.psum_tensor(f"ps{i}", [128, GT], F32)) for i in range(6)]
        bufs["psS"] = bufs["ps"][5]
        bufs["psT2"] = [E(nc.psum_tensor("psT_a", [128, 1024], BF16)),
                        E(nc.psum_tensor("psT_b", [128, 1024], BF16))]

        emit_prologue(C, io)
        x_tiles_in = io["x_in"].rearrange("(n p) d -> n p d", p=128)
        out_tiles = io["out"].rearrange("(n p) d -> n p d", p=128)
        if STAGES == 0:
            pass
        elif STAGES == 1:
            emit_ffn(C, 0, x_tiles_in, out_tiles, io)
        elif STAGES == 2:
            emit_ffn(C, 0, x_tiles_in, io["xres"], io)
            emit_attn(C, io["xres"], out_tiles, io)
        else:
            emit_ffn(C, 0, x_tiles_in, io["xres"], io)
            emit_attn(C, io["xres"], io["xres"], io)
            emit_ffn(C, 1, io["xres"], out_tiles, io)

        snap = C.snap()

        def _(g):
            for s in SEMS:
                if snap[s]:
                    g.wait_ge(C.sem[s], snap[s])
        block.gpsimd(_)
    return nc



_CACHE = {}


def kernel(**inputs):
    x = np.ascontiguousarray(np.asarray(inputs["L0_attn_x"], np.float32)).reshape(B * S, D)
    womT = np.ascontiguousarray(np.asarray(inputs["L0_attn_W_omega"], np.float32).T)
    w1T = np.ascontiguousarray(np.asarray(inputs["L0_attn_W1"], np.float32).T)
    w2T = np.ascontiguousarray(np.asarray(inputs["L0_attn_W2"], np.float32).T)
    bom = np.ascontiguousarray(np.asarray(inputs["L0_attn_b_omega"], np.float32).reshape(1, D))
    b1 = np.ascontiguousarray(np.asarray(inputs["L0_attn_b1"], np.float32).reshape(1, D2))
    b2 = np.ascontiguousarray(np.asarray(inputs["L0_attn_b2"], np.float32).reshape(1, D))
    gamma = np.ascontiguousarray(np.asarray(inputs["L0_attn_ln_gamma"], np.float32).reshape(FC, 128).T)
    beta = np.ascontiguousarray(np.asarray(inputs["L0_attn_ln_beta"], np.float32).reshape(FC, 128).T)
    lsc = np.ascontiguousarray(np.asarray(inputs["L0_attn_log_scale"], np.float32).reshape(DC, 128).T)
    wup = np.ascontiguousarray(np.asarray(inputs["L0_ffn_w_up"], np.float32))
    wgate = np.ascontiguousarray(np.asarray(inputs["L0_ffn_w_gate"], np.float32))
    wdown = np.ascontiguousarray(np.asarray(inputs["L0_ffn_w_down"], np.float32))
    identb = np.eye(128, dtype=NPBF)
    pos = np.arange(1, S + 1, dtype=np.float64)
    invsq_full = (1.0 / np.sqrt(pos)).astype(np.float32)
    invp_full = (1.0 / pos).astype(np.float32)

    if "nc" not in _CACHE:
        _CACHE["nc"] = build_nc()
    nc = _CACHE["nc"]

    in_maps = []
    for c in range(NCORES):
        sl = slice(c * TPC, (c + 1) * TPC)
        so = (c % 4) * TPC
        cmask = np.zeros((1, NCORES), np.float32)
        for cp in range((c // 4) * 4, c):
            cmask[0, cp] = 1.0
        in_maps.append({
            "x": np.ascontiguousarray(x[sl]),
            "womT": womT, "w1T": w1T, "w2T": w2T,
            "bom": bom, "b1": b1, "b2": b2,
            "gammac": gamma, "betac": beta, "lsc": lsc,
            "wup": wup, "wgate": wgate, "wdown": wdown,
            "identb": identb,
            "invsq": np.ascontiguousarray(invsq_full[so:so + TPC].reshape(1, TPC)),
            "invp": np.ascontiguousarray(invp_full[so:so + TPC].reshape(1, TPC)),
            "cmask": cmask,
        })

    res = run_bass_kernel_spmd(nc, in_maps, core_ids=list(range(NCORES)))
    outp = np.concatenate([np.asarray(res.results[c]["out"], np.float32)
                           for c in range(NCORES)], axis=0)
    return outp.reshape(B, S, D)


if __name__ == "__main__":
    nc = build_nc()
    n = sum(1 for _ in nc.all_instructions())
    print("built, instructions:", n)
